# version 02
# baseline (speedup 1.0000x reference)
import math
from contextlib import ExitStack

import numpy as np

import concourse.bass as bass
import concourse.mybir as mybir

F32 = mybir.dt.float32
BF16 = mybir.dt.bfloat16
AF = mybir.ActivationFunctionType
ALU = mybir.AluOpType
AX = mybir.AxisListType

ENGINES = ("sync", "tensor", "vector", "scalar", "gpsimd")


class P:

    CONST_CAP = 64

    def __init__(self, nc, es: ExitStack):
        self.nc = nc
        self.es = es
        self.ops = {e: [] for e in ENGINES}
        self.tot = {}
        self.sems = {}
        self._uid = 0
        self._epoch = 0
        self._epoch_sems = []
        es.callback(lambda: [self.nc.release_semaphore(self.sems[n])
                             for n in self._epoch_sems])
        self.sem("cld_")
        self.const_vals = []
        self.const_dram = nc.dram_tensor("consts_", [128, self.CONST_CAP], F32,
                                         kind="ExternalInput").ap()
        self.const_sb = es.enter_context(
            nc.sbuf_tensor("consts_sb", [128, self.CONST_CAP], F32))

    def const(self, val):
        val = float(val)
        if val not in self.const_vals:
            assert len(self.const_vals) < self.CONST_CAP
            i = len(self.const_vals)
            self.const_vals.append(val)
            self.nc.const_aps.register(F32, val, self.const_sb[:, i:i + 1])
        return val

    def const_array(self):
        arr = np.zeros((128, self.CONST_CAP), np.float32)
        for i, v in enumerate(self.const_vals):
            arr[:, i] = v
        return arr

    def uid(self):
        self._uid += 1
        return self._uid

    def sem(self, name):
        if name not in self.sems:
            self.sems[name] = self.nc.alloc_semaphore(name=name)
            self.tot[name] = 0
            self._epoch_sems.append(name)
        return self.sems[name]

    def emit(self, eng, fname, *args, wait=(), inc=None, inc_n=1, **kw):
        wait = tuple(wait)
        ret = None
        if inc is not None:
            self.sem(inc)
            self.tot[inc] += inc_n
            ret = self.tot[inc]
        sems = self.sems

        def run(h, fname=fname, args=args, kw=kw, wait=wait, inc=inc, inc_n=inc_n):
            for s, v in wait:
                h.wait_ge(sems[s], v)
            r = getattr(h, fname)(*args, **kw)
            if inc is not None:
                if inc_n == 1:
                    r.then_inc(sems[inc])
                else:
                    r.then_inc(sems[inc], inc_n)
        self.ops[eng].append(run)
        return ret

    def wait(self, eng, sem_name, val=None):
        if val is None:
            val = self.tot[sem_name]
        sems = self.sems
        self.sem(sem_name)

        def run(h, s=sem_name, v=val):
            h.wait_ge(sems[s], v)
        self.ops[eng].append(run)
        return val

    def raw(self, eng, fn):
        self.ops[eng].append(fn)

    def barrier(self, snap):
        for e in ENGINES:
            for s, v in self.tot.items():
                if v > snap.get(s, 0):
                    self.wait(e, s, v)
        return dict(self.tot)

    def epoch_end(self, snap):
        sems = self.sems
        self.sem("r_")
        self.sem("c_")
        snap2 = self.barrier(snap)
        self._epoch += 1
        k = self._epoch
        dead = [n for n in self._epoch_sems if n not in ("r_", "c_", "cld_")]
        self._epoch_sems = []
        for e in ENGINES:
            if e != "gpsimd":
                self.tot["r_"] += 1

                def rinc(h):
                    h.sem_inc(sems["r_"], 1)
                self.ops[e].append(rinc)
        self.wait("gpsimd", "r_", 4 * k)
        handles = [sems[n] for n in dead]

        def clr(h, handles=handles):
            for hd in handles:
                h.dma_reset(range(hd.num, hd.num + 1))
                h.sem_clear(hd)
            h.sem_inc(sems["c_"], 1)
        self.ops["gpsimd"].append(clr)
        self.tot["c_"] += 1
        for e in ENGINES:
            if e != "gpsimd":
                self.wait(e, "c_", k)
        for n in dead:
            self.nc.release_semaphore(sems[n])
        return dict(self.tot)

    def dma(self, q, out, in_, inc=None, wait=(), **kw):
        return self.emit(q, "dma_start", wait=wait, inc=inc, inc_n=16,
                         out=out, in_=in_, **kw)

    def build_block(self):
        nc = self.nc
        sems = self.sems
        cld = self.sem("cld_")

        def pre(h, is_sync):
            if is_sync:
                h.dma_start(out=self.const_sb[:], in_=self.const_dram)\
                    .then_inc(sems["cld_"], 16)
            h.wait_ge(sems["cld_"], 16)

        with nc.Block() as block:
            @block.sync
            def _(h):
                pre(h, True)
                for f in self.ops["sync"]:
                    f(h)

            @block.tensor
            def _(h):
                pre(h, False)
                for f in self.ops["tensor"]:
                    f(h)

            @block.vector
            def _(h):
                pre(h, False)
                for f in self.ops["vector"]:
                    f(h)

            @block.scalar
            def _(h):
                pre(h, False)
                for f in self.ops["scalar"]:
                    f(h)

            @block.gpsimd
            def _(h):
                pre(h, False)
                for f in self.ops["gpsimd"]:
                    f(h)


def ceil_div(a, b):
    return (a + b - 1) // b


"""Block emitters for mk_seq_preln_d2. Dev copy; inlined into kernel.py later.

Layout conventions (per core, T=1024 tokens, 8 t-chunks of 128):
- token-major act tile:  sb[p, tb*W + j]   p = token-in-chunk, j = feature
- feature-major act:     sb[p, kc*T + t]   feature f = kc*128 + p, t = token
- weights in HBM pre-transposed so contraction dim is the row (partition) dim.
"""
import math
import numpy as np
from contextlib import ExitStack


T = 1024
NTB = 8
D = 2048
NKC = 16
EPS_LN = 1e-5
RMS_EPS = 1e-5


def emit_ln_yT(p, x_sb, yT_sb, ident_sb, reg, tag, x_ready_v=(), x_ready_s=(),
               side=None):
    nc = p.nc
    sSQ, sVAR, sSD, sISTD, sXM, sTP, sEV = (f"{tag}{n}" for n in
        ("sq", "var", "sd", "istd", "xm", "tp", "ev"))
    p.const(EPS_LN)
    stat = reg.enter_context(nc.sbuf_tensor(f"{tag}_stat", [128, 64], F32, side=side))
    sq = reg.enter_context(nc.sbuf_tensor(f"{tag}_sq", [128, 2048], F32, side=side))
    xm = reg.enter_context(nc.sbuf_tensor(f"{tag}_xm", [128, 2 * 2048], F32, side=side))
    diag = reg.enter_context(nc.sbuf_tensor(f"{tag}_diag", [128, 2 * 128], F32, side=side))
    ps = [reg.enter_context(nc.psum_tensor(f"{tag}_ps{i}", [128, 512], F32))
          for i in range(4)]

    for tb in range(NTB):
        xsl = x_sb[:, tb * D:(tb + 1) * D]
        cS, cSQS, cNM, cIS, cM2, cV, cSD = (tb, 8 + tb, 16 + tb, 24 + tb,
                                            32 + tb, 40 + tb, 48 + tb)
        sq_war = ((sVAR, p.tot[sVAR]),) if tb > 0 else ()
        p.emit("scalar", "activation", sq[:], xsl, AF.Square,
               wait=(x_ready_s if tb == 0 else ()) + sq_war, inc=sSQ)
        sC = f"{tag}c"
        p.emit("vector", "reduce_sum", stat[:, cS:cS + 1], xsl, axis=AX.X,
               wait=x_ready_v if tb == 0 else (), inc=sC)
        p.emit("vector", "reduce_sum", stat[:, cSQS:cSQS + 1], sq[:], axis=AX.X,
               wait=((sSQ, p.tot[sSQ]),), inc=sC)
        p.emit("vector", "tensor_scalar_mul", stat[:, cNM:cNM + 1],
               stat[:, cS:cS + 1], -1.0 / D,
               wait=((sC, p.tot[sC]),), inc=sC)
        p.emit("vector", "tensor_mul", stat[:, cM2:cM2 + 1],
               stat[:, cNM:cNM + 1], stat[:, cNM:cNM + 1],
               wait=((sC, p.tot[sC]),), inc=sC)
        p.emit("vector", "tensor_scalar_mul", stat[:, cV:cV + 1],
               stat[:, cSQS:cSQS + 1], 1.0 / D,
               wait=((sC, p.tot[sC]),), inc=sC)
        p.emit("vector", "tensor_sub", stat[:, cV:cV + 1],
               stat[:, cV:cV + 1], stat[:, cM2:cM2 + 1],
               wait=((sC, p.tot[sC]),), inc=sVAR)
        p.emit("scalar", "activation", stat[:, cSD:cSD + 1], stat[:, cV:cV + 1],
               AF.Sqrt, bias=EPS_LN, wait=((sVAR, p.tot[sVAR]),), inc=sSD)
        p.emit("vector", "reciprocal", stat[:, cIS:cIS + 1],
               stat[:, cSD:cSD + 1], wait=((sSD, p.tot[sSD]),), inc=sISTD)
        buf = tb % 2
        xmt = xm[:, buf * D:(buf + 1) * D]
        dg = diag[:, buf * 128:(buf + 1) * 128]
        war = ((sTP, 4 * (tb - 1)),) if tb >= 2 else ()
        p.emit("scalar", "activation", xmt, xsl, AF.Identity,
               bias=stat[:, cNM:cNM + 1], wait=war)
        p.emit("scalar", "activation", dg, ident_sb[:], AF.Copy,
               scale=stat[:, cIS:cIS + 1],
               wait=((sISTD, p.tot[sISTD]),), inc=sXM)
        sxm_rdy = p.tot[sXM]
        for g in range(4):
            bank = ps[g]
            gi = tb * 4 + g
            bank_war = ((sEV, gi - 3),) if gi >= 4 else ()
            for j in range(4):
                kc = g * 4 + j
                p.emit("tensor", "matmul", bank[:, j * 128:(j + 1) * 128],
                       xmt[:, kc * 128:(kc + 1) * 128], dg,
                       start=True, stop=True,
                       wait=((((sXM, sxm_rdy),) if j == 0 else ()) +
                             (bank_war if j == 0 else ())),
                       inc=(sTP if j == 3 else None))
            outv = yT_sb.rearrange("p (kc t) -> p kc t", kc=NKC)[
                :, g * 4:(g + 1) * 4, tb * 128:(tb + 1) * 128]
            inv = bank.rearrange("p (j t) -> p j t", j=4)
            p.emit("vector", "tensor_copy", outv, inv,
                   wait=((sTP, p.tot[sTP]),), inc=sEV)
    return sEV, p.tot[sEV]



L0_NH = 16


def l0_slopes():
    n = L0_NH
    start = 2.0 ** (-2.0 ** (-(math.log2(n) - 3)))
    return [start * start ** i for i in range(n)]


def emit_l0_attn(p, io):
    nc = p.nc
    u = p.uid()
    t = lambda n: f"a{u}{n}"
    slopes = l0_slopes()
    rr = [math.exp(-s) for s in slopes]
    rC = [r ** 128 for r in rr]

    reg = ExitStack()
    reg_q = ExitStack()
    reg_kv = ExitStack()
    ident_sb = reg.enter_context(nc.sbuf_tensor(t("id"), [128, 128], F32))
    ident_bf = reg.enter_context(nc.sbuf_tensor(t("ib"), [128, 128], BF16))
    p.dma("sync", ident_sb[:], io["ident"], inc=t("tl"))
    p.dma("gpsimd", ident_bf[:], io["ident"], inc=t("tlg"))

    reg_st = ExitStack()
    kv = reg_st.enter_context(nc.sbuf_tensor(t("kv"), [128, L0_NH * 128], F32,
                                             side="right"))
    regY = ExitStack()
    yT = regY.enter_context(nc.sbuf_tensor(t("yT"), [128, NKC * T], BF16,
                                           side="right"))
    regA = ExitStack()
    x_sb = regA.enter_context(nc.sbuf_tensor(t("x"), [128, NTB * D], F32,
                                             side="right"))
    p.dma("sync", x_sb.rearrange("p (tb d) -> p tb d", tb=NTB),
          io["x_src"].rearrange("(tb p) d -> p tb d", p=128), inc=t("xld"))
    xw = ((t("xld"), p.tot[t("xld")]), (t("tl"), p.tot[t("tl")]))
    sLN, nLN = emit_ln_yT(p, x_sb[:], yT[:], ident_sb, regA, t("ln"),
                          x_ready_v=xw, x_ready_s=xw, side="right")
    regA.close()
    attnT = reg.enter_context(nc.sbuf_tensor(t("at"), [128, NKC * T], BF16))
    qT = reg_q.enter_context(nc.sbuf_tensor(t("qT"), [128, NKC * T], BF16))
    kT = reg_kv.enter_context(nc.sbuf_tensor(t("kT"), [128, NKC * T], BF16))
    v_sb = reg_kv.enter_context(nc.sbuf_tensor(t("v"), [128, NTB * D], BF16))

    regP = ExitStack()
    Wb = [regP.enter_context(nc.sbuf_tensor(t(f"W{i}"), [128, NKC * 512], BF16,
                                            side="right"))
          for i in range(2)]
    qtok = regP.enter_context(nc.sbuf_tensor(t("qtok"), [128, 512], F32,
                                             side="right"))
    rotc = regP.enter_context(nc.sbuf_tensor(t("rc"), [128, 2 * 128], F32,
                                             side="right"))
    rots = regP.enter_context(nc.sbuf_tensor(t("rs"), [128, 2 * 128], F32,
                                             side="right"))
    rtmp = regP.enter_context(nc.sbuf_tensor(t("rt"), [128, 4 * 128], BF16,
                                             side="right"))
    psA = [regP.enter_context(nc.psum_tensor(t(f"pp{i}"), [128, 512], F32))
           for i in range(2)]
    psT = regP.enter_context(nc.psum_tensor(t("pt"), [128, 512], F32))
    sWL, sPM, sPE, sRO, sTP, sTE, sRL = (t(x) for x in
        ("wl", "pm", "pe", "ro", "tp", "te", "rl"))
    for s in (sWL, sPM, sPE, sRO, sTP, sTE, sRL):
        p.sem(s)

    pm_done = {}
    for wi, wname in enumerate(("Wq", "Wk", "Wv")):
        dstT = (qT, kT, None)[wi]
        for qu in range(4):
            hidx = wi * 4 + qu
            buf = hidx % 2
            war = ((sPM, pm_done[hidx - 2]),) if hidx >= 2 else ((sLN, nLN),)
            src = io[wname].rearrange("(kc p) n -> p kc n", p=128)[
                :, :, qu * 512:(qu + 1) * 512]
            sWLb = sWL + str(buf)
            p.dma("gpsimd", Wb[buf].rearrange("p (kc n) -> p kc n", kc=NKC),
                  src, inc=sWLb, wait=war)
            wl_sem, wlv = sWLb, p.tot[sWLb]
            for tb in range(NTB):
                bb = tb % 2
                if wi < 2:
                    sRLb = sRL + str(bb)
                    p.dma("sync", rotc[:, bb * 128:(bb + 1) * 128],
                          io["rotC"][tb * 128:(tb + 1) * 128,
                                     qu * 128:(qu + 1) * 128], inc=sRLb,
                          wait=((sRO, p.tot[sRO]),) if (tb >= 2 or hidx > 0)
                               else ((sLN, nLN),))
                    p.dma("sync", rots[:, bb * 128:(bb + 1) * 128],
                          io["rotS"][tb * 128:(tb + 1) * 128,
                                     qu * 128:(qu + 1) * 128], inc=sRLb)
                    rl_sem, rlv = sRLb, p.tot[sRLb]
                for kc in range(NKC):
                    w0 = []
                    if kc == 0:
                        w0 = [(wl_sem, wlv), (sPE, p.tot[sPE])]
                        if hidx == 0 and tb == 0:
                            w0.append((sLN, nLN))
                    p.emit("tensor", "matmul", psA[bb][:, :],
                           yT[:, kc * T + tb * 128:kc * T + (tb + 1) * 128],
                           Wb[buf][:, kc * 512:(kc + 1) * 512],
                           start=(kc == 0), stop=(kc == NKC - 1),
                           wait=w0,
                           inc=(sPM if kc == NKC - 1 else None))
                pmv = p.tot[sPM]
                if wi == 2:
                    p.emit("vector", "tensor_copy",
                           v_sb[:, tb * D + qu * 512:tb * D + (qu + 1) * 512],
                           psA[bb][:, :], wait=((sPM, pmv),), inc=sPE)
                else:
                    p.emit("vector", "tensor_copy", qtok[:], psA[bb][:, :],
                           wait=((sPM, pmv), (sTP, p.tot[sTP])), inc=sPE)
                    qv = qtok.rearrange("p (hh c) -> p hh c", hh=4)
                    x1 = qv[:, :, 0:32]
                    x2 = qv[:, :, 64:96]
                    tv = rtmp.rearrange("p (i hh c) -> p i hh c", i=4, hh=4)
                    C = rotc[:, bb * 128:(bb + 1) * 128].rearrange(
                        "p (hh c) -> p hh c", hh=4)
                    S = rots[:, bb * 128:(bb + 1) * 128].rearrange(
                        "p (hh c) -> p hh c", hh=4)
                    p.emit("vector", "tensor_mul", tv[:, 0], x1, C,
                           wait=((rl_sem, rlv), (sPE, p.tot[sPE])))
                    p.emit("vector", "tensor_mul", tv[:, 1], x2, S)
                    p.emit("vector", "tensor_mul", tv[:, 2], x2, C)
                    p.emit("vector", "tensor_mul", tv[:, 3], x1, S, inc=sRO)
                    p.emit("vector", "tensor_add", x1, tv[:, 0], tv[:, 1],
                           wait=((sRO, p.tot[sRO]),))
                    p.emit("vector", "tensor_sub", x2, tv[:, 2], tv[:, 3],
                           inc=sRO)
                    rov = p.tot[sRO]
                    for j in range(4):
                        p.emit("tensor", "transpose",
                               psT[:, j * 128:(j + 1) * 128],
                               qtok[:, j * 128:(j + 1) * 128], ident_sb[:],
                               wait=(((sRO, rov), (sTE, p.tot[sTE]))
                                     if j == 0 else ()),
                               inc=(sTP if j == 3 else None))
                    outv = dstT.rearrange("p (kc tt) -> p kc tt", kc=NKC)[
                        :, qu * 4:qu * 4 + 4, tb * 128:(tb + 1) * 128]
                    p.emit("vector", "tensor_copy", outv,
                           psT.rearrange("p (j tt) -> p j tt", j=4),
                           wait=((sTP, p.tot[sTP]),), inc=sTE)
            pm_done[hidx] = p.tot[sPM]
    nPROJ_TE = p.tot[sTE]
    nPROJ_PE = p.tot[sPE]
    regP.close()
    regY.close()
    reg_o = ExitStack()
    ps_o = reg_o.enter_context(nc.psum_tensor(t("p3"), [128, 512], F32))
    regS = ExitStack()
    qdecT = regS.enter_context(nc.sbuf_tensor(t("qd"), [128, L0_NH * 128], BF16))
    qd_sb = regS.enter_context(nc.sbuf_tensor(t("qs"), [128, 512], BF16))
    dmaskT = regS.enter_context(nc.sbuf_tensor(t("dm"), [128, L0_NH * 128], BF16))
    kdec = regS.enter_context(nc.sbuf_tensor(t("kd"), [128, 16], F32))
    ks_sb = regS.enter_context(nc.sbuf_tensor(t("ks"), [128, 512], BF16))
    kvbf = regS.enter_context(nc.sbuf_tensor(t("kb"), [128, 512], BF16))
    amt_sb = regS.enter_context(nc.sbuf_tensor(t("am"), [128, 512], BF16))
    ps_t = regS.enter_context(nc.psum_tensor(t("p0"), [128, 512], BF16))
    ps_c = regS.enter_context(nc.psum_tensor(t("p1"), [128, 512], F32))
    ps_a = regS.enter_context(nc.psum_tensor(t("p2"), [128, 512], F32))
    proj_done = [(sPM, p.tot[sPM]), (sTE, nPROJ_TE), (sPE, nPROJ_PE),
                 (sRO, p.tot[sRO])]
    p.dma("gpsimd", dmaskT.rearrange("p (h c) -> p h c", h=L0_NH),
          io["dmaskT"].rearrange("(h s) c -> s h c", s=128), inc=t("tlg"),
          wait=proj_done)
    p.dma("gpsimd", qdecT.rearrange("p (h c) -> p h c", h=L0_NH),
          io["qdecT"].rearrange("(h d) c -> d h c", d=128), inc=t("tlg"))
    p.dma("sync", kdec[:], io["kdec"], inc=t("tl"), wait=proj_done)
    p.emit("gpsimd", "memset", kv[:], 0.0, inc=t("ms"))
    tl_rdy = p.tot[t("tl")]
    tlg_rdy = p.tot[t("tlg")]
    ms_rdy = p.tot[t("ms")]

    sKT, sKS, sCB, sKC, sAT, sAM, sQD, sPO, sPV, sKU = (t(x) for x in
        ("skt", "sks", "scb", "skc", "sat", "sam", "sqd", "spo", "spv", "sku"))
    for s in (sKT, sKS, sCB, sKC, sAT, sAM, sQD, sPO, sPV, sKU):
        p.sem(s)
    qT_v = qT.rearrange("p (h tt) -> p h tt", h=NKC)
    at_v = attnT.rearrange("p (h tt) -> p h tt", h=NKC)
    for i in range(NTB):
        for hg in range(4):
            g = i * 4 + hg
            hs = [hg * 4 + j for j in range(4)]
            for j, h in enumerate(hs):
                w = []
                if j == 0:
                    w.append((sKS, 4 * g))
                    if g == 0:
                        w.append((sTE, nPROJ_TE))
                        w.append((sPE, nPROJ_PE))
                        w.append((t("tlg"), tlg_rdy))
                p.emit("tensor", "transpose", ps_t[:, j * 128:(j + 1) * 128],
                       kT[:, h * T + i * 128:h * T + (i + 1) * 128], ident_bf[:],
                       wait=w, inc=(sKT if j == 3 else None))
            for j, h in enumerate(hs):
                w = [(sKT, p.tot[sKT])] if j == 0 else []
                if j == 0:
                    w.append((sCB, g))
                    if g == 0:
                        w.append((t("tl"), tl_rdy))
                        w.append((t("tlg"), tlg_rdy))
                p.emit("scalar", "activation", ks_sb[:, j * 128:(j + 1) * 128],
                       ps_t[:, j * 128:(j + 1) * 128], AF.Copy,
                       scale=kdec[:, h:h + 1], wait=w,
                       inc=(sKS if j == 3 else None), inc_n=4)
            w = [(sPO, g)]
            if i > 0:
                w.append((sKU, 4 * ((i - 1) * 4 + hg) + 4))
            else:
                w.append((t("ms"), ms_rdy))
            p.emit("scalar", "activation", kvbf[:],
                   kv[:, hg * 512:(hg + 1) * 512], AF.Copy, wait=w, inc=sKC)
            for j, h in enumerate(hs):
                w = []
                if j == 0:
                    w = [(sKS, p.tot[sKS]), (sKU, 4 * (g - 1) + 4 if g > 0 else 0)]
                p.emit("tensor", "matmul", ps_c[:, j * 128:(j + 1) * 128],
                       ks_sb[:, j * 128:(j + 1) * 128],
                       v_sb[:, i * D + h * 128:i * D + (h + 1) * 128],
                       start=True, stop=True, wait=w,
                       inc=(sCB if j == 3 else None))
            for j, h in enumerate(hs):
                w = [(sAM, g)] if j == 0 else []
                p.emit("tensor", "matmul", ps_a[:, j * 128:(j + 1) * 128],
                       kT[:, h * T + i * 128:h * T + (i + 1) * 128],
                       qT[:, h * T + i * 128:h * T + (i + 1) * 128],
                       start=True, stop=True, wait=w,
                       inc=(sAT if j == 3 else None))
            p.emit("vector", "tensor_mul", amt_sb.rearrange("p (j c) -> p j c", j=4),
                   ps_a.rearrange("p (j c) -> p j c", j=4),
                   dmaskT.rearrange("p (h c) -> p h c", h=L0_NH)[:, hg * 4:hg * 4 + 4, :],
                   wait=[(sAT, p.tot[sAT]), (sPO, g)] +
                        ([(t("tl"), tl_rdy), (t("tlg"), tlg_rdy)] if g == 0 else []),
                   inc=sAM)
            p.emit("vector", "tensor_mul", qd_sb.rearrange("p (j c) -> p j c", j=4),
                   qT_v[:, hg * 4:hg * 4 + 4, i * 128:(i + 1) * 128],
                   qdecT.rearrange("p (h c) -> p h c", h=L0_NH)[:, hg * 4:hg * 4 + 4, :],
                   inc=sQD)
            for j, h in enumerate(hs):
                w = []
                if j == 0:
                    w = [(sAM, p.tot[sAM]), (sQD, p.tot[sQD]),
                         (sKC, p.tot[sKC]), (sPV, g)]
                p.emit("tensor", "matmul", ps_o[:, j * 128:(j + 1) * 128],
                       v_sb[:, i * D + h * 128:i * D + (h + 1) * 128],
                       amt_sb[:, j * 128:(j + 1) * 128],
                       start=True, stop=False, wait=w)
                p.emit("tensor", "matmul", ps_o[:, j * 128:(j + 1) * 128],
                       kvbf[:, j * 128:(j + 1) * 128],
                       qd_sb[:, j * 128:(j + 1) * 128],
                       start=False, stop=True,
                       inc=(sPO if j == 3 else None))
            p.emit("vector", "tensor_copy",
                   at_v[:, hg * 4:hg * 4 + 4, i * 128:(i + 1) * 128],
                   ps_o.rearrange("p (j c) -> p j c", j=4),
                   wait=[(sPO, p.tot[sPO])], inc=sPV)
            for j, h in enumerate(hs):
                w = [(sCB, p.tot[sCB]), (sKC, p.tot[sKC])] if j == 0 else []
                p.emit("vector", "scalar_tensor_tensor",
                       kv[:, h * 128:(h + 1) * 128],
                       kv[:, h * 128:(h + 1) * 128], rC[h],
                       ps_c[:, j * 128:(j + 1) * 128],
                       ALU.mult, ALU.add, wait=w, inc=sKU)

    regS.close()
    reg_kv.close()
    nKU = p.tot[sKU]
    p.dma("sync", io["cc_in"].rearrange("(h d) e -> d h e", d=128),
          kv.rearrange("p (h e) -> p h e", h=L0_NH), inc=t("cx"),
          wait=[(sKU, nKU)])
    p.emit("gpsimd", "collective_compute", "AllGather", ALU.bypass,
           replica_groups=[list(range(8))],
           ins=[io["cc_in"]], outs=[io["cc_out"]],
           wait=[(t("cx"), p.tot[t("cx")])], inc=t("cc"))
    p.emit("gpsimd", "memset", kv[:], 0.0, wait=[(t("cx"), p.tot[t("cx")])],
           inc=t("ms2"))

    regF = ExitStack()
    qdecT = regF.enter_context(nc.sbuf_tensor(t("qd2"), [128, L0_NH * 128], BF16))
    qd_sb = regF.enter_context(nc.sbuf_tensor(t("qs2"), [128, 512], BF16))
    p.dma("gpsimd", qdecT.rearrange("p (h c) -> p h c", h=L0_NH),
          io["qdecT"].rearrange("(h d) c -> d h c", d=128), inc=t("tl2g"))
    gbuf = regF.enter_context(nc.sbuf_tensor(t("gb"), [128, 2 * 2048], F32))
    w_sb = regF.enter_context(nc.sbuf_tensor(t("wt"), [128, 128], F32))
    ssf = regF.enter_context(nc.sbuf_tensor(t("sf"), [128, 2048], F32))
    ssb = regF.enter_context(nc.sbuf_tensor(t("sb"), [128, 2048], BF16))
    p.dma("sync", w_sb[:], io["w_l0"], inc=t("tl2"))
    sGL, sCMB, sSB, sQD2, sCR, sCA = (t(x) for x in
        ("gl", "cmb", "ssb", "qd2", "cr", "ca"))
    for s in (sGL, sCMB, sSB, sQD2, sCR, sCA):
        p.sem(s)
    for cp in range(8):
        buf = cp % 2
        slot = io["cc_out"][cp * 2048:(cp + 1) * 2048, :]
        war = [(sCMB, 16 * (cp - 1))] if cp >= 2 else []
        sGLb = sGL + str(buf)
        p.dma("sync", gbuf.rearrange("p (b h e) -> p b h e", b=2, h=L0_NH)[:, buf],
              slot.rearrange("(h d) e -> d h e", d=128), inc=sGLb,
              wait=[(t("cc"), p.tot[t("cc")])] + war)
        gl_sem, glv = sGLb, p.tot[sGLb]
        for h in range(L0_NH):
            w = [(gl_sem, glv)] if h == 0 else []
            if cp == 0 and h == 0:
                w += [(t("ms2"), p.tot[t("ms2")]), (t("tl2"), p.tot[t("tl2")])]
            if cp > 0:
                w.append((sCMB, (cp - 1) * L0_NH + h + 1))
            p.emit("vector", "scalar_tensor_tensor",
                   kv[:, h * 128:(h + 1) * 128],
                   gbuf[:, buf * 2048 + h * 128:buf * 2048 + (h + 1) * 128],
                   w_sb[:, cp * 16 + h:cp * 16 + h + 1],
                   kv[:, h * 128:(h + 1) * 128],
                   ALU.mult, ALU.add, wait=w, inc=sCMB)
    nCMB = p.tot[sCMB]
    sSF = t("ssf")
    p.sem(sSF)
    for n in range(4):
        p.emit("scalar", "activation", ssf[:, n * 512:(n + 1) * 512],
               kv[:, n * 512:(n + 1) * 512], AF.Copy,
               wait=[(sCMB, nCMB)] if n == 0 else (), inc=sSF)
    for h in range(L0_NH):
        p.const(rC[h])
    for i in range(NTB):
        if i > 0:
            for h in range(L0_NH):
                p.emit("scalar", "activation", ssf[:, h * 128:(h + 1) * 128],
                       ssf[:, h * 128:(h + 1) * 128], AF.Copy, scale=rC[h],
                       wait=([(sCR, 4 * i)] if h == 0 else []) +
                            ([(sSB, i)] if h == 0 else []),
                       inc=sSF)
        for n in range(4):
            p.emit("scalar", "activation", ssb[:, n * 512:(n + 1) * 512],
                   ssf[:, n * 512:(n + 1) * 512], AF.Copy,
                   wait=([(sCR, 4 * i)] if (i > 0 and n == 0) else []) +
                        ([(sSF, p.tot[sSF])] if n == 0 else []),
                   inc=(sSB if n == 3 else None))
        for hg in range(4):
            g = i * 4 + hg
            p.emit("vector", "tensor_mul",
                   qd_sb.rearrange("p (j c) -> p j c", j=4),
                   qT_v[:, hg * 4:hg * 4 + 4, i * 128:(i + 1) * 128],
                   qdecT.rearrange("p (h c) -> p h c", h=L0_NH)[
                       :, hg * 4:hg * 4 + 4, :],
                   wait=[(sCR, g)] +
                        ([(t("tl2g"), p.tot[t("tl2g")])] if g == 0 else []),
                   inc=sQD2)
            for j in range(4):
                h = hg * 4 + j
                w = []
                if j == 0:
                    w = [(sSB, p.tot[sSB]), (sQD2, p.tot[sQD2]), (sCA, g)]
                p.emit("tensor", "matmul", ps_o[:, j * 128:(j + 1) * 128],
                       ssb[:, h * 128:(h + 1) * 128],
                       qd_sb[:, j * 128:(j + 1) * 128],
                       start=True, stop=True, wait=w,
                       inc=(sCR if j == 3 else None))
            p.emit("vector", "tensor_tensor",
                   at_v[:, hg * 4:hg * 4 + 4, i * 128:(i + 1) * 128],
                   at_v[:, hg * 4:hg * 4 + 4, i * 128:(i + 1) * 128],
                   ps_o.rearrange("p (j c) -> p j c", j=4), ALU.add,
                   wait=[(sCR, p.tot[sCR])], inc=sCA)
    nCA = p.tot[sCA]
    regF.close()
    reg_o.close()
    reg_q.close()
    reg_st.close()

    regW = ExitStack()
    Wo_sb = regW.enter_context(nc.sbuf_tensor(t("wo"), [128, NKC * 2048], BF16))
    rmsw = regW.enter_context(nc.sbuf_tensor(t("rw"), [128, 16], F32))
    sqb = regW.enter_context(nc.sbuf_tensor(t("sqb"), [128, NKC * T], BF16))
    ones = regW.enter_context(nc.sbuf_tensor(t("on"), [128, 1], BF16))
    istd_sb = regW.enter_context(nc.sbuf_tensor(t("is"), [1, 3 * 1024], F32))
    istd_t = regW.enter_context(nc.sbuf_tensor(t("it"), [128, 8], F32))
    xin = regW.enter_context(nc.sbuf_tensor(t("xi"), [128, 2 * D], F32))
    xout = regW.enter_context(nc.sbuf_tensor(t("xo"), [128, 2 * D], F32))
    psW = [regW.enter_context(nc.psum_tensor(t(f"pw{i}"), [128, 512], F32))
           for i in range(4)]
    ps_ss = [regW.enter_context(nc.psum_tensor(t(f"px{i}"), [1, 512], F32))
             for i in range(2)]
    ps_it = regW.enter_context(nc.psum_tensor(t("pi"), [128, 8], F32))
    (sWOL, sWOG, sONE, sFLD, sSQ2, sSQM, sIC, sIT, sITE, sMM, sEVW, sXL,
     sXA, sXO) = (
        t(x) for x in ("wol", "wog", "one", "fld", "sq2", "sqm", "ic", "it",
                       "ite", "mm", "evw", "xl", "xa", "xo"))
    for s in (sWOL, sWOG, sONE, sFLD, sSQ2, sSQM, sIC, sIT, sITE, sMM, sEVW,
              sXL, sXA, sXO):
        p.sem(s)
    p.const(RMS_EPS)
    f_done = [(sCA, nCA), (sSB, p.tot[sSB])]
    p.dma("sync", rmsw[:], io["rms_w_t"], inc=sWOL, wait=f_done)
    for q in range(4):
        p.dma("gpsimd",
              Wo_sb.rearrange("p (kc n) -> p kc n", kc=NKC)[:, q * 4:(q + 1) * 4],
              io["Wo"].rearrange("(kc p) n -> p kc n", p=128)[:, q * 4:(q + 1) * 4],
              inc=sWOG, wait=f_done if q == 0 else ())
    p.emit("gpsimd", "memset", ones[:], 1.0, inc=sONE, wait=f_done)
    for kc in range(NKC):
        p.emit("scalar", "activation", Wo_sb[:, kc * 2048:(kc + 1) * 2048],
               Wo_sb[:, kc * 2048:(kc + 1) * 2048], AF.Copy,
               scale=rmsw[:, kc:kc + 1],
               wait=([(sWOL, p.tot[sWOL]), (sWOG, p.tot[sWOG])]
                     if kc == 0 else ()), inc=sFLD)
    p.emit("scalar", "activation", sqb[:], attnT[:], AF.Square,
           wait=[(sCA, nCA)], inc=sSQ2)
    for half in range(2):
        for kc in range(NKC):
            w = []
            if kc == 0:
                w = [(sSQ2, p.tot[sSQ2]), (sONE, p.tot[sONE])]
            p.emit("tensor", "matmul", ps_ss[half][:, :], ones[:],
                   sqb[:, kc * T + half * 512:kc * T + half * 512 + 512],
                   start=(kc == 0), stop=(kc == NKC - 1), wait=w,
                   inc=(sSQM if kc == NKC - 1 else None))
    for half in range(2):
        p.emit("vector", "tensor_scalar_mul",
               istd_sb[0:1, half * 512:(half + 1) * 512],
               ps_ss[half][:, :], 1.0 / D,
               wait=[(sSQM, p.tot[sSQM])], inc=sIC)
        p.emit("scalar", "activation",
               istd_sb[0:1, 1024 + half * 512:1024 + (half + 1) * 512],
               istd_sb[0:1, half * 512:(half + 1) * 512],
               AF.Sqrt, bias=RMS_EPS, wait=[(sIC, p.tot[sIC])], inc=sIC)
        p.emit("vector", "reciprocal",
               istd_sb[0:1, 2048 + half * 512:2048 + (half + 1) * 512],
               istd_sb[0:1, 1024 + half * 512:1024 + (half + 1) * 512],
               wait=[(sIC, p.tot[sIC])], inc=sIC)
    nIC = p.tot[sIC]
    for tb in range(NTB):
        p.emit("tensor", "transpose", ps_it[:, tb:tb + 1],
               istd_sb[0:1, 2048 + tb * 128:2048 + (tb + 1) * 128],
               ident_sb[0:1, 0:1],
               wait=[(sIC, nIC)] if tb == 0 else (),
               inc=(sIT if tb == NTB - 1 else None))
    p.emit("vector", "tensor_copy", istd_t[:], ps_it[:],
           wait=[(sIT, p.tot[sIT])], inc=sITE)
    for tb in range(NTB):
        bi = tb % 2
        sXLb = sXL + str(bi)
        p.dma("sync", xin[:, bi * D:(bi + 1) * D],
              io["x_src"][tb * 128:(tb + 1) * 128, :], inc=sXLb,
              wait=[(sXA, max(0, tb - 1))] if tb >= 2 else ())
        xl_sem, xlv = sXLb, p.tot[sXLb]
        for n in range(4):
            for kc in range(NKC):
                w = []
                if kc == 0:
                    w = [(sEVW, p.tot[sEVW])]
                    if n == 0 and tb == 0:
                        w += [(sFLD, p.tot[sFLD])]
                p.emit("tensor", "matmul", psW[n][:, :],
                       attnT[:, kc * T + tb * 128:kc * T + (tb + 1) * 128],
                       Wo_sb[:, kc * 2048 + n * 512:kc * 2048 + (n + 1) * 512],
                       start=(kc == 0), stop=(kc == NKC - 1), wait=w,
                       inc=(sMM if kc == NKC - 1 else None))
        nMM = p.tot[sMM]
        for n in range(4):
            w = [(sMM, nMM)] if n == 0 else []
            if tb == 0 and n == 0:
                w.append((sITE, p.tot[sITE]))
            if tb >= 2 and n == 0:
                w.append((sXO + str(tb % 2), p.tot[sXO + str(tb % 2)]))
            p.emit("scalar", "activation",
                   xout[:, bi * D + n * 512:bi * D + (n + 1) * 512],
                   psW[n][:, :], AF.Copy, scale=istd_t[:, tb:tb + 1],
                   wait=w, inc=sEVW)
        p.emit("vector", "tensor_add", xout[:, bi * D:(bi + 1) * D],
               xout[:, bi * D:(bi + 1) * D], xin[:, bi * D:(bi + 1) * D],
               wait=[(sEVW, p.tot[sEVW]), (xl_sem, xlv)], inc=sXA)
        p.dma("sync", io["x_dst"][tb * 128:(tb + 1) * 128, :],
              xout[:, bi * D:(bi + 1) * D], inc=sXO + str(bi),
              wait=[(sXA, p.tot[sXA])])
    p.wait("sync", sXO + "0", p.tot[sXO + "0"])
    p.wait("sync", sXO + "1", p.tot[sXO + "1"])
    done = [(sXO + "0", p.tot[sXO + "0"]), (sXO + "1", p.tot[sXO + "1"])]
    regW.close()
    reg.close()
    return done


MOE_H = 1408
MOE_NHC = 11
MOE_E = 8
MOE_JS = [(0, 384), (384, 384), (768, 384), (1152, 256)]


def emit_l0_ffn(p, io):
    nc = p.nc
    u = p.uid()
    t = lambda n: f"m{u}{n}"
    p.const(1.0)
    p.const(-1e30)

    reg = ExitStack()
    ident_sb = reg.enter_context(nc.sbuf_tensor(t("id"), [128, 128], F32))
    ident_bf = reg.enter_context(nc.sbuf_tensor(t("ib"), [128, 128], BF16))
    acc = reg.enter_context(nc.sbuf_tensor(t("acc"), [128, NTB * D], F32))
    yT = reg.enter_context(nc.sbuf_tensor(t("yT"), [128, NKC * T], BF16))
    ws = reg.enter_context(nc.sbuf_tensor(t("ws"), [128, NTB * 8], F32))
    p.dma("sync", ident_sb[:], io["ident"], inc=t("tl"))
    p.dma("gpsimd", ident_bf[:], io["ident"], inc=t("tlg"))
    p.dma("sync", acc.rearrange("p (tb d) -> p tb d", tb=NTB),
          io["x_src"].rearrange("(tb p) d -> p tb d", p=128), inc=t("xld"))

    regA = ExitStack()
    xw = ((t("xld"), p.tot[t("xld")]), (t("tl"), p.tot[t("tl")]))
    sLN, nLN = emit_ln_yT(p, acc[:], yT[:], ident_sb, regA, t("ln"),
                          x_ready_v=xw, x_ready_s=xw, side="right")
    regA.close()

    regG = ExitStack()
    gw = regG.enter_context(nc.sbuf_tensor(t("gw"), [128, NKC * 8], BF16,
                                           side="right"))
    bias = regG.enter_context(nc.sbuf_tensor(t("bi"), [128, 8], F32,
                                             side="right"))
    gs = regG.enter_context(nc.sbuf_tensor(t("gs"), [128, 64], F32,
                                           side="right"))
    psG = regG.enter_context(nc.psum_tensor(t("pg"), [128, 8], F32))
    p.dma("gpsimd", gw.rearrange("p (kc e) -> p kc e", kc=NKC),
          io["gate_wT"].rearrange("(kc p) e -> p kc e", p=128), inc=t("glg"),
          wait=((sLN, nLN),))
    p.dma("sync", bias[:], io["bias_t"], inc=t("gl"), wait=((sLN, nLN),))
    sGM, sGE, sGC = t("gm"), t("ge"), t("gc")
    for s in (sGM, sGE, sGC):
        p.sem(s)
    for tb in range(NTB):
        for kc in range(NKC):
            w = []
            if kc == 0:
                w = ([(t("gl"), p.tot[t("gl")]), (t("glg"), p.tot[t("glg")])]
                     if tb == 0 else [])
                w.append((sGE, p.tot[sGE]))
            p.emit("tensor", "matmul", psG[:],
                   yT[:, kc * T + tb * 128:kc * T + (tb + 1) * 128],
                   gw[:, kc * 8:(kc + 1) * 8],
                   start=(kc == 0), stop=(kc == NKC - 1), wait=w,
                   inc=(sGM if kc == NKC - 1 else None))
        p.emit("scalar", "activation", gs[:, 0:8], psG[:], AF.Sigmoid,
               wait=[(sGM, p.tot[sGM]), (sGC, p.tot[sGC])], inc=sGE)

        def vop(fn, *a, w=(), **k):
            p.emit("vector", fn, *a,
                   wait=[(sGC, p.tot[sGC])] + list(w), inc=sGC, **k)
        vop("tensor_add", gs[:, 8:16], gs[:, 0:8], bias[:],
            w=[(sGE, p.tot[sGE]), (t("gl"), p.tot[t("gl")])])

        def top1(vcol, ohcol):
            vop("reduce_max", gs[:, 56:57], gs[:, vcol:vcol + 8], axis=AX.X)
            vop("tensor_single_scalar", gs[:, 16:24], gs[:, vcol:vcol + 8],
                gs[:, 56:57], ALU.is_equal)
            vop("tensor_copy", gs[:, 24:32], gs[:, 16:24])
            a, b = 24, 32
            for k in (1, 2, 4):
                vop("tensor_copy", gs[:, b:b + k], gs[:, a:a + k])
                vop("tensor_add", gs[:, b + k:b + 8], gs[:, a + k:a + 8],
                    gs[:, a:a + 8 - k])
                a, b = b, a
            vop("tensor_single_scalar", gs[:, b:b + 8], gs[:, a:a + 8],
                1.0, ALU.is_equal)
            vop("tensor_mul", gs[:, ohcol:ohcol + 8], gs[:, 16:24],
                gs[:, b:b + 8])
        top1(8, 40)
        vop("tensor_scalar_mul", gs[:, 48:56], gs[:, 40:48], -1e30)
        vop("tensor_add", gs[:, 8:16], gs[:, 8:16], gs[:, 48:56])
        top1(8, 48)
        vop("tensor_add", gs[:, 40:48], gs[:, 40:48], gs[:, 48:56])
        vop("tensor_mul", ws[:, tb * 8:(tb + 1) * 8], gs[:, 40:48],
            gs[:, 0:8])
    nWS = p.tot[sGC]
    regG.close()

    regE = ExitStack()
    hT = regE.enter_context(nc.sbuf_tensor(t("hT"), [128, MOE_NHC * T], BF16))
    W2T = regE.enter_context(nc.sbuf_tensor(t("w2"), [128, MOE_NHC * D], BF16))
    w1b = [regE.enter_context(nc.sbuf_tensor(t(f"w1{i}"), [128, NKC * 384], BF16))
           for i in range(2)]
    w3b = regE.enter_context(nc.sbuf_tensor(t("w3"), [128, NKC * 384], BF16))
    htok = regE.enter_context(nc.sbuf_tensor(t("ht"), [128, 512], BF16))
    t3 = regE.enter_context(nc.sbuf_tensor(t("t3"), [128, 512], BF16))
    ps1 = regE.enter_context(nc.psum_tensor(t("p1"), [128, 384], F32))
    ps3 = regE.enter_context(nc.psum_tensor(t("p3"), [128, 384], F32))
    psT = regE.enter_context(nc.psum_tensor(t("pt"), [128, 384], BF16))
    psW = [regE.enter_context(nc.psum_tensor(t(f"pw{i}"), [128, 512], F32))
           for i in range(4)]
    sW1, sW3, sW2L, sZ1, sZ3, sH1, sH3, sHM, sTP, sTE, sW2M, sAC = (
        t(x) for x in ("w1l", "w3l", "w2l", "z1", "z3", "h1", "h3", "hm",
                       "tp", "te", "w2m", "ac"))
    for s in (sW1, sW3, sW2L, sZ1, sZ3, sH1, sH3, sHM, sTP, sTE, sW2M, sAC):
        p.sem(s)

    def w13_src(key, e, j0, jw):
        src = io[key]
        if e == 0:
            ap = src.rearrange("(kc p) n -> p kc n", p=128)
        else:
            ap = src[e - 1].rearrange("(kc p) n -> p kc n", p=128)
        return ap[:, :, j0:j0 + jw]

    nslice = len(MOE_JS)
    w1_done = {}
    for e in range(MOE_E + 1):
        k1, k3, k2 = (("sw1T", "sw3T", "sw2T") if e == 0
                      else ("w1T", "w3T", "w2T"))
        w2src = (io[k2] if e == 0 else io[k2][e - 1])
        p.dma("gpsimd",
              W2T.rearrange("p (hc n) -> p hc n", hc=MOE_NHC),
              w2src.rearrange("(hc p) n -> p hc n", p=128), inc=sW2L,
              wait=[(sW2M, p.tot[sW2M])] +
                   ([(sLN, nLN)] if e == 0 else []))
        w2lv = p.tot[sW2L]
        for j, (j0, jw) in enumerate(MOE_JS):
            sl = e * nslice + j
            buf = sl % 2
            war = [(sZ1, w1_done[sl - 2])] if sl >= 2 else [(sLN, nLN)]
            p.dma("gpsimd", w1b[buf][:, 0:NKC * jw].rearrange(
                      "p (kc n) -> p kc n", kc=NKC),
                  w13_src(k1, e, j0, jw), inc=sW1 + str(buf), wait=war)
            w1lv = p.tot[sW1 + str(buf)]
            p.dma("gpsimd", w3b[:, 0:NKC * jw].rearrange(
                      "p (kc n) -> p kc n", kc=NKC),
                  w13_src(k3, e, j0, jw), inc=sW3,
                  wait=[(sZ3, p.tot[sZ3])])
            w3lv = p.tot[sW3]
            for tb in range(NTB):
                scale = 1.0 if e == 0 else ws[:, tb * 8 + (e - 1):
                                             tb * 8 + e]
                for kc in range(NKC):
                    w = []
                    if kc == 0:
                        w = [(sW1 + str(buf), w1lv), (sH1, p.tot[sH1])]
                        if e == 0 and j == 0 and tb == 0:
                            w.append((sLN, nLN))
                            w.append((sGE, p.tot[sGE]))
                    p.emit("tensor", "matmul", ps1[:, 0:jw],
                           yT[:, kc * T + tb * 128:kc * T + (tb + 1) * 128],
                           w1b[buf][:, kc * jw:(kc + 1) * jw],
                           start=(kc == 0), stop=(kc == NKC - 1), wait=w,
                           inc=(sZ1 if kc == NKC - 1 else None))
                nz1 = p.tot[sZ1]
                for kc in range(NKC):
                    w = []
                    if kc == 0:
                        w = [(sW3, w3lv), (sH3, p.tot[sH3])]
                    p.emit("tensor", "matmul", ps3[:, 0:jw],
                           yT[:, kc * T + tb * 128:kc * T + (tb + 1) * 128],
                           w3b[:, kc * jw:(kc + 1) * jw],
                           start=(kc == 0), stop=(kc == NKC - 1), wait=w,
                           inc=(sZ3 if kc == NKC - 1 else None))
                nz3 = p.tot[sZ3]
                p.emit("scalar", "activation", htok[:, 0:jw], ps1[:, 0:jw],
                       AF.Silu, scale=scale,
                       wait=[(sZ1, nz1), (sHM, p.tot[sHM]), (sTP, p.tot[sTP])] +
                            ([(sGC, nWS)] if (e == 1 and j == 0 and tb == 0)
                             else []),
                       inc=sH1)
                p.emit("scalar", "activation", t3[:, 0:jw], ps3[:, 0:jw],
                       AF.Identity, scale=scale,
                       wait=[(sZ3, nz3)], inc=sH3)
                p.emit("vector", "tensor_mul", htok[:, 0:jw], htok[:, 0:jw],
                       t3[:, 0:jw],
                       wait=[(sH1, p.tot[sH1]), (sH3, p.tot[sH3])], inc=sHM)
                nhm = p.tot[sHM]
                ntp = jw // 128
                for jj in range(ntp):
                    p.emit("tensor", "transpose",
                           psT[:, jj * 128:(jj + 1) * 128],
                           htok[:, jj * 128:(jj + 1) * 128], ident_bf[:],
                           wait=([(sHM, nhm), (sTE, p.tot[sTE])] +
                                 ([(t("tlg"), p.tot[t("tlg")])]
                                  if (e == 0 and j == 0 and tb == 0) else [])
                                 if jj == 0 else []),
                           inc=(sTP if jj == ntp - 1 else None))
                hcb = j0 // 128
                outv = hT.rearrange("p (hc tt) -> p hc tt", hc=MOE_NHC)[
                    :, hcb:hcb + ntp, tb * 128:(tb + 1) * 128]
                p.emit("vector", "tensor_copy", outv,
                       psT[:, 0:ntp * 128].rearrange(
                           "p (jj tt) -> p jj tt", jj=ntp),
                       wait=[(sTP, p.tot[sTP])], inc=sTE)
            w1_done[sl] = p.tot[sZ1]
        nTE = p.tot[sTE]
        for tb in range(NTB):
            for hc in range(MOE_NHC):
                for n in range(4):
                    w = []
                    if hc == 0 and n == 0:
                        w = [(sAC, p.tot[sAC])]
                        if tb == 0:
                            w += [(sW2L, w2lv), (sTE, nTE)]
                    p.emit("tensor", "matmul", psW[n][:],
                           hT[:, hc * T + tb * 128:hc * T + (tb + 1) * 128],
                           W2T[:, hc * D + n * 512:hc * D + (n + 1) * 512],
                           start=(hc == 0), stop=(hc == MOE_NHC - 1), wait=w,
                           inc=(sW2M if (hc == MOE_NHC - 1 and n == 3)
                                else None))
            nw2m = p.tot[sW2M]
            for n in range(4):
                p.emit("vector", "tensor_add",
                       acc[:, tb * D + n * 512:tb * D + (n + 1) * 512],
                       acc[:, tb * D + n * 512:tb * D + (n + 1) * 512],
                       psW[n][:],
                       wait=[(sW2M, nw2m)] if n == 0 else [], inc=sAC)
    nAC = p.tot[sAC]
    for tb in range(NTB):
        p.dma("sync", io["x_dst"][tb * 128:(tb + 1) * 128, :],
              acc[:, tb * D:(tb + 1) * D], inc=t("xo"),
              wait=[(sAC, nAC)] if tb == 0 else ())
    p.wait("sync", t("xo"), p.tot[t("xo")])
    done = [(t("xo"), p.tot[t("xo")])]
    regE.close()
    reg.close()
    return done


G_H = 4
G_DK = 512
G_DV = 256
G_C = 64
G_NC = 16
G_SCALE = G_DK ** -0.5
G_NORM = 16.0


def emit_l1_attn(p, io):
    nc = p.nc
    u = p.uid()
    t = lambda n: f"g{u}{n}"
    p.const(1.0 / G_NORM)
    p.const(-1.0 / G_NORM)
    p.const(1.0)
    p.const(RMS_EPS)

    reg = ExitStack()
    ident_sb = reg.enter_context(nc.sbuf_tensor(t("id"), [128, 128], F32))
    ident_bf = reg.enter_context(nc.sbuf_tensor(t("ib"), [128, 128], BF16))
    p.dma("sync", ident_sb[:], io["ident"], inc=t("tl"))
    p.dma("gpsimd", ident_bf[:], io["ident"], inc=t("tlg"))
    reg_q2 = ExitStack()
    reg_kv2 = ExitStack()

    regY = ExitStack()
    yT = regY.enter_context(nc.sbuf_tensor(t("yT"), [128, NKC * T], BF16,
                                           side="right"))
    regA = ExitStack()
    x_sb = regA.enter_context(nc.sbuf_tensor(t("x"), [128, NTB * D], F32,
                                             side="right"))
    p.dma("sync", x_sb.rearrange("p (tb d) -> p tb d", tb=NTB),
          io["x_src"].rearrange("(tb p) d -> p tb d", p=128), inc=t("xld"))
    xw = ((t("xld"), p.tot[t("xld")]), (t("tl"), p.tot[t("tl")]))
    sLN, nLN = emit_ln_yT(p, x_sb[:], yT[:], ident_sb, regA, t("ln"),
                          x_ready_v=xw, x_ready_s=xw, side="right")
    regA.close()
    g_sb = reg.enter_context(nc.sbuf_tensor(t("g"), [128, NTB * 1024], BF16))
    qT = reg_q2.enter_context(nc.sbuf_tensor(t("qT"), [128, NKC * T], BF16))
    gk = reg_q2.enter_context(nc.sbuf_tensor(t("gk"), [128, NTB * D], BF16))
    Gcum = reg_q2.enter_context(
        nc.sbuf_tensor(t("gc"), [128, 16 * (G_NC + 1) + 16 * 17], F32))
    LT_sb = reg_q2.enter_context(nc.sbuf_tensor(t("lt"), [128, 64], BF16))
    id64f = reg_q2.enter_context(nc.sbuf_tensor(t("i6f"), [128, 64], F32))
    id64b = reg_q2.enter_context(nc.sbuf_tensor(t("i6b"), [128, 64], BF16))
    b_sb = reg_q2.enter_context(nc.sbuf_tensor(t("b"), [128, 2048], F32))
    bT_sb = reg_q2.enter_context(nc.sbuf_tensor(t("bT"), [128, 1024], F32))
    ktok = reg_kv2.enter_context(nc.sbuf_tensor(t("kt"), [128, NTB * D], BF16))
    v_sb = reg_kv2.enter_context(nc.sbuf_tensor(t("v"), [128, NTB * 1024], BF16))

    regP = ExitStack()
    Wb = [regP.enter_context(nc.sbuf_tensor(t(f"W{i}"), [128, NKC * 512], BF16,
                                            side="right"))
          for i in range(2)]
    psA = [regP.enter_context(nc.psum_tensor(t(f"pp{i}"), [128, 512], F32))
           for i in range(2)]
    psT = regP.enter_context(nc.psum_tensor(t("pt"), [128, 512], F32))
    sWL, sPM, sPE, sTP, sTE = (t(x) for x in ("wl", "pm", "pe", "tp", "te"))
    for s in (sWL, sPM, sPE, sTP, sTE):
        p.sem(s)
    jobs = [("Wq", "qTf", 4), ("Wk", "ktok", 4), ("Wv", "v", 2), ("Wg", "g", 2)]
    hidx = 0
    pm_done = {}
    for wname, kind, nq in jobs:
        for qu in range(nq):
            buf = hidx % 2
            war = ([(sPM, pm_done[hidx - 2])] if hidx >= 2 else [(sLN, nLN)])
            p.dma("gpsimd", Wb[buf].rearrange("p (kc n) -> p kc n", kc=NKC),
                  io[wname].rearrange("(kc p) n -> p kc n", p=128)[
                      :, :, qu * 512:(qu + 1) * 512],
                  inc=sWL + str(buf), wait=war)
            wlv = p.tot[sWL + str(buf)]
            if kind == "qTf":
                for oc in range(4):
                    kcg = qu * 4 + oc
                    for kc in range(NKC):
                        w = []
                        if kc == 0:
                            w = [(sWL + str(buf), wlv), (sPE, p.tot[sPE])]
                        for nn in range(2):
                            p.emit("tensor", "matmul",
                                   psA[nn][:, :],
                                   Wb[buf][:, kc * 512 + oc * 128:
                                           kc * 512 + (oc + 1) * 128],
                                   yT[:, kc * T + nn * 512:
                                      kc * T + (nn + 1) * 512],
                                   start=(kc == 0), stop=(kc == NKC - 1),
                                   wait=w if nn == 0 else [],
                                   inc=(sPM if (kc == NKC - 1 and nn == 1)
                                        else None))
                    pmv = p.tot[sPM]
                    for nn in range(2):
                        p.emit("scalar", "activation",
                               qT[:, kcg * T + nn * 512:kcg * T + (nn + 1) * 512],
                               psA[nn][:], AF.Copy, scale=G_SCALE,
                               wait=[(sPM, pmv)] if nn == 0 else (), inc=sPE)
                pm_done[hidx] = p.tot[sPM]
            else:
                dst = {"ktok": ktok, "v": v_sb, "g": g_sb}[kind]
                wid = D if kind == "ktok" else 1024
                for tb in range(NTB):
                    for kc in range(NKC):
                        w = []
                        if kc == 0:
                            w = [(sWL + str(buf), wlv), (sPE, p.tot[sPE])]
                        p.emit("tensor", "matmul", psA[tb % 2][:, :],
                               yT[:, kc * T + tb * 128:kc * T + (tb + 1) * 128],
                               Wb[buf][:, kc * 512:(kc + 1) * 512],
                               start=(kc == 0), stop=(kc == NKC - 1),
                               wait=w,
                               inc=(sPM if kc == NKC - 1 else None))
                    pmv = p.tot[sPM]
                    p.emit("vector", "tensor_copy",
                           dst[:, tb * wid + qu * 512:tb * wid + qu * 512 + 512],
                           psA[tb % 2][:], wait=[(sPM, pmv)], inc=sPE)
                pm_done[hidx] = p.tot[sPM]
            hidx += 1
    regP.close()
    regGK = ExitStack()
    wgk1 = regGK.enter_context(nc.sbuf_tensor(t("w1"), [128, NKC * 16], BF16,
                                              side="right"))
    wgk2 = regGK.enter_context(nc.sbuf_tensor(t("w2"), [32, 2048], BF16,
                                              side="right"))
    bgk = regGK.enter_context(nc.sbuf_tensor(t("bg"), [1, 2048], BF16,
                                             side="right"))
    bgkb = regGK.enter_context(nc.sbuf_tensor(t("bb"), [128, 2048], F32,
                                              side="right"))
    z16 = regGK.enter_context(nc.sbuf_tensor(t("z6"), [128, NTB * 16], F32,
                                             side="right"))
    z16T = regGK.enter_context(nc.sbuf_tensor(t("zT"), [32, NTB * 128], BF16,
                                              side="right"))
    zb = regGK.enter_context(nc.sbuf_tensor(t("zb"), [128, 1024], F32,
                                            side="right"))
    t1 = regGK.enter_context(nc.sbuf_tensor(t("t1"), [128, 1024], F32,
                                            side="right"))
    onecol = regGK.enter_context(nc.sbuf_tensor(t("oc"), [1, 128], BF16,
                                                side="right"))
    psB = regGK.enter_context(nc.psum_tensor(t("pb"), [128, 512], F32))
    sGKL, sGKM, sGKE, sGKT, sGKV, sGKS = (t(x) for x in
        ("gkl", "gkm", "gke", "gkt", "gkv", "gks"))
    for s in (sGKL, sGKM, sGKE, sGKT, sGKV, sGKS):
        p.sem(s)
    p.dma("gpsimd", wgk1.rearrange("p (kc e) -> p kc e", kc=NKC),
          io["Wgk1"].rearrange("(kc p) e -> p kc e", p=128), inc=t("gkg"),
          wait=[(sPM, p.tot[sPM])])
    p.emit("gpsimd", "memset", wgk2[:], 0.0, inc=t("gkg2"))
    p.emit("gpsimd", "memset", z16T[:], 0.0, inc=t("gkg2"))
    p.dma("gpsimd", wgk2[0:16, :], io["Wgk2"], inc=t("gkg"),
          wait=[(t("gkg2"), 1)])
    p.dma("gpsimd", bgk[:], io["bgk2_r"], inc=t("gkg"))
    p.emit("gpsimd", "memset", onecol[:], 1.0, inc=t("gkg2"))
    ngkg = p.tot[t("gkg")]
    for nn in range(4):
        p.emit("tensor", "matmul", psB[:],
               onecol[:], bgk[0:1, nn * 512:(nn + 1) * 512],
               start=True, stop=True,
               wait=([(t("gkg2"), p.tot[t("gkg2")]), (t("gkg"), ngkg),
                      (sPE, p.tot[sPE])] if nn == 0 else
                     [(sGKE, p.tot[sGKE])]),
               inc=sGKM)
        p.emit("vector", "tensor_copy", bgkb[:, nn * 512:(nn + 1) * 512],
               psB[:], wait=[(sGKM, p.tot[sGKM])], inc=sGKE)
    psZ = regGK.enter_context(nc.psum_tensor(t("pz"), [16, 128], F32))
    for tb in range(NTB):
        for kc in range(NKC):
            w = []
            if kc == 0:
                w = [(sGKE, p.tot[sGKE]), (sGKV, p.tot[sGKV])]
            p.emit("tensor", "matmul", psB[:, 0:16],
                   yT[:, kc * T + tb * 128:kc * T + (tb + 1) * 128],
                   wgk1[:, kc * 16:(kc + 1) * 16],
                   start=(kc == 0), stop=(kc == NKC - 1), wait=w,
                   inc=(sGKM if kc == NKC - 1 else None))
        p.emit("vector", "tensor_copy", z16[:, tb * 16:(tb + 1) * 16],
               psB[:, 0:16], wait=[(sGKM, p.tot[sGKM])], inc=sGKE)
        p.emit("tensor", "transpose", psZ[:],
               z16[:, tb * 16:(tb + 1) * 16], ident_sb[:],
               wait=[(sGKE, p.tot[sGKE]), (sGKT, p.tot[sGKT])], inc=sGKM)
        p.emit("vector", "tensor_copy", z16T[0:16, tb * 128:(tb + 1) * 128],
               psZ[:], wait=[(sGKM, p.tot[sGKM])] +
                            ([(t("gkg2"), p.tot[t("gkg2")])] if tb == 0
                             else []), inc=sGKT)
        ntz = p.tot[sGKT]
        for hh in range(2):
            for nn in range(2):
                p.emit("tensor", "matmul", psB[:, 0:512],
                       z16T[0:32, tb * 128:(tb + 1) * 128],
                       wgk2[0:32, (2 * hh + nn) * 512:(2 * hh + nn + 1) * 512],
                       start=True, stop=True,
                       wait=([(sGKT, ntz)] if (hh == 0 and nn == 0) else []) +
                            [(sGKV, p.tot[sGKV])],
                       inc=sGKM)
                nm = p.tot[sGKM]
                p.emit("vector", "tensor_add", zb[:, nn * 512:(nn + 1) * 512],
                       psB[:, 0:512],
                       bgkb[:, (2 * hh + nn) * 512:(2 * hh + nn + 1) * 512],
                       wait=[(sGKM, nm)] +
                            ([(sGKS, p.tot[sGKS])] if nn == 0 else []),
                       inc=sGKV)
            nzb = p.tot[sGKV]
            p.emit("scalar", "activation", t1[:], zb[:], AF.Abs,
                   wait=[(sGKV, nzb), (sGKS, p.tot[sGKS])], inc=sGKS)
            p.emit("scalar", "activation", t1[:], t1[:], AF.Exp, scale=-1.0,
                   wait=[(sGKS, p.tot[sGKS])], inc=sGKS)
            p.emit("scalar", "activation", t1[:], t1[:], AF.Ln, bias=1.0,
                   wait=[(sGKS, p.tot[sGKS])], inc=sGKS)
            p.emit("vector", "tensor_scalar_min", zb[:], zb[:], 0.0,
                   wait=[(sGKS, p.tot[sGKS])], inc=sGKV)
            p.emit("vector", "tensor_sub",
                   gk[:, tb * D + hh * 1024:tb * D + (hh + 1) * 1024], zb[:],
                   t1[:], wait=[(sGKS, p.tot[sGKS]), (sGKV, p.tot[sGKV])],
                   inc=sGKV)

    nGK = p.tot[sGKV]
    nPE_proj = p.tot[sPE]
    if "dbg_z16" in io:
        p.dma("sync", io["dbg_bb"], bgkb[:], inc=t("dz6"), wait=[(sGKV, nGK)])
        p.dma("sync", io["dbg_z16"], z16[:], inc=t("dz6"), wait=[(sGKV, nGK)])
        p.dma("gpsimd", io["dbg_z16T"], z16T[:], inc=t("dz7"),
              wait=[(sGKV, nGK)])
        p.dma("gpsimd", io["dbg_w2"], wgk2[:], inc=t("dz7"))
        p.wait("sync", t("dz6"), p.tot[t("dz6")])
        p.wait("gpsimd", t("dz7"), p.tot[t("dz7")])
    regGK.close()
    regY.close()
    o_sb = reg.enter_context(nc.sbuf_tensor(t("o"), [128, NTB * 1024], BF16,
                                            side="right"))

    reg_ps = ExitStack()
    ps_b = reg_ps.enter_context(nc.psum_tensor(t("p0"), [128, 512], F32))
    ps_t2 = reg_ps.enter_context(nc.psum_tensor(t("p1"), [128, 1024], F32))
    ps_o = reg_ps.enter_context(nc.psum_tensor(t("p3"), [128, 1024], F32))
    regS = ExitStack()
    maskT = regS.enter_context(nc.sbuf_tensor(t("mk"), [128, 256], BF16))
    S_all = regS.enter_context(nc.sbuf_tensor(t("S"), [128, 16 * 256], F32))
    S_bf = regS.enter_context(nc.sbuf_tensor(t("Sb"), [128, 16 * 256], BF16))
    enb = regS.enter_context(nc.sbuf_tensor(t("en"), [128, 2048], F32))
    kd_sb = regS.enter_context(nc.sbuf_tensor(t("kd"), [128, 2048], BF16))
    kdT = regS.enter_context(nc.sbuf_tensor(t("kdT"), [128, 1024], BF16))
    qeT = regS.enter_context(nc.sbuf_tensor(t("qe"), [128, 1024], BF16))
    ebT = regS.enter_context(nc.sbuf_tensor(t("eb"), [128, 16], F32))
    amt = regS.enter_context(nc.sbuf_tensor(t("am"), [128, 256], BF16))
    tS = regS.enter_context(nc.sbuf_tensor(t("tS"), [128, 1024], F32))
    ps_a = regS.enter_context(nc.psum_tensor(t("p2"), [128, 256], F32))
    ps_c = regS.enter_context(nc.psum_tensor(t("p4"), [128, 512], F32))
    ps_tb = regS.enter_context(nc.psum_tensor(t("p5"), [128, 1024], BF16))
    p.dma("gpsimd", LT_sb[:], io["LT"], inc=t("sl"))
    p.dma("gpsimd", maskT[:], io["maskTg"], inc=t("sl"))
    p.dma("sync", id64f[:], io["ident64"], inc=t("sl3"))
    p.dma("gpsimd", id64b[:], io["ident64"], inc=t("sl"))
    p.emit("gpsimd", "memset", S_all[:], 0.0, inc=t("sl2"),
           wait=[(sGKV, nGK), (sGKM, p.tot[sGKM]), (sGKS, p.tot[sGKS]),
                 (sGKE, p.tot[sGKE]), (sGKT, p.tot[sGKT])] +
                ([(t("dz6"), p.tot[t("dz6")]), (t("dz7"), p.tot[t("dz7")])]
                 if "dbg_z16" in io else []))
    p.emit("gpsimd", "memset", S_bf[:], 0.0, inc=t("sl2"))
    p.emit("gpsimd", "memset", Gcum[:, 0:16 * (G_NC + 1)], 0.0, inc=t("sl2"))
    nSL = p.tot[t("sl")]
    nSL2 = p.tot[t("sl2")]
    nSL3 = p.tot[t("sl3")]
    (sBM, sBE, sBT, sBTe, sENB, sKD, sKDT, sQE, sEBT, sAT, sAM, sOM, sOE,
     sCM, sTS, sSS, sSB2, sGC2) = (t(x) for x in
        ("bm", "be", "bt", "bte", "enb", "kd", "kdt", "qe2", "ebt", "at",
         "am2", "om", "oe", "cm", "ts", "ss", "sb2", "gc2"))
    for s in (sBM, sBE, sBT, sBTe, sENB, sKD, sKDT, sQE, sEBT, sAT, sAM,
              sOM, sOE, sCM, sTS, sSS, sSB2, sGC2):
        p.sem(s)

    for i in range(G_NC):
        tb = i // 2
        pb = 64 * (i % 2)
        tok0 = tb * D
        for nn in range(4):
            w = [(sBE, p.tot[sBE])]
            if i == 0 and nn == 0:
                w += [(sGKV, nGK), (t("sl"), nSL), (t("tlg"),
                      p.tot[t("tlg")])]
            p.emit("tensor", "matmul",
                   ps_b[pb:pb + 64, :],
                   LT_sb[pb:pb + 64, :],
                   gk[pb:pb + 64, tok0 + nn * 512:tok0 + (nn + 1) * 512],
                   start=True, stop=True, wait=w, inc=sBM)
            p.emit("vector", "tensor_copy",
                   b_sb[pb:pb + 64, nn * 512:(nn + 1) * 512],
                   ps_b[pb:pb + 64, :], wait=[(sBM, p.tot[sBM])], inc=sBE)
        nbe = p.tot[sBE]
        for kc in range(NKC):
            w = []
            if kc == 0:
                w = [(sBE, nbe), (sBTe, p.tot[sBTe])]
                if i == 0:
                    w.append((t("sl3"), nSL3))
            p.emit("tensor", "transpose",
                   ps_t2[:, kc * 64:(kc + 1) * 64],
                   b_sb[pb:pb + 64, kc * 128:(kc + 1) * 128],
                   id64f[pb:pb + 64, :],
                   wait=w, inc=(sBT if kc == NKC - 1 else None))
        p.emit("vector", "tensor_copy", bT_sb[:], ps_t2[:],
               wait=[(sBT, p.tot[sBT])], inc=sBTe)
        nbte = p.tot[sBTe]
        p.emit("scalar", "activation", enb[pb:pb + 64, :], b_sb[pb:pb + 64, :],
               AF.Exp, scale=-1.0 / G_NORM,
               wait=[(sBE, nbe), (sKD, p.tot[sKD])], inc=sENB)
        p.emit("vector", "tensor_mul", kd_sb[pb:pb + 64, :],
               ktok[pb:pb + 64, tok0:tok0 + D], enb[pb:pb + 64, :],
               wait=[(sENB, p.tot[sENB]), (sKDT, p.tot[sKDT])], inc=sKD)
        nkd = p.tot[sKD]
        for kc in range(NKC):
            w = []
            if kc == 0:
                w = [(sKD, nkd), (sAT, p.tot[sAT]), (sKDT, p.tot[sKDT])]
            p.emit("tensor", "transpose",
                   ps_tb[:, kc * 64:(kc + 1) * 64],
                   kd_sb[pb:pb + 64, kc * 128:(kc + 1) * 128],
                   id64b[pb:pb + 64, :],
                   wait=w, inc=(sBT if kc == NKC - 1 else None))
        p.emit("vector", "tensor_copy", kdT[:], ps_tb[:],
               wait=[(sBT, p.tot[sBT])], inc=sKDT)
        nkdt = p.tot[sKDT]
        p.emit("scalar", "activation", tS[:], bT_sb[:], AF.Exp,
               scale=1.0 / G_NORM,
               wait=[(sBTe, nbte), (sQE, p.tot[sQE])], inc=sENB)
        qv = qT.rearrange("p (kc tt) -> p kc tt", kc=NKC)[
            :, :, i * 64:(i + 1) * 64]
        p.emit("vector", "tensor_mul",
               qeT.rearrange("p (kc c) -> p kc c", kc=NKC),
               qv, tS.rearrange("p (kc c) -> p kc c", kc=NKC),
               wait=[(sENB, p.tot[sENB]), (sOM, p.tot[sOM])], inc=sQE)
        nqe = p.tot[sQE]
        for kc in range(NKC):
            p.emit("scalar", "activation", ebT[:, kc:kc + 1],
                   bT_sb[:, kc * 64 + 63:kc * 64 + 64], AF.Exp,
                   scale=1.0 / G_NORM,
                   wait=[(sBTe, nbte), (sSS, p.tot[sSS])] if kc == 0 else (),
                   inc=(sEBT if kc == NKC - 1 else None))
        for kc in range(NKC):
            w = []
            if kc == 0:
                w = [(sBE, nbe)]
                if i == 0:
                    w.append((t("sl2"), nSL2))
            p.emit("vector", "tensor_add",
                   Gcum[:, kc * (G_NC + 1) + i + 1:kc * (G_NC + 1) + i + 2],
                   Gcum[:, kc * (G_NC + 1) + i:kc * (G_NC + 1) + i + 1],
                   b_sb[pb + 63:pb + 64, kc * 128:kc * 128 + 1]
                   if False else
                   bT_sb[:, kc * 64 + 63:kc * 64 + 64],
                   wait=w + ([(sGC2, p.tot[sGC2])] if i > 0 else []),
                   inc=sGC2)
        for h in range(G_H):
            for dc in range(4):
                kc = h * 4 + dc
                w = []
                if h == 0 and dc == 0:
                    w = [(sKDT, nkdt), (sQE, nqe), (sAM, p.tot[sAM])]
                p.emit("tensor", "matmul",
                       ps_a[pb:pb + 64, h * 64:(h + 1) * 64],
                       kdT[:, kc * 64:(kc + 1) * 64],
                       qeT[:, kc * 64:(kc + 1) * 64],
                       start=(dc == 0), stop=(dc == 3), wait=w,
                       inc=(sAT if (h == 3 and dc == 3) else None))
        p.emit("vector", "tensor_mul", amt[pb:pb + 64, :],
               ps_a[pb:pb + 64, :], maskT[pb:pb + 64, :],
               wait=[(sAT, p.tot[sAT]), (sOM, p.tot[sOM])], inc=sAM)
        nam = p.tot[sAM]
        for h in range(G_H):
            w = []
            if h == 0:
                w = [(sAM, nam), (sOE, p.tot[sOE]), (sSB2, p.tot[sSB2])]
            p.emit("tensor", "matmul",
                   ps_o[pb:pb + 64, h * 256:(h + 1) * 256],
                   amt[pb:pb + 64, h * 64:(h + 1) * 64],
                   v_sb[pb:pb + 64, tb * 1024 + h * 256:
                        tb * 1024 + (h + 1) * 256],
                   start=True, stop=False, wait=w)
            for dc in range(4):
                kc = h * 4 + dc
                p.emit("tensor", "matmul",
                       ps_o[pb:pb + 64, h * 256:(h + 1) * 256],
                       qeT[:, kc * 64:(kc + 1) * 64],
                       S_bf[:, kc * 256:(kc + 1) * 256],
                       start=False, stop=(dc == 3),
                       inc=(sOM if (h == 3 and dc == 3) else None))
        p.emit("vector", "tensor_copy",
               o_sb[pb:pb + 64, tb * 1024:(tb + 1) * 1024],
               ps_o[pb:pb + 64, :], wait=[(sOM, p.tot[sOM])], inc=sOE)
        for h in range(G_H):
            for dcp in range(2):
                for dci in range(2):
                    dc = dcp * 2 + dci
                    kc = h * 4 + dc
                    w = []
                    if dci == 0:
                        w = [(sTS, p.tot[sTS])]
                        if h == 0 and dcp == 0:
                            w.append((sKD, nkd))
                    p.emit("tensor", "matmul",
                           ps_c[:, dci * 256:(dci + 1) * 256],
                           kd_sb[pb:pb + 64, kc * 128:(kc + 1) * 128],
                           v_sb[pb:pb + 64, tb * 1024 + h * 256:
                                tb * 1024 + (h + 1) * 256],
                           start=True, stop=True, wait=w,
                           inc=(sCM if dci == 1 else None))
                ncm = p.tot[sCM]
                p.emit("vector", "tensor_add",
                       tS[:, dcp * 512:(dcp + 1) * 512],
                       S_all[:, h * 1024 + dcp * 512:
                             h * 1024 + (dcp + 1) * 512],
                       ps_c[:, 0:512],
                       wait=[(sCM, ncm), (sSS, p.tot[sSS])] +
                            ([(sQE, nqe)] if (h == 0 and dcp == 0) else []),
                       inc=sTS)
            nts = p.tot[sTS]
            for dc in range(4):
                kc = h * 4 + dc
                p.emit("scalar", "activation",
                       S_all[:, kc * 256:(kc + 1) * 256],
                       tS[:, dc * 256:(dc + 1) * 256], AF.Copy,
                       scale=ebT[:, kc:kc + 1],
                       wait=([(sTS, nts), (sEBT, p.tot[sEBT]),
                              (sOM, p.tot[sOM])] if dc == 0 else []) +
                            ([(t("sl2"), nSL2)] if (i == 0 and h == 0
                              and dc == 0) else []),
                       inc=(sSS if dc == 3 else None))
            p.emit("scalar", "activation", S_bf[:, h * 1024:(h + 1) * 1024],
                   S_all[:, h * 1024:(h + 1) * 1024], AF.Copy,
                   wait=[(sSS, p.tot[sSS])], inc=sSB2)
    nOE = p.tot[sOE]
    nSS_f = p.tot[sSS]
    nGC2_f = p.tot[sGC2]

    sGX, sCC1, sCC2, sEC, sSIN, sSINB, sBM2, sBE2, sBT2, sBTe2, sEBG, sQEC, \
        sCRM, sCRA = (t(x) for x in
        ("gx2", "cc1", "cc2", "ecs", "sin", "sinb", "bm2", "be2", "bt2",
         "bte2", "ebgs", "qec2", "crm", "cra"))
    for s in (sGX, sCC1, sCC2, sEC, sSIN, sSINB, sBM2, sBE2, sBT2, sBTe2,
              sEBG, sQEC, sCRM, sCRA):
        p.sem(s)
    Gx = reg_q2.enter_context(nc.sbuf_tensor(t("gx"), [128, 16], F32,
                                              side="right"))
    p.emit("vector", "tensor_copy", Gx[:],
           Gcum.rearrange("p (kc j) -> p kc j", j=G_NC + 1)[:, 0:16, G_NC],
           wait=[(sGC2, nGC2_f)], inc=sGX)
    p.emit("vector", "tensor_scalar_mul", Gcum[:, 272:544],
           Gcum[:, 0:272], 1.0 / G_NORM, inc=sGX)
    p.dma("sync", io["cc1_in"].rearrange("(kc p) v -> p kc v", p=128),
          S_all.rearrange("p (kc v) -> p kc v", kc=16), inc=t("cx1"),
          wait=[(sSS, nSS_f)])
    p.dma("sync", io["cc2_in"], Gx[:], inc=t("cx2"),
          wait=[(sGX, p.tot[sGX])])
    p.emit("gpsimd", "collective_compute", "AllGather", ALU.bypass,
           replica_groups=[list(range(8))],
           ins=[io["cc1_in"]], outs=[io["cc1_out"]],
           wait=[(t("cx1"), p.tot[t("cx1")])], inc=sCC1)
    p.emit("gpsimd", "collective_compute", "AllGather", ALU.bypass,
           replica_groups=[list(range(8))],
           ins=[io["cc2_in"]], outs=[io["cc2_out"]],
           wait=[(t("cx2"), p.tot[t("cx2")])], inc=sCC2)
    regS.close()
    reg_kv2.close()
    regC = ExitStack()
    m_sb = regC.enter_context(nc.sbuf_tensor(t("m"), [128, 64], F32))
    pen_sb = regC.enter_context(nc.sbuf_tensor(t("pn"), [128, 8], F32))
    Gg = regC.enter_context(nc.sbuf_tensor(t("gg"), [128, 8 * 16], F32))
    Ec = regC.enter_context(nc.sbuf_tensor(t("ec"), [128, 32], F32))
    S_in = regC.enter_context(nc.sbuf_tensor(t("si"), [128, 16 * 256], F32))
    S_inb = regC.enter_context(nc.sbuf_tensor(t("sib"), [128, 16 * 256], BF16))
    Cg = regC.enter_context(nc.sbuf_tensor(t("cg"), [128, 2 * 2048], F32))
    ebg = regC.enter_context(nc.sbuf_tensor(t("ebg"), [128, 1024], F32))
    qec = regC.enter_context(nc.sbuf_tensor(t("qec"), [128, 1024], BF16))
    guard = [(t("cx1"), p.tot[t("cx1")]), (t("cx2"), p.tot[t("cx2")]),
             (sOE, nOE), (sSB2, p.tot[sSB2])]
    p.dma("sync", m_sb[:], io["m_tab"], inc=t("ctl"), wait=guard)
    p.dma("sync", pen_sb[:], io["pen"], inc=t("ctl"))
    p.dma("sync", Gg.rearrange("p (s k) -> p s k", s=8),
          io["cc2_out"].rearrange("(s p) k -> p s k", p=128), inc=t("cgl"),
          wait=[(sCC2, p.tot[sCC2])])
    ncgl = p.tot[t("cgl")]
    p.emit("gpsimd", "memset", S_in[:], 0.0, inc=t("cms"),
           wait=[(t("cx1"), p.tot[t("cx1")])])
    for cp in range(8):
        for half in range(2):
            buf = (2 * cp + half) % 2
            war = [(sSIN, 16 * cp + 8 * half - 8)] if (2 * cp + half) >= 2                 else []
            p.dma("sync",
                  Cg.rearrange("p (b kc v) -> p b kc v", b=2, kc=8)[:, buf],
                  io["cc1_out"][cp * 2048 + half * 1024:
                                cp * 2048 + (half + 1) * 1024, :].rearrange(
                      "(kc p) v -> p kc v", p=128),
                  inc=t("cgb") + str(buf),
                  wait=[(sCC1, p.tot[sCC1])] + war)
        p.emit("vector", "tensor_single_scalar", Ec[:, 0:16],
               Gg[:, 0:16], m_sb[:, cp * 8:cp * 8 + 1], ALU.mult,
               wait=[(t("cgl"), ncgl), (t("ctl"), p.tot[t("ctl")]),
                     (sEC, p.tot[sEC])], inc=sEC)
        for j in range(1, 8):
            p.emit("vector", "scalar_tensor_tensor", Ec[:, 0:16],
                   Gg[:, j * 16:(j + 1) * 16],
                   m_sb[:, cp * 8 + j:cp * 8 + j + 1],
                   Ec[:, 0:16], ALU.mult, ALU.add,
                   wait=[(sEC, p.tot[sEC])], inc=sEC)
        p.emit("vector", "tensor_single_scalar", Ec[:, 16:32], Ec[:, 0:16],
               pen_sb[:, cp:cp + 1], ALU.add,
               wait=[(sEC, p.tot[sEC])], inc=sEC)
        p.emit("scalar", "activation", Ec[:, 16:32], Ec[:, 16:32], AF.Exp,
               scale=1.0 / G_NORM,
               wait=[(sEC, p.tot[sEC]), (sSIN, p.tot[sSIN])], inc=sEC)
        for kc in range(16):
            half = kc // 8
            buf = (2 * cp + half) % 2
            kcl = kc % 8
            w = []
            if cp == 0 and kc == 0:
                w.append((t("cms"), p.tot[t("cms")]))
            if cp > 0:
                w.append((sSIN, (cp - 1) * 16 + kc + 1))
            if kcl == 0:
                w.append((sEC, p.tot[sEC]))
                w.append((t("cgb") + str(buf), p.tot[t("cgb") + str(buf)]))
            p.emit("vector", "scalar_tensor_tensor",
                   S_in[:, kc * 256:(kc + 1) * 256],
                   Cg[:, buf * 2048 + kcl * 256:buf * 2048 + (kcl + 1) * 256],
                   Ec[:, 16 + kc:17 + kc], S_in[:, kc * 256:(kc + 1) * 256],
                   ALU.mult, ALU.add, wait=w, inc=sSIN)
    nSIN = p.tot[sSIN]
    for nn in range(2):
        p.emit("scalar", "activation", S_inb[:, nn * 2048:(nn + 1) * 2048],
               S_in[:, nn * 2048:(nn + 1) * 2048], AF.Copy,
               wait=[(sSIN, nSIN)] if nn == 0 else (), inc=sSINB)
    for i in range(G_NC):
        tb = i // 2
        pb = 64 * (i % 2)
        tok0 = tb * D
        for nn in range(4):
            w = [(sBE2, p.tot[sBE2])]
            if i == 0 and nn == 0:
                w += [(sSINB, p.tot[sSINB]), (sOE, nOE)]
            p.emit("tensor", "matmul", ps_b[pb:pb + 64, :],
                   LT_sb[pb:pb + 64, :],
                   gk[pb:pb + 64, tok0 + nn * 512:tok0 + (nn + 1) * 512],
                   start=True, stop=True, wait=w, inc=sBM2)
            p.emit("vector", "tensor_copy",
                   b_sb[pb:pb + 64, nn * 512:(nn + 1) * 512],
                   ps_b[pb:pb + 64, :], wait=[(sBM2, p.tot[sBM2])], inc=sBE2)
        nbe = p.tot[sBE2]
        for kc in range(NKC):
            w = []
            if kc == 0:
                w = [(sBE2, nbe), (sBTe2, p.tot[sBTe2])]
            p.emit("tensor", "transpose",
                   ps_t2[:, kc * 64:(kc + 1) * 64],
                   b_sb[pb:pb + 64, kc * 128:(kc + 1) * 128],
                   id64f[pb:pb + 64, :],
                   wait=w, inc=(sBT2 if kc == NKC - 1 else None))
        p.emit("vector", "tensor_copy", bT_sb[:], ps_t2[:],
               wait=[(sBT2, p.tot[sBT2])], inc=sBTe2)
        nbte = p.tot[sBTe2]
        for kc in range(NKC):
            w = [(sBTe2, nbte), (sQEC, p.tot[sQEC])] if kc == 0 else ()
            p.emit("scalar", "activation", ebg[:, kc * 64:(kc + 1) * 64],
                   bT_sb[:, kc * 64:(kc + 1) * 64], AF.Exp,
                   scale=1.0 / G_NORM,
                   bias=Gcum[:, 272 + kc * (G_NC + 1) + i:
                             273 + kc * (G_NC + 1) + i],
                   wait=w, inc=(sEBG if kc == NKC - 1 else None))
        qv = qT.rearrange("p (kc tt) -> p kc tt", kc=NKC)[
            :, :, i * 64:(i + 1) * 64]
        p.emit("vector", "tensor_mul",
               qec.rearrange("p (kc c) -> p kc c", kc=NKC),
               qv, ebg.rearrange("p (kc c) -> p kc c", kc=NKC),
               wait=[(sEBG, p.tot[sEBG]), (sCRM, p.tot[sCRM])], inc=sQEC)
        nqec = p.tot[sQEC]
        for h in range(G_H):
            for dc in range(4):
                kc = h * 4 + dc
                w = []
                if h == 0 and dc == 0:
                    w = [(sQEC, nqec), (sCRA, p.tot[sCRA])]
                p.emit("tensor", "matmul",
                       ps_o[pb:pb + 64, h * 256:(h + 1) * 256],
                       qec[:, kc * 64:(kc + 1) * 64],
                       S_inb[:, kc * 256:(kc + 1) * 256],
                       start=(dc == 0), stop=(dc == 3), wait=w,
                       inc=(sCRM if (h == 3 and dc == 3) else None))
        p.emit("vector", "tensor_add",
               o_sb[pb:pb + 64, tb * 1024:(tb + 1) * 1024],
               o_sb[pb:pb + 64, tb * 1024:(tb + 1) * 1024],
               ps_o[pb:pb + 64, :],
               wait=[(sCRM, p.tot[sCRM])], inc=sCRA)
    nCRA = p.tot[sCRA]
    if "dbg_gk" in io:
        p.dma("gpsimd", io["dbg_gk"].rearrange("(tb p) d -> p tb d", p=128),
              gk.rearrange("p (tb d) -> p tb d", tb=NTB), inc=t("dbg"),
              wait=[(sCRA, nCRA)])
        p.dma("gpsimd", io["dbg_o"].rearrange("(tb p) d -> p tb d", p=128),
              o_sb.rearrange("p (tb d) -> p tb d", tb=NTB), inc=t("dbg"))
        p.dma("gpsimd", io["dbg_q"].rearrange("(kc p) t -> p kc t", p=128),
              qT.rearrange("p (kc t) -> p kc t", kc=NKC), inc=t("dbg"))
        p.dma("gpsimd", io["dbg_g"].rearrange("(tb p) d -> p tb d", p=128),
              g_sb.rearrange("p (tb d) -> p tb d", tb=NTB), inc=t("dbg"))
        p.wait("gpsimd", t("dbg"), p.tot[t("dbg")])
    regC.close()
    reg_ps.close()
    reg_q2.close()

    regW = ExitStack()
    Wo_sb = regW.enter_context(nc.sbuf_tensor(t("wo"), [128, 8 * 2048], BF16))
    wnorm = regW.enter_context(nc.sbuf_tensor(t("wn"), [128, 2], F32))
    sg_sb = regW.enter_context(nc.sbuf_tensor(t("sg"), [128, 1024], BF16))
    z_sb = regW.enter_context(nc.sbuf_tensor(t("z"), [128, 1024], BF16))
    zT_sb = regW.enter_context(nc.sbuf_tensor(t("zT2"), [128, 1024], BF16))
    osq = regW.enter_context(nc.sbuf_tensor(t("oq"), [128, 1024], F32))
    ist = regW.enter_context(nc.sbuf_tensor(t("ist"), [128, 16], F32))
    xin = regW.enter_context(nc.sbuf_tensor(t("xi"), [128, 2 * D], F32))
    xout = regW.enter_context(nc.sbuf_tensor(t("xo"), [128, 2 * D], F32))
    psW = [regW.enter_context(nc.psum_tensor(t(f"pw{i}"), [128, 512], F32))
           for i in range(4)]
    psT3 = regW.enter_context(nc.psum_tensor(t("pt3"), [128, 1024], BF16))
    (sWOL, sWOG, sFLD, sOSQ, sRS, sSGs, sZs, sZM, sTP3, sTE3, sMM, sEVW,
     sXA) = (t(x) for x in ("wol", "wog", "fld", "osq", "rs", "sgs", "zs",
                            "zm", "tp3", "te3", "mm", "evw", "xa"))
    for s in (sWOL, sWOG, sFLD, sOSQ, sRS, sSGs, sZs, sZM, sTP3, sTE3, sMM,
              sEVW, sXA):
        p.sem(s)
    dbgw = ([(t("dbg"), p.tot[t("dbg")])] if "dbg_gk" in io else [])
    p.dma("sync", wnorm[:], io["wnorm_t"], inc=sWOL,
          wait=[(sCRA, nCRA)] + dbgw)
    p.dma("gpsimd", Wo_sb.rearrange("p (c n) -> p c n", c=8),
          io["Wo"].rearrange("(c p) n -> p c n", p=128), inc=sWOG,
          wait=[(sCRA, nCRA)] + dbgw)
    for c in range(8):
        p.emit("scalar", "activation", Wo_sb[:, c * 2048:(c + 1) * 2048],
               Wo_sb[:, c * 2048:(c + 1) * 2048], AF.Copy,
               scale=wnorm[:, c % 2:c % 2 + 1],
               wait=([(sWOL, p.tot[sWOL]), (sWOG, p.tot[sWOG])]
                     if c == 0 else ()), inc=sFLD)
    for tb in range(NTB):
        bi = tb % 2
        osl = o_sb[:, tb * 1024:(tb + 1) * 1024]
        p.emit("scalar", "activation", osq[:], osl, AF.Square,
               wait=([(sCRA, nCRA)] if tb == 0 else [(sRS, p.tot[sRS])]),
               inc=sOSQ)
        for h in range(G_H):
            p.emit("vector", "reduce_sum", ist[:, h:h + 1],
                   osq[:, h * 256:(h + 1) * 256], axis=AX.X,
                   wait=([(sOSQ, p.tot[sOSQ]), (sZs, p.tot[sZs])]
                         if h == 0 else [(sRS, p.tot[sRS])]), inc=sRS)
            p.emit("vector", "tensor_scalar_mul", ist[:, 4 + h:5 + h],
                   ist[:, h:h + 1], 1.0 / G_DV,
                   wait=[(sRS, p.tot[sRS])], inc=sRS)
            p.emit("scalar", "activation", ist[:, 8 + h:9 + h],
                   ist[:, 4 + h:5 + h], AF.Sqrt, bias=RMS_EPS,
                   wait=[(sRS, p.tot[sRS])], inc=sRS)
            p.emit("vector", "reciprocal", ist[:, 12 + h:13 + h],
                   ist[:, 8 + h:9 + h], wait=[(sRS, p.tot[sRS])], inc=sRS)
        nrs = p.tot[sRS]
        p.emit("scalar", "activation", sg_sb[:],
               g_sb[:, tb * 1024:(tb + 1) * 1024], AF.Silu,
               wait=[(sZM, p.tot[sZM])], inc=sSGs)
        for h in range(G_H):
            p.emit("scalar", "activation", z_sb[:, h * 256:(h + 1) * 256],
                   osl[:, h * 256:(h + 1) * 256], AF.Identity,
                   scale=ist[:, 12 + h:13 + h],
                   wait=([(sRS, nrs), (sZM, p.tot[sZM]),
                          (sTP3, p.tot[sTP3])] if h == 0 else ()),
                   inc=(sZs if h == G_H - 1 else None))
        p.emit("vector", "tensor_mul", z_sb[:], z_sb[:], sg_sb[:],
               wait=[(sZs, p.tot[sZs]), (sSGs, p.tot[sSGs])], inc=sZM)
        nzm = p.tot[sZM]
        for c in range(8):
            p.emit("tensor", "transpose", psT3[:, c * 128:(c + 1) * 128],
                   z_sb[:, c * 128:(c + 1) * 128], ident_bf[:],
                   wait=([(sZM, nzm), (sTE3, p.tot[sTE3])] if c == 0 else ()),
                   inc=(sTP3 if c == 7 else None))
        p.emit("vector", "tensor_copy", zT_sb[:], psT3[:],
               wait=[(sTP3, p.tot[sTP3]), (sMM, p.tot[sMM])], inc=sTE3)
        nte3 = p.tot[sTE3]
        p.dma("sync", xin[:, bi * D:(bi + 1) * D],
              io["x_src"][tb * 128:(tb + 1) * 128, :], inc=t("xl") + str(bi),
              wait=[(sXA, max(0, p.tot[sXA] - 1))] if tb >= 2 else ())
        xlv = p.tot[t("xl") + str(bi)]
        for c in range(8):
            for n in range(4):
                w = []
                if c == 0 and n == 0:
                    w = [(sTE3, nte3), (sEVW, p.tot[sEVW])]
                    if tb == 0:
                        w.append((sFLD, p.tot[sFLD]))
                p.emit("tensor", "matmul", psW[n][:],
                       zT_sb[:, c * 128:(c + 1) * 128],
                       Wo_sb[:, c * 2048 + n * 512:c * 2048 + (n + 1) * 512],
                       start=(c == 0), stop=(c == 7), wait=w,
                       inc=(sMM if (c == 7 and n == 3) else None))
        nmm = p.tot[sMM]
        for n in range(4):
            w = [(sMM, nmm)] if n == 0 else []
            if tb >= 2 and n == 0:
                w.append((t("xo2") + str(bi), p.tot[t("xo2") + str(bi)]))
            p.emit("scalar", "activation",
                   xout[:, bi * D + n * 512:bi * D + (n + 1) * 512],
                   psW[n][:], AF.Copy, wait=w, inc=sEVW)
        p.emit("vector", "tensor_add", xout[:, bi * D:(bi + 1) * D],
               xout[:, bi * D:(bi + 1) * D], xin[:, bi * D:(bi + 1) * D],
               wait=[(sEVW, p.tot[sEVW]), (t("xl") + str(bi), xlv)], inc=sXA)
        p.dma("sync", io["x_dst"][tb * 128:(tb + 1) * 128, :],
              xout[:, bi * D:(bi + 1) * D], inc=t("xo2") + str(bi),
              wait=[(sXA, p.tot[sXA])])
    p.wait("sync", t("xo2") + "0", p.tot[t("xo2") + "0"])
    p.wait("sync", t("xo2") + "1", p.tot[t("xo2") + "1"])
    done = [(t("xo2") + "0", p.tot[t("xo2") + "0"]),
            (t("xo2") + "1", p.tot[t("xo2") + "1"])]
    regW.close()
    reg.close()
    return done


FW_NRM = 1.0 / math.sqrt(8192.0)


def emit_l1_ffn(p, io):
    nc = p.nc
    u = p.uid()
    t = lambda n: f"f{u}{n}"
    p.const(FW_NRM)

    reg = ExitStack()
    ident_sb = reg.enter_context(nc.sbuf_tensor(t("id"), [128, 128], F32))
    ident_bf = reg.enter_context(nc.sbuf_tensor(t("ib"), [128, 128], BF16))
    H128 = reg.enter_context(nc.sbuf_tensor(t("h1"), [128, 128], BF16))
    H64b = reg.enter_context(nc.sbuf_tensor(t("h6"), [128, 128], BF16))
    p.dma("sync", ident_sb[:], io["ident"], inc=t("tl"))
    p.dma("gpsimd", ident_bf[:], io["ident"], inc=t("tlg"))
    p.dma("gpsimd", H128[:], io["H128"], inc=t("tlg"))
    p.dma("gpsimd", H64b[:], io["H64blk"], inc=t("tlg"))

    regK = ExitStack()
    w2p = [regK.enter_context(nc.sbuf_tensor(t(f"wp{i}"), [128, 2048], BF16))
           for i in range(2)]
    w2e = [regK.enter_context(nc.sbuf_tensor(t(f"we{i}"), [128, 2048], BF16))
           for i in range(2)]
    psK = [regK.enter_context(nc.psum_tensor(t(f"pk{i}"), [128, 512], F32))
           for i in range(2)]
    sKL0, sKL1, sKM, sKE, sKX0, sKX1 = (t(x) for x in
        ("kl0", "kl1", "km", "ke", "kx0", "kx1"))
    for s in (sKL0, sKL1, sKM, sKE, sKX0, sKX1):
        p.sem(s)
    w2v = io["W2T"].rearrange("(mh ml) n -> ml mh n", ml=128)
    for jp in range(64):
        buf = jp % 2
        sL = (sKL0, sKL1)[buf]
        sX = (sKX0, sKX1)[buf]
        p.dma("gpsimd", w2p[buf][:], w2v[2 * jp:2 * jp + 2],
              inc=sL, wait=[(sKM, max(0, (jp - 1) * 4))] if jp >= 2 else [])
        wlv = p.tot[sL]
        for nn in range(4):
            w = [(sKE, max(0, p.tot[sKE] - 1))]
            if nn == 0:
                w.append((sL, wlv))
                if jp == 0:
                    w.append((t("tlg"), p.tot[t("tlg")]))
            p.emit("tensor", "matmul", psK[nn % 2][:],
                   H64b[:],
                   w2p[buf][:, nn * 512:(nn + 1) * 512],
                   start=True, stop=True, wait=w, inc=sKM)
            p.emit("scalar", "activation",
                   w2e[buf][:, nn * 512:(nn + 1) * 512], psK[nn % 2][:],
                   AF.Copy,
                   wait=[(sKM, p.tot[sKM])] +
                        ([(sX, p.tot[sX])] if nn == 0 else []),
                   inc=sKE)
        p.dma("sync", io["w2s"][:, 2 * jp:2 * jp + 2, :].rearrange(
                  "c g n -> g c n"),
              w2e[buf][:],
              inc=sX, wait=[(sKE, p.tot[sKE])])
    nKX0, nKX1 = p.tot[sKX0], p.tot[sKX1]
    nKE = p.tot[sKE]

    regY = ExitStack()
    yT = regY.enter_context(nc.sbuf_tensor(t("yT"), [128, NKC * T], BF16,
                                           side="right"))
    regA = ExitStack()
    x_sb = regA.enter_context(nc.sbuf_tensor(t("x"), [128, NTB * D], F32,
                                             side="right"))
    p.dma("sync", x_sb.rearrange("p (tb d) -> p tb d", tb=NTB),
          io["x_src"].rearrange("(tb p) d -> p tb d", p=128), inc=t("xld"))
    xw = ((t("xld"), p.tot[t("xld")]), (t("tl"), p.tot[t("tl")]))
    sLN, nLN = emit_ln_yT(p, x_sb[:], yT[:], ident_sb, regA, t("ln"),
                          x_ready_v=xw, x_ready_s=xw, side="right")
    regA.close()

    regZ = ExitStack()
    w1b = [regZ.enter_context(nc.sbuf_tensor(t(f"w1{i}"), [128, NKC * 256],
                                             BF16, side="right"))
           for i in range(2)]
    w3b = regZ.enter_context(nc.sbuf_tensor(t("w3"), [128, NKC * 256], BF16,
                                            side="right"))
    htok = regZ.enter_context(nc.sbuf_tensor(t("ht"), [128, 256], BF16,
                                             side="right"))
    t3 = regZ.enter_context(nc.sbuf_tensor(t("t3"), [128, 256], BF16,
                                           side="right"))
    hTs = regZ.enter_context(nc.sbuf_tensor(t("hTs"), [128, 2 * 1024], BF16,
                                            side="right"))
    aev = regZ.enter_context(nc.sbuf_tensor(t("ae"), [128, 2 * 1024], BF16,
                                            side="right"))
    ps1 = regZ.enter_context(nc.psum_tensor(t("p1"), [128, 256], F32))
    ps3 = regZ.enter_context(nc.psum_tensor(t("p3"), [128, 256], F32))
    psT = regZ.enter_context(nc.psum_tensor(t("pt"), [128, 256], BF16))
    psA = regZ.enter_context(nc.psum_tensor(t("pa"), [128, 1024], F32))
    (sW1, sW3, sZ1, sZ3, sH1, sH3, sHM, sTP, sTE, sAM, sAE, sAX) = (
        t(x) for x in ("w1l", "w3l", "z1", "z3", "h1", "h3", "hm", "tp",
                       "te", "am", "ae2", "ax"))
    for s in (sW1, sW3, sZ1, sZ3, sH1, sH3, sHM, sTP, sTE, sAM, sAE, sAX):
        p.sem(s)
    w1_done = {}
    for sl in range(32):
        buf = sl % 2
        sW1b = sW1 + str(buf)
        war = [(sZ1, w1_done[sl - 2])] if sl >= 2 else [(sLN, nLN)]
        p.dma("gpsimd", w1b[buf].rearrange("p (kc n) -> p kc n", kc=NKC),
              io["W1T"].rearrange("(kc p) n -> p kc n", p=128)[
                  :, :, sl * 256:(sl + 1) * 256], inc=sW1b, wait=war)
        w1lv = p.tot[sW1b]
        p.dma("gpsimd", w3b.rearrange("p (kc n) -> p kc n", kc=NKC),
              io["W3T"].rearrange("(kc p) n -> p kc n", p=128)[
                  :, :, sl * 256:(sl + 1) * 256], inc=sW3,
              wait=[(sZ3, p.tot[sZ3])])
        w3lv = p.tot[sW3]
        for tb in range(NTB):
            for kc in range(NKC):
                w = []
                if kc == 0:
                    w = [(sW1b, w1lv), (sH1, p.tot[sH1])]
                    if sl == 0 and tb == 0:
                        w.append((sLN, nLN))
                p.emit("tensor", "matmul", ps1[:],
                       yT[:, kc * T + tb * 128:kc * T + (tb + 1) * 128],
                       w1b[buf][:, kc * 256:(kc + 1) * 256],
                       start=(kc == 0), stop=(kc == NKC - 1), wait=w,
                       inc=(sZ1 if kc == NKC - 1 else None))
            nz1 = p.tot[sZ1]
            for kc in range(NKC):
                w = []
                if kc == 0:
                    w = [(sW3, w3lv), (sH3, p.tot[sH3])]
                p.emit("tensor", "matmul", ps3[:],
                       yT[:, kc * T + tb * 128:kc * T + (tb + 1) * 128],
                       w3b[:, kc * 256:(kc + 1) * 256],
                       start=(kc == 0), stop=(kc == NKC - 1), wait=w,
                       inc=(sZ3 if kc == NKC - 1 else None))
            nz3 = p.tot[sZ3]
            p.emit("scalar", "activation", htok[:], ps1[:], AF.Silu,
                   wait=[(sZ1, nz1), (sHM, p.tot[sHM]), (sTP, p.tot[sTP])],
                   inc=sH1)
            p.emit("scalar", "activation", t3[:], ps3[:], AF.Identity,
                   wait=[(sZ3, nz3)], inc=sH3)
            p.emit("vector", "tensor_mul", htok[:], htok[:], t3[:],
                   wait=[(sH1, p.tot[sH1]), (sH3, p.tot[sH3])], inc=sHM)
            nhm = p.tot[sHM]
            for jj in range(2):
                p.emit("tensor", "transpose", psT[:, jj * 128:(jj + 1) * 128],
                       htok[:, jj * 128:(jj + 1) * 128], ident_bf[:],
                       wait=([(sHM, nhm), (sTE, p.tot[sTE])] +
                             ([(t("tlg"), p.tot[t("tlg")])]
                              if sl == 0 and tb == 0 else [])
                             if jj == 0 else []),
                       inc=(sTP if jj == 1 else None))
            outv = hTs.rearrange("p (jj tt) -> p jj tt", jj=2)[
                :, :, tb * 128:(tb + 1) * 128]
            p.emit("vector", "tensor_copy", outv,
                   psT.rearrange("p (jj tt) -> p jj tt", jj=2),
                   wait=[(sTP, p.tot[sTP])] +
                        ([(sAM, p.tot[sAM])] if tb == 0 else []),
                   inc=sTE)
        w1_done[sl] = p.tot[sZ1]
        nTE = p.tot[sTE]
        for jj in range(2):
            c = sl * 2 + jj
            for nn in range(2):
                w = []
                if nn == 0:
                    w = [(sTE, nTE)] if jj == 0 else []
                    w.append((sAE, p.tot[sAE]))
                p.emit("tensor", "matmul",
                       psA[:, nn * 512:(nn + 1) * 512],
                       H128[:],
                       hTs[:, jj * 1024 + nn * 512:jj * 1024 + (nn + 1) * 512],
                       start=True, stop=True, wait=w,
                       inc=(sAM if nn == 1 else None))
            p.emit("scalar", "activation", aev[:, jj * 1024:(jj + 1) * 1024],
                   psA[:], AF.Copy, scale=FW_NRM,
                   wait=[(sAM, p.tot[sAM])] +
                        ([(sAX, p.tot[sAX])] if jj == 0 else []),
                   inc=sAE)
        p.dma("sync",
              io["a_scr"][2 * sl:2 * sl + 2].rearrange("jj p tt -> p jj tt"),
              aev.rearrange("p (jj tt) -> p jj tt", jj=2),
              inc=sAX, wait=[(sAE, p.tot[sAE])])
    nAX = p.tot[sAX]
    regZ.close()
    regY.close()
    regK.close()

    regW = ExitStack()
    A_sb = regW.enter_context(nc.sbuf_tensor(t("A"), [128, 64 * 1024], BF16))
    w2c = [regW.enter_context(nc.sbuf_tensor(t(f"wc{i}"), [128, 8 * 512],
                                             BF16)) for i in range(2)]
    xin = regW.enter_context(nc.sbuf_tensor(t("xi"), [128, 8 * 512], F32))
    xout = regW.enter_context(nc.sbuf_tensor(t("xo"), [128, 8 * 512], F32))
    psW = [regW.enter_context(nc.psum_tensor(t(f"pw{i}"), [128, 512], F32))
           for i in range(8)]
    (sAL, sWC0, sWC1, sMM, sEV, sXL, sXA, sXO) = (t(x) for x in
        ("al", "wc0", "wc1", "mm", "ev", "xl", "xa", "xo"))
    for s in (sAL, sWC0, sWC1, sMM, sEV, sXL, sXA, sXO):
        p.sem(s)
    for q in range(8):
        p.dma("sync",
              A_sb.rearrange("p (c tt) -> p c tt", c=64)[:, q * 8:(q + 1) * 8],
              io["a_scr"][q * 8:(q + 1) * 8].rearrange("c p tt -> p c tt"),
              inc=sAL,
              wait=[(sAX, nAX), (sKX0, nKX0), (sKX1, nKX1)] if q == 0 else [])
    nAL = p.tot[sAL]
    for ds in range(4):
        for cg in range(8):
            buf = cg % 2
            sW = (sWC0, sWC1)[buf]
            war = []
            gi = ds * 8 + cg
            if gi == 0:
                war = [(sKX0, nKX0), (sKX1, nKX1), (sAX, nAX)]
            if gi >= 2:
                war = [(sMM, gi - 1)]
            p.dma("gpsimd",
                  w2c[buf].rearrange("p (c n) -> p c n", c=8),
                  io["w2s"].rearrange("(cg c) ml n -> cg ml c n", c=8)[
                      cg, :, :, ds * 512:(ds + 1) * 512],
                  inc=sW, wait=war)
            wv = p.tot[sW]
            for ci in range(8):
                c = cg * 8 + ci
                for tb in range(NTB):
                    w = []
                    if ci == 0 and tb == 0:
                        w = [(sW, wv)]
                        if cg == 0:
                            w.append((sAL, nAL))
                            w.append((sXA, p.tot[sXA]))
                    p.emit("tensor", "matmul", psW[tb][:],
                           A_sb[:, c * 1024 + tb * 128:c * 1024 + (tb + 1) * 128],
                           w2c[buf][:, ci * 512:(ci + 1) * 512],
                           start=(c == 0), stop=(c == 63), wait=w,
                           inc=(sMM if (ci == 7 and tb == NTB - 1) else None))
        nMM = p.tot[sMM]
        for tb in range(NTB):
            p.dma("sync", xin[:, tb * 512:(tb + 1) * 512],
                  io["x_src"][tb * 128:(tb + 1) * 128,
                              ds * 512:(ds + 1) * 512], inc=sXL,
                  wait=[(sXA, p.tot[sXA])] if (ds >= 1 and tb == 0) else [])
        nXL = p.tot[sXL]
        for tb in range(NTB):
            p.emit("vector", "tensor_add", xout[:, tb * 512:(tb + 1) * 512],
                   xin[:, tb * 512:(tb + 1) * 512], psW[tb][:],
                   wait=[(sMM, nMM), (sXL, nXL)] +
                        ([(sXO, p.tot[sXO])] if (ds >= 1 and tb == 0)
                         else []),
                   inc=sXA)
            p.dma("sync", io["x_dst"][tb * 128:(tb + 1) * 128,
                                      ds * 512:(ds + 1) * 512],
                  xout[:, tb * 512:(tb + 1) * 512], inc=sXO,
                  wait=[(sXA, p.tot[sXA])])
    p.wait("sync", sXO, p.tot[sXO])
    done = [(sXO, p.tot[sXO])]
    regW.close()
    reg.close()
    return done



NCORES = 8
TPC = 1024


def l0_slopes(n=16):
    start = 2.0 ** (-2.0 ** (-(math.log2(n) - 3)))
    return np.array([start * start ** i for i in range(n)], np.float64)


def l0_tables(core):
    H, C = 16, 128
    slopes = l0_slopes(H)
    r = np.exp(-slopes)
    freq = (1.0 / 1024.0) ** np.linspace(0.0, 1.0, 32)
    tg = np.arange(core * TPC, (core + 1) * TPC, dtype=np.float64)
    theta = tg[:, None] * freq[None, :]
    rotC = np.tile(np.cos(theta), (1, 16)).astype(np.float32)
    rotS = np.tile(np.sin(theta), (1, 16)).astype(np.float32)
    i = np.arange(C, dtype=np.float64)
    dmaskT = np.zeros((H * C, C), np.float32)
    qdecT = np.zeros((H * C, C), np.float32)
    kdec = np.zeros((C, H), np.float32)
    for h in range(H):
        diff = i[None, :] - i[:, None]
        m = diff >= 0
        dmaskT[h * C:(h + 1) * C, :] = np.where(m, r[h] ** np.where(m, diff, 0), 0.0)
        qdecT[h * C:(h + 1) * C, :] = (r[h] ** (i + 1.0))[None, :]
        kdec[:, h] = r[h] ** (C - 1.0 - i)
    w = np.zeros((128, 128), np.float32)
    for cp in range(8):
        for h in range(H):
            if cp < core:
                w[:, cp * 16 + h] = r[h] ** (1024.0 * (core - 1 - cp))
    return dict(rotC=rotC, rotS=rotS, dmaskT=dmaskT, qdecT=qdecT,
                kdec=kdec, w_l0=w)


def vec_pt(v):
    v = np.asarray(v, np.float32)
    return np.ascontiguousarray(v.reshape(-1, 128).T)


IDENT = np.eye(128, dtype=np.float32)


def l1_tables(core):
    LT1 = np.triu(np.ones((64, 64), np.float32))
    LT = np.tile(LT1, (2, 1))
    maskTg = np.tile(LT1, (2, 4)).astype(np.float32)
    ident64 = np.tile(np.eye(64, dtype=np.float32), (2, 1))
    m_tab = np.zeros((128, 64), np.float32)
    pen = np.zeros((128, 8), np.float32)
    for cp in range(8):
        for j in range(8):
            if cp < j < core:
                m_tab[:, cp * 8 + j] = 1.0
        pen[:, cp] = 0.0 if cp < core else -1e9
    return dict(LT=LT, maskTg=maskTg, m_tab=m_tab, pen=pen, ident64=ident64)


from concourse.bass_utils import run_bass_kernel_spmd


def _hadamard(n):
    H = np.array([[1.0]], np.float32)
    while H.shape[0] < n:
        H = np.block([[H, H], [H, -H]])
    return H


H128M = _hadamard(128)
_H64M = _hadamard(64)
H64BLK = np.block([[_H64M, np.zeros((64, 64), np.float32)],
                   [np.zeros((64, 64), np.float32), _H64M]])


def _build_fused():
    nc = bass.Bass("TRN2", target_bir_lowering=False)

    def dt(name, sh, kind="ExternalInput", dtype=F32):
        return nc.dram_tensor(name, sh, dtype, kind=kind).ap()

    x0 = dt("x_src", [1024, 2048])
    x1 = dt("x1", [1024, 2048], "Internal")
    x2 = dt("x2", [1024, 2048], "Internal")
    x3 = dt("x3", [1024, 2048], "Internal")
    xo = dt("x_dst", [1024, 2048], "ExternalOutput")
    ident = dt("ident", [128, 128])

    io0 = dict(x_src=x0, x_dst=x1, ident=ident,
               Wq=dt("Wq0", [2048, 2048]), Wk=dt("Wk0", [2048, 2048]),
               Wv=dt("Wv0", [2048, 2048]), Wo=dt("Wo0", [2048, 2048]),
               rms_w_t=dt("rms_w_t", [128, 16]),
               rotC=dt("rotC", [1024, 512]), rotS=dt("rotS", [1024, 512]),
               dmaskT=dt("dmaskT", [2048, 128]), qdecT=dt("qdecT", [2048, 128]),
               kdec=dt("kdec", [128, 16]), w_l0=dt("w_l0", [128, 128]),
               cc_in=dt("cc_in_l0", [2048, 128], "Internal"),
               cc_out=dt("cc_out_l0", [16384, 128], "Internal"))
    io1 = dict(x_src=x1, x_dst=x2, ident=ident,
               gate_wT=dt("gate_wT", [2048, 8]), bias_t=dt("bias_t", [128, 8]),
               w1T=dt("w1T", [8, 2048, 1408]), w3T=dt("w3T", [8, 2048, 1408]),
               w2T=dt("w2T", [8, 1408, 2048]),
               sw1T=dt("sw1T", [2048, 1408]), sw3T=dt("sw3T", [2048, 1408]),
               sw2T=dt("sw2T", [1408, 2048]))
    io2 = dict(x_src=x2, x_dst=x3, ident=ident,
               Wq=dt("Wq1", [2048, 2048]), Wk=dt("Wk1", [2048, 2048]),
               Wv=dt("Wv1", [2048, 1024]), Wg=dt("Wg", [2048, 1024]),
               Wgk1=dt("Wgk1", [2048, 16]), Wgk2=dt("Wgk2", [16, 2048]),
               bgk2_r=dt("bgk2_r", [1, 2048]), wnorm_t=dt("wnorm_t", [128, 2]),
               Wo=dt("Wo1", [1024, 2048]), LT=dt("LT", [128, 64]),
               maskTg=dt("maskTg", [128, 256]), ident64=dt("ident64", [128, 64]),
               m_tab=dt("m_tab", [128, 64]), pen=dt("pen", [128, 8]),
               cc1_in=dt("cc1_in", [2048, 256], "Internal"),
               cc1_out=dt("cc1_out", [16384, 256], "Internal"),
               cc2_in=dt("cc2_in", [128, 16], "Internal"),
               cc2_out=dt("cc2_out", [1024, 16], "Internal"))
    io3 = dict(x_src=x3, x_dst=xo, ident=ident,
               W1T=dt("W1T", [2048, 8192]), W3T=dt("W3T", [2048, 8192]),
               W2T=dt("W2T", [8192, 2048]),
               H128=dt("H128", [128, 128]), H64blk=dt("H64blk", [128, 128]),
               a_scr=dt("a_scr", [64, 128, 1024], "Internal", BF16),
               w2s=dt("w2s", [64, 128, 2048], "Internal", BF16))

    with ExitStack() as es:
        p = P(nc, es)
        snap = dict(p.tot)
        emit_l0_attn(p, io0)
        snap = p.epoch_end(snap)
        emit_l0_ffn(p, io1)
        snap = p.epoch_end(snap)
        emit_l1_attn(p, io2)
        snap = p.epoch_end(snap)
        emit_l1_ffn(p, io3)
        p.build_block()
        carr = p.const_array()
    return nc, carr


def _in_maps(inputs, carr):
    f = lambda k: np.ascontiguousarray(np.asarray(inputs[k], np.float32))
    tr = lambda a: np.ascontiguousarray(np.swapaxes(np.asarray(a, np.float32),
                                                    -1, -2))
    x = f("L0_attn_x").reshape(8192, 2048)
    eb = f("L0_ffn_expert_bias")
    shared = dict(
        Wq0=f("L0_attn_Wq"), Wk0=f("L0_attn_Wk"), Wv0=f("L0_attn_Wv"),
        Wo0=f("L0_attn_Wo"), rms_w_t=vec_pt(f("L0_attn_rms_w")),
        gate_wT=tr(f("L0_ffn_gate_w")), bias_t=np.tile(eb[None, :], (128, 1)),
        w1T=tr(f("L0_ffn_w1")), w3T=tr(f("L0_ffn_w3")), w2T=tr(f("L0_ffn_w2")),
        sw1T=tr(f("L0_ffn_sw1")), sw3T=tr(f("L0_ffn_sw3")),
        sw2T=tr(f("L0_ffn_sw2")),
        Wq1=f("L1_attn_Wq"), Wk1=f("L1_attn_Wk"), Wv1=f("L1_attn_Wv"),
        Wg=f("L1_attn_Wg"), Wgk1=f("L1_attn_Wgk1"), Wgk2=f("L1_attn_Wgk2"),
        bgk2_r=f("L1_attn_bgk2")[None, :],
        wnorm_t=np.ascontiguousarray(f("L1_attn_Wnorm").reshape(2, 128).T),
        Wo1=f("L1_attn_Wo"),
        W1T=tr(f("L1_ffn_W1")), W3T=tr(f("L1_ffn_W3")), W2T=tr(f("L1_ffn_W2")),
        H128=H128M, H64blk=H64BLK, ident=IDENT, consts_=carr)
    ms = []
    for c in range(NCORES):
        m = dict(shared)
        m["x_src"] = np.ascontiguousarray(x[c * TPC:(c + 1) * TPC])
        m.update(l0_tables(c))
        m.update(l1_tables(c))
        ms.append(m)
    return ms


def kernel(**inputs):
    nc, carr = _build_fused()
    ms = _in_maps(inputs, carr)
    res = run_bass_kernel_spmd(nc, ms, core_ids=list(range(NCORES)))
    out = np.concatenate([np.asarray(res.results[c]["x_dst"])
                          for c in range(NCORES)], 0)
    return out[None].astype(np.float32)


# version 03 (final)
# speedup vs baseline: 1.0879x; 1.0879x over previous
import math
from contextlib import ExitStack

import numpy as np

import concourse.bass as bass
import concourse.mybir as mybir

F32 = mybir.dt.float32
BF16 = mybir.dt.bfloat16
AF = mybir.ActivationFunctionType
ALU = mybir.AluOpType
AX = mybir.AxisListType

ENGINES = ("sync", "tensor", "vector", "scalar", "gpsimd")


class P:

    CONST_CAP = 64

    def __init__(self, nc, es: ExitStack):
        self.nc = nc
        self.es = es
        self.ops = {e: [] for e in ENGINES}
        self.tot = {}
        self.sems = {}
        self._uid = 0
        self._epoch = 0
        self._epoch_sems = []
        self._marks = []
        es.callback(lambda: [self.nc.release_semaphore(self.sems[n])
                             for n in self._epoch_sems])
        self.sem("cld_")
        self.const_vals = []
        self.const_dram = nc.dram_tensor("consts_", [128, self.CONST_CAP], F32,
                                         kind="ExternalInput").ap()
        self.const_sb = es.enter_context(
            nc.sbuf_tensor("consts_sb", [128, self.CONST_CAP], F32))

    def const(self, val):
        val = float(val)
        if val not in self.const_vals:
            assert len(self.const_vals) < self.CONST_CAP
            i = len(self.const_vals)
            self.const_vals.append(val)
            self.nc.const_aps.register(F32, val, self.const_sb[:, i:i + 1])
        return val

    def const_array(self):
        arr = np.zeros((128, self.CONST_CAP), np.float32)
        for i, v in enumerate(self.const_vals):
            arr[:, i] = v
        return arr

    def uid(self):
        self._uid += 1
        return self._uid

    def sem(self, name):
        if name not in self.sems:
            self.sems[name] = self.nc.alloc_semaphore(name=name)
            self.tot[name] = 0
            self._epoch_sems.append(name)
        return self.sems[name]

    def emit(self, eng, fname, *args, wait=(), inc=None, inc_n=1, **kw):
        wait = tuple(wait)
        ret = None
        if inc is not None:
            self.sem(inc)
            self.tot[inc] += inc_n
            ret = self.tot[inc]
        sems = self.sems

        def run(h, fname=fname, args=args, kw=kw, wait=wait, inc=inc, inc_n=inc_n):
            for s, v in wait:
                h.wait_ge(sems[s], v)
            r = getattr(h, fname)(*args, **kw)
            if inc is not None:
                if inc_n == 1:
                    r.then_inc(sems[inc])
                else:
                    r.then_inc(sems[inc], inc_n)
        self.ops[eng].append(run)
        return ret

    def wait(self, eng, sem_name, val=None):
        if val is None:
            val = self.tot[sem_name]
        sems = self.sems
        self.sem(sem_name)

        def run(h, s=sem_name, v=val):
            h.wait_ge(sems[s], v)
        self.ops[eng].append(run)
        return val

    def raw(self, eng, fn):
        self.ops[eng].append(fn)

    def barrier(self, snap):
        for e in ENGINES:
            for s, v in self.tot.items():
                if v > snap.get(s, 0):
                    self.wait(e, s, v)
        return dict(self.tot)

    def epoch_end(self, snap):
        sems = self.sems
        self.sem("r_")
        self.sem("c_")
        snap2 = self.barrier(snap)
        self._epoch += 1
        k = self._epoch
        dead = [n for n in self._epoch_sems if n not in ("r_", "c_", "cld_")]
        self._epoch_sems = []
        for e in ENGINES:
            if e != "gpsimd":
                self.tot["r_"] += 1

                def rinc(h):
                    h.sem_inc(sems["r_"], 1)
                self.ops[e].append(rinc)
        self.wait("gpsimd", "r_", 4 * k)
        handles = [sems[n] for n in dead]

        def clr(h, handles=handles):
            for hd in handles:
                h.dma_reset(range(hd.num, hd.num + 1))
                h.sem_clear(hd)
            h.sem_inc(sems["c_"], 1)
        self.ops["gpsimd"].append(clr)
        self.tot["c_"] += 1
        for e in ENGINES:
            if e != "gpsimd":
                self.wait(e, "c_", k)
        for n in dead:
            self.nc.release_semaphore(sems[n])
        self._marks.append({e: len(self.ops[e]) for e in ENGINES})
        return dict(self.tot)

    def dma(self, q, out, in_, inc=None, wait=(), **kw):
        return self.emit(q, "dma_start", wait=wait, inc=inc, inc_n=16,
                         out=out, in_=in_, **kw)

    def build_block(self):
        nc = self.nc
        sems = self.sems
        cld = self.sem("cld_")

        def replay(h, eng):
            marks = [0] + [m[eng] for m in self._marks] + [len(self.ops[eng])]
            for i in range(len(marks) - 1):
                seg = self.ops[eng][marks[i]:marks[i + 1]]
                if not seg:
                    continue
                scoped = eng == "sync"
                if scoped:
                    sid, _ = nc._enter_named_scope(h, f"ep{i}", False)
                for f in seg:
                    f(h)
                if scoped:
                    nc._leave_named_scope(h, f"ep{i}", sid, False)

        def pre(h, is_sync):
            if is_sync:
                h.dma_start(out=self.const_sb[:], in_=self.const_dram)\
                    .then_inc(sems["cld_"], 16)
            h.wait_ge(sems["cld_"], 16)

        with nc.Block() as block:
            @block.sync
            def _(h):
                pre(h, True)
                replay(h, "sync")

            @block.tensor
            def _(h):
                pre(h, False)
                replay(h, "tensor")

            @block.vector
            def _(h):
                pre(h, False)
                replay(h, "vector")

            @block.scalar
            def _(h):
                pre(h, False)
                replay(h, "scalar")

            @block.gpsimd
            def _(h):
                pre(h, False)
                replay(h, "gpsimd")


def ceil_div(a, b):
    return (a + b - 1) // b


"""Block emitters for mk_seq_preln_d2. Dev copy; inlined into kernel.py later.

Layout conventions (per core, T=1024 tokens, 8 t-chunks of 128):
- token-major act tile:  sb[p, tb*W + j]   p = token-in-chunk, j = feature
- feature-major act:     sb[p, kc*T + t]   feature f = kc*128 + p, t = token
- weights in HBM pre-transposed so contraction dim is the row (partition) dim.
"""
import math
import numpy as np
from contextlib import ExitStack


T = 1024
NTB = 8
D = 2048
NKC = 16
EPS_LN = 1e-5
RMS_EPS = 1e-5


def emit_ln_yT(p, x_sb, yT_sb, ident_sb, reg, tag, x_ready_v=(), x_ready_s=(),
               side=None):
    nc = p.nc
    sSQ, sVAR, sSD, sISTD, sXM, sTP, sEV = (f"{tag}{n}" for n in
        ("sq", "var", "sd", "istd", "xm", "tp", "ev"))
    p.const(EPS_LN)
    stat = reg.enter_context(nc.sbuf_tensor(f"{tag}_stat", [128, 64], F32, side=side))
    sq = reg.enter_context(nc.sbuf_tensor(f"{tag}_sq", [128, 2048], F32, side=side))
    xm = reg.enter_context(nc.sbuf_tensor(f"{tag}_xm", [128, 2 * 2048], F32, side=side))
    diag = reg.enter_context(nc.sbuf_tensor(f"{tag}_diag", [128, 2 * 128], F32, side=side))
    ps = [reg.enter_context(nc.psum_tensor(f"{tag}_ps{i}", [128, 512], F32))
          for i in range(4)]

    for tb in range(NTB):
        xsl = x_sb[:, tb * D:(tb + 1) * D]
        cS, cSQS, cNM, cIS, cM2, cV, cSD = (tb, 8 + tb, 16 + tb, 24 + tb,
                                            32 + tb, 40 + tb, 48 + tb)
        sq_war = ((sVAR, p.tot[sVAR]),) if tb > 0 else ()
        p.emit("scalar", "activation", sq[:], xsl, AF.Square,
               wait=(x_ready_s if tb == 0 else ()) + sq_war, inc=sSQ)
        sC = f"{tag}c"
        p.emit("vector", "reduce_sum", stat[:, cS:cS + 1], xsl, axis=AX.X,
               wait=x_ready_v if tb == 0 else (), inc=sC)
        p.emit("vector", "reduce_sum", stat[:, cSQS:cSQS + 1], sq[:], axis=AX.X,
               wait=((sSQ, p.tot[sSQ]),), inc=sC)
        p.emit("vector", "tensor_scalar_mul", stat[:, cNM:cNM + 1],
               stat[:, cS:cS + 1], -1.0 / D,
               wait=((sC, p.tot[sC]),), inc=sC)
        p.emit("vector", "tensor_mul", stat[:, cM2:cM2 + 1],
               stat[:, cNM:cNM + 1], stat[:, cNM:cNM + 1],
               wait=((sC, p.tot[sC]),), inc=sC)
        p.emit("vector", "tensor_scalar_mul", stat[:, cV:cV + 1],
               stat[:, cSQS:cSQS + 1], 1.0 / D,
               wait=((sC, p.tot[sC]),), inc=sC)
        p.emit("vector", "tensor_sub", stat[:, cV:cV + 1],
               stat[:, cV:cV + 1], stat[:, cM2:cM2 + 1],
               wait=((sC, p.tot[sC]),), inc=sVAR)
        p.emit("scalar", "activation", stat[:, cSD:cSD + 1], stat[:, cV:cV + 1],
               AF.Sqrt, bias=EPS_LN, wait=((sVAR, p.tot[sVAR]),), inc=sSD)
        p.emit("vector", "reciprocal", stat[:, cIS:cIS + 1],
               stat[:, cSD:cSD + 1], wait=((sSD, p.tot[sSD]),), inc=sISTD)
        buf = tb % 2
        xmt = xm[:, buf * D:(buf + 1) * D]
        dg = diag[:, buf * 128:(buf + 1) * 128]
        war = ((sTP, 4 * (tb - 1)),) if tb >= 2 else ()
        p.emit("scalar", "activation", xmt, xsl, AF.Identity,
               bias=stat[:, cNM:cNM + 1], wait=war)
        p.emit("scalar", "activation", dg, ident_sb[:], AF.Copy,
               scale=stat[:, cIS:cIS + 1],
               wait=((sISTD, p.tot[sISTD]),), inc=sXM)
        sxm_rdy = p.tot[sXM]
        for g in range(4):
            bank = ps[g]
            gi = tb * 4 + g
            bank_war = ((sEV, gi - 3),) if gi >= 4 else ()
            for j in range(4):
                kc = g * 4 + j
                p.emit("tensor", "matmul", bank[:, j * 128:(j + 1) * 128],
                       xmt[:, kc * 128:(kc + 1) * 128], dg,
                       start=True, stop=True,
                       wait=((((sXM, sxm_rdy),) if j == 0 else ()) +
                             (bank_war if j == 0 else ())),
                       inc=(sTP if j == 3 else None))
            outv = yT_sb.rearrange("p (kc t) -> p kc t", kc=NKC)[
                :, g * 4:(g + 1) * 4, tb * 128:(tb + 1) * 128]
            inv = bank.rearrange("p (j t) -> p j t", j=4)
            p.emit("vector", "tensor_copy", outv, inv,
                   wait=((sTP, p.tot[sTP]),), inc=sEV)
    return sEV, p.tot[sEV]



L0_NH = 16


def l0_slopes():
    n = L0_NH
    start = 2.0 ** (-2.0 ** (-(math.log2(n) - 3)))
    return [start * start ** i for i in range(n)]


def emit_l0_attn(p, io):
    nc = p.nc
    u = p.uid()
    t = lambda n: f"a{u}{n}"
    slopes = l0_slopes()
    rr = [math.exp(-s) for s in slopes]
    rC = [r ** 128 for r in rr]

    reg = ExitStack()
    reg_q = ExitStack()
    reg_kv = ExitStack()
    ident_sb = reg.enter_context(nc.sbuf_tensor(t("id"), [128, 128], F32))
    ident_bf = reg.enter_context(nc.sbuf_tensor(t("ib"), [128, 128], BF16))
    p.dma("sync", ident_sb[:], io["ident"], inc=t("tl"))
    p.dma("gpsimd", ident_bf[:], io["ident"], inc=t("tlg"))

    reg_st = ExitStack()
    kv = reg_st.enter_context(nc.sbuf_tensor(t("kv"), [128, L0_NH * 128], F32,
                                             side="right"))
    regY = ExitStack()
    yT = regY.enter_context(nc.sbuf_tensor(t("yT"), [128, NKC * T], BF16,
                                           side="right"))
    regA = ExitStack()
    x_sb = regA.enter_context(nc.sbuf_tensor(t("x"), [128, NTB * D], F32,
                                             side="right"))
    p.dma("sync", x_sb.rearrange("p (tb d) -> p tb d", tb=NTB),
          io["x_src"].rearrange("(tb p) d -> p tb d", p=128), inc=t("xld"))
    xw = ((t("xld"), p.tot[t("xld")]), (t("tl"), p.tot[t("tl")]))
    sLN, nLN = emit_ln_yT(p, x_sb[:], yT[:], ident_sb, regA, t("ln"),
                          x_ready_v=xw, x_ready_s=xw, side="right")
    regA.close()
    attnT = reg.enter_context(nc.sbuf_tensor(t("at"), [128, NKC * T], BF16))
    qT = reg_q.enter_context(nc.sbuf_tensor(t("qT"), [128, NKC * T], BF16))
    kT = reg_kv.enter_context(nc.sbuf_tensor(t("kT"), [128, NKC * T], BF16))
    v_sb = reg_kv.enter_context(nc.sbuf_tensor(t("v"), [128, NTB * D], BF16))

    regP = ExitStack()
    Wb = [regP.enter_context(nc.sbuf_tensor(t(f"W{i}"), [128, NKC * 512], BF16,
                                            side="right"))
          for i in range(2)]
    qtok = regP.enter_context(nc.sbuf_tensor(t("qtok"), [128, 512], F32,
                                             side="right"))
    rotc = regP.enter_context(nc.sbuf_tensor(t("rc"), [128, 2 * 128], F32,
                                             side="right"))
    rots = regP.enter_context(nc.sbuf_tensor(t("rs"), [128, 2 * 128], F32,
                                             side="right"))
    rtmp = regP.enter_context(nc.sbuf_tensor(t("rt"), [128, 4 * 128], BF16,
                                             side="right"))
    psA = [regP.enter_context(nc.psum_tensor(t(f"pp{i}"), [128, 512], F32))
           for i in range(2)]
    psT = regP.enter_context(nc.psum_tensor(t("pt"), [128, 512], F32))
    sWL, sPM, sPE, sRO, sTP, sTE, sRL = (t(x) for x in
        ("wl", "pm", "pe", "ro", "tp", "te", "rl"))
    for s in (sWL, sPM, sPE, sRO, sTP, sTE, sRL):
        p.sem(s)

    pm_done = {}
    for wi, wname in enumerate(("Wq", "Wk", "Wv")):
        dstT = (qT, kT, None)[wi]
        for qu in range(4):
            hidx = wi * 4 + qu
            buf = hidx % 2
            war = ((sPM, pm_done[hidx - 2]),) if hidx >= 2 else ((sLN, nLN),)
            src = io[wname].rearrange("(kc p) n -> p kc n", p=128)[
                :, :, qu * 512:(qu + 1) * 512]
            sWLb = sWL + str(buf)
            p.dma("gpsimd", Wb[buf].rearrange("p (kc n) -> p kc n", kc=NKC),
                  src, inc=sWLb, wait=war)
            wl_sem, wlv = sWLb, p.tot[sWLb]
            for tb in range(NTB):
                bb = tb % 2
                if wi < 2:
                    sRLb = sRL + str(bb)
                    p.dma("sync", rotc[:, bb * 128:(bb + 1) * 128],
                          io["rotC"][tb * 128:(tb + 1) * 128,
                                     qu * 128:(qu + 1) * 128], inc=sRLb,
                          wait=((sRO, p.tot[sRO]),) if (tb >= 2 or hidx > 0)
                               else ((sLN, nLN),))
                    p.dma("sync", rots[:, bb * 128:(bb + 1) * 128],
                          io["rotS"][tb * 128:(tb + 1) * 128,
                                     qu * 128:(qu + 1) * 128], inc=sRLb)
                    rl_sem, rlv = sRLb, p.tot[sRLb]
                for kc in range(NKC):
                    w0 = []
                    if kc == 0:
                        w0 = [(wl_sem, wlv), (sPE, p.tot[sPE])]
                        if hidx == 0 and tb == 0:
                            w0.append((sLN, nLN))
                    p.emit("tensor", "matmul", psA[bb][:, :],
                           yT[:, kc * T + tb * 128:kc * T + (tb + 1) * 128],
                           Wb[buf][:, kc * 512:(kc + 1) * 512],
                           start=(kc == 0), stop=(kc == NKC - 1),
                           wait=w0,
                           inc=(sPM if kc == NKC - 1 else None))
                pmv = p.tot[sPM]
                if wi == 2:
                    p.emit("vector", "tensor_copy",
                           v_sb[:, tb * D + qu * 512:tb * D + (qu + 1) * 512],
                           psA[bb][:, :], wait=((sPM, pmv),), inc=sPE)
                else:
                    p.emit("vector", "tensor_copy", qtok[:], psA[bb][:, :],
                           wait=((sPM, pmv), (sTP, p.tot[sTP])), inc=sPE)
                    qv = qtok.rearrange("p (hh c) -> p hh c", hh=4)
                    x1 = qv[:, :, 0:32]
                    x2 = qv[:, :, 64:96]
                    tv = rtmp.rearrange("p (i hh c) -> p i hh c", i=4, hh=4)
                    C = rotc[:, bb * 128:(bb + 1) * 128].rearrange(
                        "p (hh c) -> p hh c", hh=4)
                    S = rots[:, bb * 128:(bb + 1) * 128].rearrange(
                        "p (hh c) -> p hh c", hh=4)
                    p.emit("vector", "tensor_mul", tv[:, 0], x1, C,
                           wait=((rl_sem, rlv), (sPE, p.tot[sPE])))
                    p.emit("vector", "tensor_mul", tv[:, 1], x2, S)
                    p.emit("vector", "tensor_mul", tv[:, 2], x2, C)
                    p.emit("vector", "tensor_mul", tv[:, 3], x1, S, inc=sRO)
                    p.emit("vector", "tensor_add", x1, tv[:, 0], tv[:, 1],
                           wait=((sRO, p.tot[sRO]),))
                    p.emit("vector", "tensor_sub", x2, tv[:, 2], tv[:, 3],
                           inc=sRO)
                    rov = p.tot[sRO]
                    for j in range(4):
                        p.emit("tensor", "transpose",
                               psT[:, j * 128:(j + 1) * 128],
                               qtok[:, j * 128:(j + 1) * 128], ident_sb[:],
                               wait=(((sRO, rov), (sTE, p.tot[sTE]))
                                     if j == 0 else ()),
                               inc=(sTP if j == 3 else None))
                    outv = dstT.rearrange("p (kc tt) -> p kc tt", kc=NKC)[
                        :, qu * 4:qu * 4 + 4, tb * 128:(tb + 1) * 128]
                    p.emit("vector", "tensor_copy", outv,
                           psT.rearrange("p (j tt) -> p j tt", j=4),
                           wait=((sTP, p.tot[sTP]),), inc=sTE)
            pm_done[hidx] = p.tot[sPM]
    nPROJ_TE = p.tot[sTE]
    nPROJ_PE = p.tot[sPE]
    regP.close()
    regY.close()
    reg_o = ExitStack()
    ps_o = reg_o.enter_context(nc.psum_tensor(t("p3"), [128, 512], F32))
    regS = ExitStack()
    qdecT = regS.enter_context(nc.sbuf_tensor(t("qd"), [128, L0_NH * 128], BF16))
    qd_sb = regS.enter_context(nc.sbuf_tensor(t("qs"), [128, 512], BF16))
    dmaskT = regS.enter_context(nc.sbuf_tensor(t("dm"), [128, L0_NH * 128], BF16))
    kdec = regS.enter_context(nc.sbuf_tensor(t("kd"), [128, 16], F32))
    ks_sb = regS.enter_context(nc.sbuf_tensor(t("ks"), [128, 512], BF16))
    kvbf = regS.enter_context(nc.sbuf_tensor(t("kb"), [128, 512], BF16))
    amt_sb = regS.enter_context(nc.sbuf_tensor(t("am"), [128, 512], BF16))
    ps_t = regS.enter_context(nc.psum_tensor(t("p0"), [128, 512], BF16))
    ps_c = regS.enter_context(nc.psum_tensor(t("p1"), [128, 512], F32))
    ps_a = regS.enter_context(nc.psum_tensor(t("p2"), [128, 512], F32))
    proj_done = [(sPM, p.tot[sPM]), (sTE, nPROJ_TE), (sPE, nPROJ_PE),
                 (sRO, p.tot[sRO])]
    p.dma("gpsimd", dmaskT.rearrange("p (h c) -> p h c", h=L0_NH),
          io["dmaskT"].rearrange("(h s) c -> s h c", s=128), inc=t("tlg"),
          wait=proj_done)
    p.dma("gpsimd", qdecT.rearrange("p (h c) -> p h c", h=L0_NH),
          io["qdecT"].rearrange("(h d) c -> d h c", d=128), inc=t("tlg"))
    p.dma("sync", kdec[:], io["kdec"], inc=t("tl"), wait=proj_done)
    p.emit("gpsimd", "memset", kv[:], 0.0, inc=t("ms"))
    tl_rdy = p.tot[t("tl")]
    tlg_rdy = p.tot[t("tlg")]
    ms_rdy = p.tot[t("ms")]

    sKT, sKS, sCB, sKC, sAT, sAM, sQD, sPO, sPV, sKU = (t(x) for x in
        ("skt", "sks", "scb", "skc", "sat", "sam", "sqd", "spo", "spv", "sku"))
    for s in (sKT, sKS, sCB, sKC, sAT, sAM, sQD, sPO, sPV, sKU):
        p.sem(s)
    qT_v = qT.rearrange("p (h tt) -> p h tt", h=NKC)
    at_v = attnT.rearrange("p (h tt) -> p h tt", h=NKC)
    for i in range(NTB):
        for hg in range(4):
            g = i * 4 + hg
            hs = [hg * 4 + j for j in range(4)]
            for j, h in enumerate(hs):
                w = []
                if j == 0:
                    w.append((sKS, 4 * g))
                    if g == 0:
                        w.append((sTE, nPROJ_TE))
                        w.append((sPE, nPROJ_PE))
                        w.append((t("tlg"), tlg_rdy))
                p.emit("tensor", "transpose", ps_t[:, j * 128:(j + 1) * 128],
                       kT[:, h * T + i * 128:h * T + (i + 1) * 128], ident_bf[:],
                       wait=w, inc=(sKT if j == 3 else None))
            for j, h in enumerate(hs):
                w = [(sKT, p.tot[sKT])] if j == 0 else []
                if j == 0:
                    w.append((sCB, g))
                    if g == 0:
                        w.append((t("tl"), tl_rdy))
                        w.append((t("tlg"), tlg_rdy))
                p.emit("scalar", "activation", ks_sb[:, j * 128:(j + 1) * 128],
                       ps_t[:, j * 128:(j + 1) * 128], AF.Copy,
                       scale=kdec[:, h:h + 1], wait=w,
                       inc=(sKS if j == 3 else None), inc_n=4)
            w = [(sPO, g)]
            if i > 0:
                w.append((sKU, 4 * ((i - 1) * 4 + hg) + 4))
            else:
                w.append((t("ms"), ms_rdy))
            p.emit("scalar", "activation", kvbf[:],
                   kv[:, hg * 512:(hg + 1) * 512], AF.Copy, wait=w, inc=sKC)
            for j, h in enumerate(hs):
                w = []
                if j == 0:
                    w = [(sKS, p.tot[sKS]), (sKU, 4 * (g - 1) + 4 if g > 0 else 0)]
                p.emit("tensor", "matmul", ps_c[:, j * 128:(j + 1) * 128],
                       ks_sb[:, j * 128:(j + 1) * 128],
                       v_sb[:, i * D + h * 128:i * D + (h + 1) * 128],
                       start=True, stop=True, wait=w,
                       inc=(sCB if j == 3 else None))
            for j, h in enumerate(hs):
                w = [(sAM, g)] if j == 0 else []
                p.emit("tensor", "matmul", ps_a[:, j * 128:(j + 1) * 128],
                       kT[:, h * T + i * 128:h * T + (i + 1) * 128],
                       qT[:, h * T + i * 128:h * T + (i + 1) * 128],
                       start=True, stop=True, wait=w,
                       inc=(sAT if j == 3 else None))
            p.emit("vector", "tensor_mul", amt_sb.rearrange("p (j c) -> p j c", j=4),
                   ps_a.rearrange("p (j c) -> p j c", j=4),
                   dmaskT.rearrange("p (h c) -> p h c", h=L0_NH)[:, hg * 4:hg * 4 + 4, :],
                   wait=[(sAT, p.tot[sAT]), (sPO, g)] +
                        ([(t("tl"), tl_rdy), (t("tlg"), tlg_rdy)] if g == 0 else []),
                   inc=sAM)
            p.emit("vector", "tensor_mul", qd_sb.rearrange("p (j c) -> p j c", j=4),
                   qT_v[:, hg * 4:hg * 4 + 4, i * 128:(i + 1) * 128],
                   qdecT.rearrange("p (h c) -> p h c", h=L0_NH)[:, hg * 4:hg * 4 + 4, :],
                   inc=sQD)
            for j, h in enumerate(hs):
                w = []
                if j == 0:
                    w = [(sAM, p.tot[sAM]), (sQD, p.tot[sQD]),
                         (sKC, p.tot[sKC]), (sPV, g)]
                p.emit("tensor", "matmul", ps_o[:, j * 128:(j + 1) * 128],
                       v_sb[:, i * D + h * 128:i * D + (h + 1) * 128],
                       amt_sb[:, j * 128:(j + 1) * 128],
                       start=True, stop=False, wait=w)
                p.emit("tensor", "matmul", ps_o[:, j * 128:(j + 1) * 128],
                       kvbf[:, j * 128:(j + 1) * 128],
                       qd_sb[:, j * 128:(j + 1) * 128],
                       start=False, stop=True,
                       inc=(sPO if j == 3 else None))
            p.emit("vector", "tensor_copy",
                   at_v[:, hg * 4:hg * 4 + 4, i * 128:(i + 1) * 128],
                   ps_o.rearrange("p (j c) -> p j c", j=4),
                   wait=[(sPO, p.tot[sPO])], inc=sPV)
            for j, h in enumerate(hs):
                w = [(sCB, p.tot[sCB]), (sKC, p.tot[sKC])] if j == 0 else []
                p.emit("vector", "scalar_tensor_tensor",
                       kv[:, h * 128:(h + 1) * 128],
                       kv[:, h * 128:(h + 1) * 128], rC[h],
                       ps_c[:, j * 128:(j + 1) * 128],
                       ALU.mult, ALU.add, wait=w, inc=sKU)

    regS.close()
    reg_kv.close()
    nKU = p.tot[sKU]
    p.dma("sync", io["cc_in"].rearrange("(h d) e -> d h e", d=128),
          kv.rearrange("p (h e) -> p h e", h=L0_NH), inc=t("cx"),
          wait=[(sKU, nKU)])
    p.emit("gpsimd", "collective_compute", "AllGather", ALU.bypass,
           replica_groups=[list(range(8))],
           ins=[io["cc_in"]], outs=[io["cc_out"]],
           wait=[(t("cx"), p.tot[t("cx")])], inc=t("cc"))
    p.emit("gpsimd", "memset", kv[:], 0.0, wait=[(t("cx"), p.tot[t("cx")])],
           inc=t("ms2"))

    regF = ExitStack()
    qdecT = regF.enter_context(nc.sbuf_tensor(t("qd2"), [128, L0_NH * 128], BF16))
    qd_sb = regF.enter_context(nc.sbuf_tensor(t("qs2"), [128, 512], BF16))
    p.dma("gpsimd", qdecT.rearrange("p (h c) -> p h c", h=L0_NH),
          io["qdecT"].rearrange("(h d) c -> d h c", d=128), inc=t("tl2g"))
    gbuf = regF.enter_context(nc.sbuf_tensor(t("gb"), [128, 2 * 2048], F32))
    w_sb = regF.enter_context(nc.sbuf_tensor(t("wt"), [128, 128], F32))
    ssf = regF.enter_context(nc.sbuf_tensor(t("sf"), [128, 2048], F32))
    ssb = regF.enter_context(nc.sbuf_tensor(t("sb"), [128, 2048], BF16))
    p.dma("sync", w_sb[:], io["w_l0"], inc=t("tl2"))
    sGL, sCMB, sSB, sQD2, sCR, sCA = (t(x) for x in
        ("gl", "cmb", "ssb", "qd2", "cr", "ca"))
    for s in (sGL, sCMB, sSB, sQD2, sCR, sCA):
        p.sem(s)
    for cp in range(8):
        buf = cp % 2
        slot = io["cc_out"][cp * 2048:(cp + 1) * 2048, :]
        war = [(sCMB, 16 * (cp - 1))] if cp >= 2 else []
        sGLb = sGL + str(buf)
        p.dma("sync", gbuf.rearrange("p (b h e) -> p b h e", b=2, h=L0_NH)[:, buf],
              slot.rearrange("(h d) e -> d h e", d=128), inc=sGLb,
              wait=[(t("cc"), p.tot[t("cc")])] + war)
        gl_sem, glv = sGLb, p.tot[sGLb]
        for h in range(L0_NH):
            w = [(gl_sem, glv)] if h == 0 else []
            if cp == 0 and h == 0:
                w += [(t("ms2"), p.tot[t("ms2")]), (t("tl2"), p.tot[t("tl2")])]
            if cp > 0:
                w.append((sCMB, (cp - 1) * L0_NH + h + 1))
            p.emit("vector", "scalar_tensor_tensor",
                   kv[:, h * 128:(h + 1) * 128],
                   gbuf[:, buf * 2048 + h * 128:buf * 2048 + (h + 1) * 128],
                   w_sb[:, cp * 16 + h:cp * 16 + h + 1],
                   kv[:, h * 128:(h + 1) * 128],
                   ALU.mult, ALU.add, wait=w, inc=sCMB)
    nCMB = p.tot[sCMB]
    sSF = t("ssf")
    p.sem(sSF)
    for n in range(4):
        p.emit("scalar", "activation", ssf[:, n * 512:(n + 1) * 512],
               kv[:, n * 512:(n + 1) * 512], AF.Copy,
               wait=[(sCMB, nCMB)] if n == 0 else (), inc=sSF)
    for h in range(L0_NH):
        p.const(rC[h])
    for i in range(NTB):
        if i > 0:
            for h in range(L0_NH):
                p.emit("scalar", "activation", ssf[:, h * 128:(h + 1) * 128],
                       ssf[:, h * 128:(h + 1) * 128], AF.Copy, scale=rC[h],
                       wait=([(sCR, 4 * i)] if h == 0 else []) +
                            ([(sSB, i)] if h == 0 else []),
                       inc=sSF)
        for n in range(4):
            p.emit("scalar", "activation", ssb[:, n * 512:(n + 1) * 512],
                   ssf[:, n * 512:(n + 1) * 512], AF.Copy,
                   wait=([(sCR, 4 * i)] if (i > 0 and n == 0) else []) +
                        ([(sSF, p.tot[sSF])] if n == 0 else []),
                   inc=(sSB if n == 3 else None))
        for hg in range(4):
            g = i * 4 + hg
            p.emit("vector", "tensor_mul",
                   qd_sb.rearrange("p (j c) -> p j c", j=4),
                   qT_v[:, hg * 4:hg * 4 + 4, i * 128:(i + 1) * 128],
                   qdecT.rearrange("p (h c) -> p h c", h=L0_NH)[
                       :, hg * 4:hg * 4 + 4, :],
                   wait=[(sCR, g)] +
                        ([(t("tl2g"), p.tot[t("tl2g")])] if g == 0 else []),
                   inc=sQD2)
            for j in range(4):
                h = hg * 4 + j
                w = []
                if j == 0:
                    w = [(sSB, p.tot[sSB]), (sQD2, p.tot[sQD2]), (sCA, g)]
                p.emit("tensor", "matmul", ps_o[:, j * 128:(j + 1) * 128],
                       ssb[:, h * 128:(h + 1) * 128],
                       qd_sb[:, j * 128:(j + 1) * 128],
                       start=True, stop=True, wait=w,
                       inc=(sCR if j == 3 else None))
            p.emit("vector", "tensor_tensor",
                   at_v[:, hg * 4:hg * 4 + 4, i * 128:(i + 1) * 128],
                   at_v[:, hg * 4:hg * 4 + 4, i * 128:(i + 1) * 128],
                   ps_o.rearrange("p (j c) -> p j c", j=4), ALU.add,
                   wait=[(sCR, p.tot[sCR])], inc=sCA)
    nCA = p.tot[sCA]
    regF.close()
    reg_o.close()
    reg_q.close()
    reg_st.close()

    regW = ExitStack()
    Wo_sb = regW.enter_context(nc.sbuf_tensor(t("wo"), [128, NKC * 2048], BF16))
    rmsw = regW.enter_context(nc.sbuf_tensor(t("rw"), [128, 16], F32))
    sqb = regW.enter_context(nc.sbuf_tensor(t("sqb"), [128, NKC * T], BF16))
    ones = regW.enter_context(nc.sbuf_tensor(t("on"), [128, 1], BF16))
    istd_sb = regW.enter_context(nc.sbuf_tensor(t("is"), [1, 3 * 1024], F32))
    istd_t = regW.enter_context(nc.sbuf_tensor(t("it"), [128, 8], F32))
    xin = regW.enter_context(nc.sbuf_tensor(t("xi"), [128, 2 * D], F32))
    xout = regW.enter_context(nc.sbuf_tensor(t("xo"), [128, 2 * D], F32))
    psW = [regW.enter_context(nc.psum_tensor(t(f"pw{i}"), [128, 512], F32))
           for i in range(4)]
    ps_ss = [regW.enter_context(nc.psum_tensor(t(f"px{i}"), [1, 512], F32))
             for i in range(2)]
    ps_it = regW.enter_context(nc.psum_tensor(t("pi"), [128, 8], F32))
    (sWOL, sWOG, sONE, sFLD, sSQ2, sSQM, sIC, sIT, sITE, sMM, sEVW, sXL,
     sXA, sXO) = (
        t(x) for x in ("wol", "wog", "one", "fld", "sq2", "sqm", "ic", "it",
                       "ite", "mm", "evw", "xl", "xa", "xo"))
    for s in (sWOL, sWOG, sONE, sFLD, sSQ2, sSQM, sIC, sIT, sITE, sMM, sEVW,
              sXL, sXA, sXO):
        p.sem(s)
    p.const(RMS_EPS)
    f_done = [(sCA, nCA), (sSB, p.tot[sSB])]
    p.dma("sync", rmsw[:], io["rms_w_t"], inc=sWOL, wait=f_done)
    for q in range(4):
        p.dma("gpsimd",
              Wo_sb.rearrange("p (kc n) -> p kc n", kc=NKC)[:, q * 4:(q + 1) * 4],
              io["Wo"].rearrange("(kc p) n -> p kc n", p=128)[:, q * 4:(q + 1) * 4],
              inc=sWOG, wait=f_done if q == 0 else ())
    p.emit("gpsimd", "memset", ones[:], 1.0, inc=sONE, wait=f_done)
    for kc in range(NKC):
        p.emit("scalar", "activation", Wo_sb[:, kc * 2048:(kc + 1) * 2048],
               Wo_sb[:, kc * 2048:(kc + 1) * 2048], AF.Copy,
               scale=rmsw[:, kc:kc + 1],
               wait=([(sWOL, p.tot[sWOL]), (sWOG, p.tot[sWOG])]
                     if kc == 0 else ()), inc=sFLD)
    p.emit("scalar", "activation", sqb[:], attnT[:], AF.Square,
           wait=[(sCA, nCA)], inc=sSQ2)
    for half in range(2):
        for kc in range(NKC):
            w = []
            if kc == 0:
                w = [(sSQ2, p.tot[sSQ2]), (sONE, p.tot[sONE])]
            p.emit("tensor", "matmul", ps_ss[half][:, :], ones[:],
                   sqb[:, kc * T + half * 512:kc * T + half * 512 + 512],
                   start=(kc == 0), stop=(kc == NKC - 1), wait=w,
                   inc=(sSQM if kc == NKC - 1 else None))
    for half in range(2):
        p.emit("vector", "tensor_scalar_mul",
               istd_sb[0:1, half * 512:(half + 1) * 512],
               ps_ss[half][:, :], 1.0 / D,
               wait=[(sSQM, p.tot[sSQM])], inc=sIC)
        p.emit("scalar", "activation",
               istd_sb[0:1, 1024 + half * 512:1024 + (half + 1) * 512],
               istd_sb[0:1, half * 512:(half + 1) * 512],
               AF.Sqrt, bias=RMS_EPS, wait=[(sIC, p.tot[sIC])], inc=sIC)
        p.emit("vector", "reciprocal",
               istd_sb[0:1, 2048 + half * 512:2048 + (half + 1) * 512],
               istd_sb[0:1, 1024 + half * 512:1024 + (half + 1) * 512],
               wait=[(sIC, p.tot[sIC])], inc=sIC)
    nIC = p.tot[sIC]
    for tb in range(NTB):
        p.emit("tensor", "transpose", ps_it[:, tb:tb + 1],
               istd_sb[0:1, 2048 + tb * 128:2048 + (tb + 1) * 128],
               ident_sb[0:1, 0:1],
               wait=[(sIC, nIC)] if tb == 0 else (),
               inc=(sIT if tb == NTB - 1 else None))
    p.emit("vector", "tensor_copy", istd_t[:], ps_it[:],
           wait=[(sIT, p.tot[sIT])], inc=sITE)
    for tb in range(NTB):
        bi = tb % 2
        sXLb = sXL + str(bi)
        p.dma("sync", xin[:, bi * D:(bi + 1) * D],
              io["x_src"][tb * 128:(tb + 1) * 128, :], inc=sXLb,
              wait=[(sXA, max(0, tb - 1))] if tb >= 2 else ())
        xl_sem, xlv = sXLb, p.tot[sXLb]
        for n in range(4):
            for kc in range(NKC):
                w = []
                if kc == 0:
                    w = [(sEVW, p.tot[sEVW])]
                    if n == 0 and tb == 0:
                        w += [(sFLD, p.tot[sFLD])]
                p.emit("tensor", "matmul", psW[n][:, :],
                       attnT[:, kc * T + tb * 128:kc * T + (tb + 1) * 128],
                       Wo_sb[:, kc * 2048 + n * 512:kc * 2048 + (n + 1) * 512],
                       start=(kc == 0), stop=(kc == NKC - 1), wait=w,
                       inc=(sMM if kc == NKC - 1 else None))
        nMM = p.tot[sMM]
        for n in range(4):
            w = [(sMM, nMM)] if n == 0 else []
            if tb == 0 and n == 0:
                w.append((sITE, p.tot[sITE]))
            if tb >= 2 and n == 0:
                w.append((sXO + str(tb % 2), p.tot[sXO + str(tb % 2)]))
            p.emit("scalar", "activation",
                   xout[:, bi * D + n * 512:bi * D + (n + 1) * 512],
                   psW[n][:, :], AF.Copy, scale=istd_t[:, tb:tb + 1],
                   wait=w, inc=sEVW)
        p.emit("vector", "tensor_add", xout[:, bi * D:(bi + 1) * D],
               xout[:, bi * D:(bi + 1) * D], xin[:, bi * D:(bi + 1) * D],
               wait=[(sEVW, p.tot[sEVW]), (xl_sem, xlv)], inc=sXA)
        p.dma("sync", io["x_dst"][tb * 128:(tb + 1) * 128, :],
              xout[:, bi * D:(bi + 1) * D], inc=sXO + str(bi),
              wait=[(sXA, p.tot[sXA])])
    p.wait("sync", sXO + "0", p.tot[sXO + "0"])
    p.wait("sync", sXO + "1", p.tot[sXO + "1"])
    done = [(sXO + "0", p.tot[sXO + "0"]), (sXO + "1", p.tot[sXO + "1"])]
    regW.close()
    reg.close()
    return done


MOE_H = 1408
MOE_NHC = 11
MOE_E = 8
MOE_JS = [(0, 384), (384, 384), (768, 384), (1152, 256)]


def emit_l0_ffn(p, io):
    nc = p.nc
    u = p.uid()
    t = lambda n: f"m{u}{n}"
    p.const(1.0)
    p.const(-1e30)

    reg = ExitStack()
    ident_bf = reg.enter_context(nc.sbuf_tensor(t("ib"), [128, 128], BF16))
    acc = reg.enter_context(nc.sbuf_tensor(t("acc"), [128, NTB * D], F32))
    yT = reg.enter_context(nc.sbuf_tensor(t("yT"), [128, NKC * T], BF16))
    ws = reg.enter_context(nc.sbuf_tensor(t("ws"), [128, NTB * 8], F32))
    p.dma("gpsimd", ident_bf[:], io["ident"], inc=t("tlg"))
    p.dma("sync", acc.rearrange("p (tb d) -> p tb d", tb=NTB),
          io["x_src"].rearrange("(tb p) d -> p tb d", p=128), inc=t("xld"))

    regA = ExitStack()
    ident_sb = regA.enter_context(nc.sbuf_tensor(t("id"), [128, 128], F32))
    p.dma("sync", ident_sb[:], io["ident"], inc=t("tl"))
    xw = ((t("xld"), p.tot[t("xld")]), (t("tl"), p.tot[t("tl")]))
    sLN, nLN = emit_ln_yT(p, acc[:], yT[:], ident_sb, regA, t("ln"),
                          x_ready_v=xw, x_ready_s=xw, side="right")
    regA.close()

    regG = ExitStack()
    gw = regG.enter_context(nc.sbuf_tensor(t("gw"), [128, NKC * 8], BF16,
                                           side="right"))
    bias = regG.enter_context(nc.sbuf_tensor(t("bi"), [128, 8], F32,
                                             side="right"))
    gs = regG.enter_context(nc.sbuf_tensor(t("gs"), [128, 64], F32,
                                           side="right"))
    psG = regG.enter_context(nc.psum_tensor(t("pg"), [128, 8], F32))
    p.dma("gpsimd", gw.rearrange("p (kc e) -> p kc e", kc=NKC),
          io["gate_wT"].rearrange("(kc p) e -> p kc e", p=128), inc=t("glg"),
          wait=((sLN, nLN),))
    p.dma("sync", bias[:], io["bias_t"], inc=t("gl"), wait=((sLN, nLN),))
    sGM, sGE, sGC = t("gm"), t("ge"), t("gc")
    for s in (sGM, sGE, sGC):
        p.sem(s)
    for tb in range(NTB):
        for kc in range(NKC):
            w = []
            if kc == 0:
                w = ([(t("gl"), p.tot[t("gl")]), (t("glg"), p.tot[t("glg")])]
                     if tb == 0 else [])
                w.append((sGE, p.tot[sGE]))
            p.emit("tensor", "matmul", psG[:],
                   yT[:, kc * T + tb * 128:kc * T + (tb + 1) * 128],
                   gw[:, kc * 8:(kc + 1) * 8],
                   start=(kc == 0), stop=(kc == NKC - 1), wait=w,
                   inc=(sGM if kc == NKC - 1 else None))
        p.emit("scalar", "activation", gs[:, 0:8], psG[:], AF.Sigmoid,
               wait=[(sGM, p.tot[sGM]), (sGC, p.tot[sGC])], inc=sGE)

        def vop(fn, *a, w=(), **k):
            p.emit("vector", fn, *a,
                   wait=[(sGC, p.tot[sGC])] + list(w), inc=sGC, **k)
        vop("tensor_add", gs[:, 8:16], gs[:, 0:8], bias[:],
            w=[(sGE, p.tot[sGE]), (t("gl"), p.tot[t("gl")])])

        def top1(vcol, ohcol):
            vop("reduce_max", gs[:, 56:57], gs[:, vcol:vcol + 8], axis=AX.X)
            vop("tensor_single_scalar", gs[:, 16:24], gs[:, vcol:vcol + 8],
                gs[:, 56:57], ALU.is_equal)
            vop("tensor_copy", gs[:, 24:32], gs[:, 16:24])
            a, b = 24, 32
            for k in (1, 2, 4):
                vop("tensor_copy", gs[:, b:b + k], gs[:, a:a + k])
                vop("tensor_add", gs[:, b + k:b + 8], gs[:, a + k:a + 8],
                    gs[:, a:a + 8 - k])
                a, b = b, a
            vop("tensor_single_scalar", gs[:, b:b + 8], gs[:, a:a + 8],
                1.0, ALU.is_equal)
            vop("tensor_mul", gs[:, ohcol:ohcol + 8], gs[:, 16:24],
                gs[:, b:b + 8])
        top1(8, 40)
        vop("tensor_scalar_mul", gs[:, 48:56], gs[:, 40:48], -1e30)
        vop("tensor_add", gs[:, 8:16], gs[:, 8:16], gs[:, 48:56])
        top1(8, 48)
        vop("tensor_add", gs[:, 40:48], gs[:, 40:48], gs[:, 48:56])
        vop("tensor_mul", ws[:, tb * 8:(tb + 1) * 8], gs[:, 40:48],
            gs[:, 0:8])
    nWS = p.tot[sGC]
    regG.close()

    regE = ExitStack()
    hT = regE.enter_context(nc.sbuf_tensor(t("hT"), [128, MOE_NHC * T], BF16))
    W2T = regE.enter_context(nc.sbuf_tensor(t("w2"), [128, MOE_NHC * D], BF16))
    w1b = [regE.enter_context(nc.sbuf_tensor(t(f"w1{i}"), [128, NKC * 384], BF16))
           for i in range(2)]
    w3b = regE.enter_context(nc.sbuf_tensor(t("w3"), [128, NKC * 384], BF16))
    htok = regE.enter_context(nc.sbuf_tensor(t("ht"), [128, 384], BF16))
    ps1t = regE.enter_context(nc.psum_tensor(t("p1"), [128, 1024], F32))
    ps3t = regE.enter_context(nc.psum_tensor(t("p3"), [128, 1024], F32))
    psTt = regE.enter_context(nc.psum_tensor(t("pt"), [128, 1024], BF16))
    psW = [regE.enter_context(nc.psum_tensor(t(f"pw{i}"), [128, 512], F32))
           for i in range(2)]
    ps1 = [ps1t[:, 0:384], ps1t[:, 512:896]]
    ps3 = [ps3t[:, 0:384], ps3t[:, 512:896]]
    psT = [psTt[:, 0:384], psTt[:, 512:896]]
    sW1, sW3, sW2L, sZ1, sZ3, sH1, sH3, sHM, sTP, sTE, sW2M, sAC = (
        t(x) for x in ("w1l", "w3l", "w2l", "z1", "z3", "h1", "h3", "hm",
                       "tp", "te", "w2m", "ac"))
    for s in (sW1, sW3, sW2L, sZ1, sZ3, sH1, sH3, sHM, sTP, sTE, sW2M, sAC):
        p.sem(s)

    def w13_src(key, e, j0, jw):
        src = io[key]
        if e == 0:
            ap = src.rearrange("(kc p) n -> p kc n", p=128)
        else:
            ap = src[e - 1].rearrange("(kc p) n -> p kc n", p=128)
        return ap[:, :, j0:j0 + jw]

    nslice = len(MOE_JS)
    z1_done = {}
    z3_done = {}
    tilectr = 0
    pend = None

    def flush_pend(first_tlg=False):
        nonlocal pend
        if pend is None:
            return
        m, par, jw_, hcb_, tb_ = pend
        pend = None
        ntp = jw_ // 128
        for jj in range(ntp):
            p.emit("tensor", "transpose",
                   psT[par][:, jj * 128:(jj + 1) * 128],
                   htok[:, jj * 128:(jj + 1) * 128], ident_bf[:],
                   wait=([(sHM, m + 1), (sTE, max(0, m - 1))] +
                         ([(t("tlg"), p.tot[t("tlg")])] if m == 0 else [])
                         if jj == 0 else []),
                   inc=(sTP if jj == ntp - 1 else None))
        outv = hT.rearrange("p (hc tt) -> p hc tt", hc=MOE_NHC)[
            :, hcb_:hcb_ + ntp, tb_ * 128:(tb_ + 1) * 128]
        p.emit("vector", "tensor_copy", outv,
               psT[par][:, 0:ntp * 128].rearrange(
                   "p (jj tt) -> p jj tt", jj=ntp),
               wait=[(sTP, m + 1)], inc=sTE)

    for e in range(MOE_E + 1):
        k1, k3, k2 = (("sw1T", "sw3T", "sw2T") if e == 0
                      else ("w1T", "w3T", "w2T"))
        w2src = (io[k2] if e == 0 else io[k2][e - 1])
        p.dma("gpsimd",
              W2T.rearrange("p (hc n) -> p hc n", hc=MOE_NHC),
              w2src.rearrange("(hc p) n -> p hc n", p=128), inc=sW2L,
              wait=[(sW2M, p.tot[sW2M])] +
                   ([(sLN, nLN)] if e == 0 else []))
        w2lv = p.tot[sW2L]
        for j, (j0, jw) in enumerate(MOE_JS):
            sl = e * nslice + j
            buf = sl % 2
            war = [(sZ1, z1_done[sl - 2])] if sl >= 2 else [(sLN, nLN)]
            p.dma("gpsimd", w1b[buf][:, 0:NKC * jw].rearrange(
                      "p (kc n) -> p kc n", kc=NKC),
                  w13_src(k1, e, j0, jw), inc=sW1 + str(buf), wait=war)
            w1lv = p.tot[sW1 + str(buf)]
            p.dma("gpsimd", w3b[:, 0:NKC * jw].rearrange(
                      "p (kc n) -> p kc n", kc=NKC),
                  w13_src(k3, e, j0, jw), inc=sW3,
                  wait=[(sZ3, p.tot[sZ3])])
            w3lv = p.tot[sW3]
            for tb in range(NTB):
                n = tilectr
                par = n % 2
                scale = 1.0 if e == 0 else ws[:, tb * 8 + (e - 1):
                                             tb * 8 + e]
                for kc in range(NKC):
                    w = []
                    if kc == 0:
                        w = [(sW1 + str(buf), w1lv), (sH1, max(0, n - 1))]
                        if e == 0 and j == 0 and tb == 0:
                            w.append((sLN, nLN))
                            w.append((sGE, p.tot[sGE]))
                    p.emit("tensor", "matmul", ps1[par][:, 0:jw],
                           yT[:, kc * T + tb * 128:kc * T + (tb + 1) * 128],
                           w1b[buf][:, kc * jw:(kc + 1) * jw],
                           start=(kc == 0), stop=(kc == NKC - 1), wait=w,
                           inc=(sZ1 if kc == NKC - 1 else None))
                for kc in range(NKC):
                    w = []
                    if kc == 0:
                        w = [(sW3, w3lv), (sH3, max(0, n - 1))]
                    p.emit("tensor", "matmul", ps3[par][:, 0:jw],
                           yT[:, kc * T + tb * 128:kc * T + (tb + 1) * 128],
                           w3b[:, kc * jw:(kc + 1) * jw],
                           start=(kc == 0), stop=(kc == NKC - 1), wait=w,
                           inc=(sZ3 if kc == NKC - 1 else None))
                p.emit("scalar", "activation", htok[:, 0:jw],
                       ps1[par][:, 0:jw],
                       AF.Silu, scale=scale,
                       wait=[(sZ1, n + 1), (sHM, n), (sTP, n)] +
                            ([(sGC, nWS)] if (e == 1 and j == 0 and tb == 0)
                             else []),
                       inc=sH1)
                p.emit("vector", "tensor_mul", htok[:, 0:jw],
                       htok[:, 0:jw], ps3[par][:, 0:jw],
                       wait=[(sH1, n + 1), (sZ3, n + 1)], inc=sH3)
                p.emit("vector", "tensor_single_scalar", htok[:, 0:jw],
                       htok[:, 0:jw], scale, ALU.mult,
                       wait=[(sH3, n + 1)], inc=sHM)
                flush_pend()
                pend = (n, par, jw, j0 // 128, tb)
                tilectr += 1
            z1_done[sl] = p.tot[sZ1]
            z3_done[sl] = p.tot[sZ3]
        flush_pend()
        nTE = p.tot[sTE]
        for tb in range(NTB):
            for g in range(2):
                for hc in range(MOE_NHC):
                    for n2 in range(2):
                        nn = g * 2 + n2
                        w = []
                        if hc == 0 and n2 == 0:
                            w = [(sAC, p.tot[sAC])]
                            if tb == 0 and g == 0:
                                w += [(sW2L, w2lv), (sTE, nTE)]
                        p.emit("tensor", "matmul", psW[n2][:],
                               hT[:, hc * T + tb * 128:hc * T + (tb + 1) * 128],
                               W2T[:, hc * D + nn * 512:hc * D + (nn + 1) * 512],
                               start=(hc == 0), stop=(hc == MOE_NHC - 1),
                               wait=w,
                               inc=(sW2M if (hc == MOE_NHC - 1 and n2 == 1)
                                    else None))
                nw2m = p.tot[sW2M]
                for n2 in range(2):
                    nn = g * 2 + n2
                    p.emit("vector", "tensor_add",
                           acc[:, tb * D + nn * 512:tb * D + (nn + 1) * 512],
                           acc[:, tb * D + nn * 512:tb * D + (nn + 1) * 512],
                           psW[n2][:],
                           wait=[(sW2M, nw2m)] if n2 == 0 else [], inc=sAC)
    nAC = p.tot[sAC]
    for tb in range(NTB):
        p.dma("sync", io["x_dst"][tb * 128:(tb + 1) * 128, :],
              acc[:, tb * D:(tb + 1) * D], inc=t("xo"),
              wait=[(sAC, nAC)] if tb == 0 else ())
    p.wait("sync", t("xo"), p.tot[t("xo")])
    done = [(t("xo"), p.tot[t("xo")])]
    regE.close()
    reg.close()
    return done


G_H = 4
G_DK = 512
G_DV = 256
G_C = 64
G_NC = 16
G_SCALE = G_DK ** -0.5
G_NORM = 16.0


def emit_l1_attn(p, io):
    nc = p.nc
    u = p.uid()
    t = lambda n: f"g{u}{n}"
    p.const(1.0 / G_NORM)
    p.const(-1.0 / G_NORM)
    p.const(1.0)
    p.const(RMS_EPS)

    reg = ExitStack()
    ident_sb = reg.enter_context(nc.sbuf_tensor(t("id"), [128, 128], F32))
    ident_bf = reg.enter_context(nc.sbuf_tensor(t("ib"), [128, 128], BF16))
    p.dma("sync", ident_sb[:], io["ident"], inc=t("tl"))
    p.dma("gpsimd", ident_bf[:], io["ident"], inc=t("tlg"))
    reg_q2 = ExitStack()
    reg_kv2 = ExitStack()

    regY = ExitStack()
    yT = regY.enter_context(nc.sbuf_tensor(t("yT"), [128, NKC * T], BF16,
                                           side="right"))
    regA = ExitStack()
    x_sb = regA.enter_context(nc.sbuf_tensor(t("x"), [128, NTB * D], F32,
                                             side="right"))
    p.dma("sync", x_sb.rearrange("p (tb d) -> p tb d", tb=NTB),
          io["x_src"].rearrange("(tb p) d -> p tb d", p=128), inc=t("xld"))
    xw = ((t("xld"), p.tot[t("xld")]), (t("tl"), p.tot[t("tl")]))
    sLN, nLN = emit_ln_yT(p, x_sb[:], yT[:], ident_sb, regA, t("ln"),
                          x_ready_v=xw, x_ready_s=xw, side="right")
    regA.close()
    g_sb = reg.enter_context(nc.sbuf_tensor(t("g"), [128, NTB * 1024], BF16))
    qT = reg_q2.enter_context(nc.sbuf_tensor(t("qT"), [128, NKC * T], BF16))
    gk = reg_q2.enter_context(nc.sbuf_tensor(t("gk"), [128, NTB * D], BF16))
    Gcum = reg_q2.enter_context(
        nc.sbuf_tensor(t("gc"), [128, 16 * (G_NC + 1) + 16 * 17], F32))
    LT_sb = reg_q2.enter_context(nc.sbuf_tensor(t("lt"), [128, 64], BF16))
    id64f = reg_q2.enter_context(nc.sbuf_tensor(t("i6f"), [128, 64], F32))
    id64b = reg_q2.enter_context(nc.sbuf_tensor(t("i6b"), [128, 64], BF16))
    b_sb = reg_q2.enter_context(nc.sbuf_tensor(t("b"), [128, 2048], F32))
    bT_sb = reg_q2.enter_context(nc.sbuf_tensor(t("bT"), [128, 1024], F32))
    ktok = reg_kv2.enter_context(nc.sbuf_tensor(t("kt"), [128, NTB * D], BF16))
    v_sb = reg_kv2.enter_context(nc.sbuf_tensor(t("v"), [128, NTB * 1024], BF16))

    regP = ExitStack()
    Wb = [regP.enter_context(nc.sbuf_tensor(t(f"W{i}"), [128, NKC * 512], BF16,
                                            side="right"))
          for i in range(2)]
    psA = [regP.enter_context(nc.psum_tensor(t(f"pp{i}"), [128, 512], F32))
           for i in range(2)]
    psT = regP.enter_context(nc.psum_tensor(t("pt"), [128, 512], F32))
    sWL, sPM, sPE, sTP, sTE = (t(x) for x in ("wl", "pm", "pe", "tp", "te"))
    for s in (sWL, sPM, sPE, sTP, sTE):
        p.sem(s)
    jobs = [("Wq", "qTf", 4), ("Wk", "ktok", 4), ("Wv", "v", 2), ("Wg", "g", 2)]
    hidx = 0
    pm_done = {}
    for wname, kind, nq in jobs:
        for qu in range(nq):
            buf = hidx % 2
            war = ([(sPM, pm_done[hidx - 2])] if hidx >= 2 else [(sLN, nLN)])
            p.dma("gpsimd", Wb[buf].rearrange("p (kc n) -> p kc n", kc=NKC),
                  io[wname].rearrange("(kc p) n -> p kc n", p=128)[
                      :, :, qu * 512:(qu + 1) * 512],
                  inc=sWL + str(buf), wait=war)
            wlv = p.tot[sWL + str(buf)]
            if kind == "qTf":
                for oc in range(4):
                    kcg = qu * 4 + oc
                    for kc in range(NKC):
                        w = []
                        if kc == 0:
                            w = [(sWL + str(buf), wlv), (sPE, p.tot[sPE])]
                        for nn in range(2):
                            p.emit("tensor", "matmul",
                                   psA[nn][:, :],
                                   Wb[buf][:, kc * 512 + oc * 128:
                                           kc * 512 + (oc + 1) * 128],
                                   yT[:, kc * T + nn * 512:
                                      kc * T + (nn + 1) * 512],
                                   start=(kc == 0), stop=(kc == NKC - 1),
                                   wait=w if nn == 0 else [],
                                   inc=(sPM if (kc == NKC - 1 and nn == 1)
                                        else None))
                    pmv = p.tot[sPM]
                    for nn in range(2):
                        p.emit("scalar", "activation",
                               qT[:, kcg * T + nn * 512:kcg * T + (nn + 1) * 512],
                               psA[nn][:], AF.Copy, scale=G_SCALE,
                               wait=[(sPM, pmv)] if nn == 0 else (), inc=sPE)
                pm_done[hidx] = p.tot[sPM]
            else:
                dst = {"ktok": ktok, "v": v_sb, "g": g_sb}[kind]
                wid = D if kind == "ktok" else 1024
                for tb in range(NTB):
                    for kc in range(NKC):
                        w = []
                        if kc == 0:
                            w = [(sWL + str(buf), wlv), (sPE, p.tot[sPE])]
                        p.emit("tensor", "matmul", psA[tb % 2][:, :],
                               yT[:, kc * T + tb * 128:kc * T + (tb + 1) * 128],
                               Wb[buf][:, kc * 512:(kc + 1) * 512],
                               start=(kc == 0), stop=(kc == NKC - 1),
                               wait=w,
                               inc=(sPM if kc == NKC - 1 else None))
                    pmv = p.tot[sPM]
                    p.emit("vector", "tensor_copy",
                           dst[:, tb * wid + qu * 512:tb * wid + qu * 512 + 512],
                           psA[tb % 2][:], wait=[(sPM, pmv)], inc=sPE)
                pm_done[hidx] = p.tot[sPM]
            hidx += 1
    regP.close()
    regGK = ExitStack()
    wgk1 = regGK.enter_context(nc.sbuf_tensor(t("w1"), [128, NKC * 16], BF16,
                                              side="right"))
    wgk2 = regGK.enter_context(nc.sbuf_tensor(t("w2"), [32, 2048], BF16,
                                              side="right"))
    bgk = regGK.enter_context(nc.sbuf_tensor(t("bg"), [1, 2048], BF16,
                                             side="right"))
    bgkb = regGK.enter_context(nc.sbuf_tensor(t("bb"), [128, 2048], F32,
                                              side="right"))
    z16 = regGK.enter_context(nc.sbuf_tensor(t("z6"), [128, NTB * 16], F32,
                                             side="right"))
    z16T = regGK.enter_context(nc.sbuf_tensor(t("zT"), [32, NTB * 128], BF16,
                                              side="right"))
    zb = regGK.enter_context(nc.sbuf_tensor(t("zb"), [128, 1024], F32,
                                            side="right"))
    t1 = regGK.enter_context(nc.sbuf_tensor(t("t1"), [128, 1024], F32,
                                            side="right"))
    onecol = regGK.enter_context(nc.sbuf_tensor(t("oc"), [1, 128], BF16,
                                                side="right"))
    psB = regGK.enter_context(nc.psum_tensor(t("pb"), [128, 512], F32))
    sGKL, sGKM, sGKE, sGKT, sGKV, sGKS = (t(x) for x in
        ("gkl", "gkm", "gke", "gkt", "gkv", "gks"))
    for s in (sGKL, sGKM, sGKE, sGKT, sGKV, sGKS):
        p.sem(s)
    p.dma("gpsimd", wgk1.rearrange("p (kc e) -> p kc e", kc=NKC),
          io["Wgk1"].rearrange("(kc p) e -> p kc e", p=128), inc=t("gkg"),
          wait=[(sPM, p.tot[sPM])])
    p.emit("gpsimd", "memset", wgk2[:], 0.0, inc=t("gkg2"))
    p.emit("gpsimd", "memset", z16T[:], 0.0, inc=t("gkg2"))
    p.dma("gpsimd", wgk2[0:16, :], io["Wgk2"], inc=t("gkg"),
          wait=[(t("gkg2"), 1)])
    p.dma("gpsimd", bgk[:], io["bgk2_r"], inc=t("gkg"))
    p.emit("gpsimd", "memset", onecol[:], 1.0, inc=t("gkg2"))
    ngkg = p.tot[t("gkg")]
    for nn in range(4):
        p.emit("tensor", "matmul", psB[:],
               onecol[:], bgk[0:1, nn * 512:(nn + 1) * 512],
               start=True, stop=True,
               wait=([(t("gkg2"), p.tot[t("gkg2")]), (t("gkg"), ngkg),
                      (sPE, p.tot[sPE])] if nn == 0 else
                     [(sGKE, p.tot[sGKE])]),
               inc=sGKM)
        p.emit("vector", "tensor_copy", bgkb[:, nn * 512:(nn + 1) * 512],
               psB[:], wait=[(sGKM, p.tot[sGKM])], inc=sGKE)
    psZ = regGK.enter_context(nc.psum_tensor(t("pz"), [16, 128], F32))
    for tb in range(NTB):
        for kc in range(NKC):
            w = []
            if kc == 0:
                w = [(sGKE, p.tot[sGKE]), (sGKV, p.tot[sGKV])]
            p.emit("tensor", "matmul", psB[:, 0:16],
                   yT[:, kc * T + tb * 128:kc * T + (tb + 1) * 128],
                   wgk1[:, kc * 16:(kc + 1) * 16],
                   start=(kc == 0), stop=(kc == NKC - 1), wait=w,
                   inc=(sGKM if kc == NKC - 1 else None))
        p.emit("vector", "tensor_copy", z16[:, tb * 16:(tb + 1) * 16],
               psB[:, 0:16], wait=[(sGKM, p.tot[sGKM])], inc=sGKE)
        p.emit("tensor", "transpose", psZ[:],
               z16[:, tb * 16:(tb + 1) * 16], ident_sb[:],
               wait=[(sGKE, p.tot[sGKE]), (sGKT, p.tot[sGKT])], inc=sGKM)
        p.emit("vector", "tensor_copy", z16T[0:16, tb * 128:(tb + 1) * 128],
               psZ[:], wait=[(sGKM, p.tot[sGKM])] +
                            ([(t("gkg2"), p.tot[t("gkg2")])] if tb == 0
                             else []), inc=sGKT)
        ntz = p.tot[sGKT]
        for hh in range(2):
            for nn in range(2):
                p.emit("tensor", "matmul", psB[:, 0:512],
                       z16T[0:32, tb * 128:(tb + 1) * 128],
                       wgk2[0:32, (2 * hh + nn) * 512:(2 * hh + nn + 1) * 512],
                       start=True, stop=True,
                       wait=([(sGKT, ntz)] if (hh == 0 and nn == 0) else []) +
                            [(sGKV, p.tot[sGKV])],
                       inc=sGKM)
                nm = p.tot[sGKM]
                p.emit("vector", "tensor_add", zb[:, nn * 512:(nn + 1) * 512],
                       psB[:, 0:512],
                       bgkb[:, (2 * hh + nn) * 512:(2 * hh + nn + 1) * 512],
                       wait=[(sGKM, nm)] +
                            ([(sGKS, p.tot[sGKS])] if nn == 0 else []),
                       inc=sGKV)
            nzb = p.tot[sGKV]
            p.emit("scalar", "activation", t1[:], zb[:], AF.Abs,
                   wait=[(sGKV, nzb), (sGKS, p.tot[sGKS])], inc=sGKS)
            p.emit("scalar", "activation", t1[:], t1[:], AF.Exp, scale=-1.0,
                   wait=[(sGKS, p.tot[sGKS])], inc=sGKS)
            p.emit("scalar", "activation", t1[:], t1[:], AF.Ln, bias=1.0,
                   wait=[(sGKS, p.tot[sGKS])], inc=sGKS)
            p.emit("vector", "tensor_scalar_min", zb[:], zb[:], 0.0,
                   wait=[(sGKS, p.tot[sGKS])], inc=sGKV)
            p.emit("vector", "tensor_sub",
                   gk[:, tb * D + hh * 1024:tb * D + (hh + 1) * 1024], zb[:],
                   t1[:], wait=[(sGKS, p.tot[sGKS]), (sGKV, p.tot[sGKV])],
                   inc=sGKV)

    nGK = p.tot[sGKV]
    nPE_proj = p.tot[sPE]
    if "dbg_z16" in io:
        p.dma("sync", io["dbg_bb"], bgkb[:], inc=t("dz6"), wait=[(sGKV, nGK)])
        p.dma("sync", io["dbg_z16"], z16[:], inc=t("dz6"), wait=[(sGKV, nGK)])
        p.dma("gpsimd", io["dbg_z16T"], z16T[:], inc=t("dz7"),
              wait=[(sGKV, nGK)])
        p.dma("gpsimd", io["dbg_w2"], wgk2[:], inc=t("dz7"))
        p.wait("sync", t("dz6"), p.tot[t("dz6")])
        p.wait("gpsimd", t("dz7"), p.tot[t("dz7")])
    regGK.close()
    regY.close()
    o_sb = reg.enter_context(nc.sbuf_tensor(t("o"), [128, NTB * 1024], BF16,
                                            side="right"))

    reg_ps = ExitStack()
    ps_b = reg_ps.enter_context(nc.psum_tensor(t("p0"), [128, 512], F32))
    ps_t2 = reg_ps.enter_context(nc.psum_tensor(t("p1"), [128, 1024], F32))
    ps_o = reg_ps.enter_context(nc.psum_tensor(t("p3"), [128, 1024], F32))
    regS = ExitStack()
    maskT = regS.enter_context(nc.sbuf_tensor(t("mk"), [128, 256], BF16))
    S_all = regS.enter_context(nc.sbuf_tensor(t("S"), [128, 16 * 256], F32))
    S_bf = regS.enter_context(nc.sbuf_tensor(t("Sb"), [128, 16 * 256], BF16))
    enb = regS.enter_context(nc.sbuf_tensor(t("en"), [128, 2048], F32))
    kd_sb = regS.enter_context(nc.sbuf_tensor(t("kd"), [128, 2048], BF16))
    kdT = regS.enter_context(nc.sbuf_tensor(t("kdT"), [128, 1024], BF16))
    qeT = regS.enter_context(nc.sbuf_tensor(t("qe"), [128, 1024], BF16))
    ebT = regS.enter_context(nc.sbuf_tensor(t("eb"), [128, 16], F32))
    amt = regS.enter_context(nc.sbuf_tensor(t("am"), [128, 256], BF16))
    tS = regS.enter_context(nc.sbuf_tensor(t("tS"), [128, 1024], F32))
    ps_a = regS.enter_context(nc.psum_tensor(t("p2"), [128, 256], F32))
    ps_c = regS.enter_context(nc.psum_tensor(t("p4"), [128, 512], F32))
    ps_tb = regS.enter_context(nc.psum_tensor(t("p5"), [128, 1024], BF16))
    p.dma("gpsimd", LT_sb[:], io["LT"], inc=t("sl"))
    p.dma("gpsimd", maskT[:], io["maskTg"], inc=t("sl"))
    p.dma("sync", id64f[:], io["ident64"], inc=t("sl3"))
    p.dma("gpsimd", id64b[:], io["ident64"], inc=t("sl"))
    p.emit("gpsimd", "memset", S_all[:], 0.0, inc=t("sl2"),
           wait=[(sGKV, nGK), (sGKM, p.tot[sGKM]), (sGKS, p.tot[sGKS]),
                 (sGKE, p.tot[sGKE]), (sGKT, p.tot[sGKT])] +
                ([(t("dz6"), p.tot[t("dz6")]), (t("dz7"), p.tot[t("dz7")])]
                 if "dbg_z16" in io else []))
    p.emit("gpsimd", "memset", S_bf[:], 0.0, inc=t("sl2"))
    p.emit("gpsimd", "memset", Gcum[:, 0:16 * (G_NC + 1)], 0.0, inc=t("sl2"))
    nSL = p.tot[t("sl")]
    nSL2 = p.tot[t("sl2")]
    nSL3 = p.tot[t("sl3")]
    (sBM, sBE, sBT, sBTe, sENB, sKD, sKDT, sQE, sEBT, sAT, sAM, sOM, sOE,
     sCM, sTS, sSS, sSB2, sGC2) = (t(x) for x in
        ("bm", "be", "bt", "bte", "enb", "kd", "kdt", "qe2", "ebt", "at",
         "am2", "om", "oe", "cm", "ts", "ss", "sb2", "gc2"))
    for s in (sBM, sBE, sBT, sBTe, sENB, sKD, sKDT, sQE, sEBT, sAT, sAM,
              sOM, sOE, sCM, sTS, sSS, sSB2, sGC2):
        p.sem(s)

    for i in range(G_NC):
        tb = i // 2
        pb = 64 * (i % 2)
        tok0 = tb * D
        for nn in range(4):
            w = [(sBE, p.tot[sBE])]
            if i == 0 and nn == 0:
                w += [(sGKV, nGK), (t("sl"), nSL), (t("tlg"),
                      p.tot[t("tlg")])]
            p.emit("tensor", "matmul",
                   ps_b[pb:pb + 64, :],
                   LT_sb[pb:pb + 64, :],
                   gk[pb:pb + 64, tok0 + nn * 512:tok0 + (nn + 1) * 512],
                   start=True, stop=True, wait=w, inc=sBM)
            p.emit("vector", "tensor_copy",
                   b_sb[pb:pb + 64, nn * 512:(nn + 1) * 512],
                   ps_b[pb:pb + 64, :], wait=[(sBM, p.tot[sBM])], inc=sBE)
        nbe = p.tot[sBE]
        for kc in range(NKC):
            w = []
            if kc == 0:
                w = [(sBE, nbe), (sBTe, p.tot[sBTe])]
                if i == 0:
                    w.append((t("sl3"), nSL3))
            p.emit("tensor", "transpose",
                   ps_t2[:, kc * 64:(kc + 1) * 64],
                   b_sb[pb:pb + 64, kc * 128:(kc + 1) * 128],
                   id64f[pb:pb + 64, :],
                   wait=w, inc=(sBT if kc == NKC - 1 else None))
        p.emit("vector", "tensor_copy", bT_sb[:], ps_t2[:],
               wait=[(sBT, p.tot[sBT])], inc=sBTe)
        nbte = p.tot[sBTe]
        p.emit("scalar", "activation", enb[pb:pb + 64, :], b_sb[pb:pb + 64, :],
               AF.Exp, scale=-1.0 / G_NORM,
               wait=[(sBE, nbe), (sKD, p.tot[sKD])], inc=sENB)
        p.emit("vector", "tensor_mul", kd_sb[pb:pb + 64, :],
               ktok[pb:pb + 64, tok0:tok0 + D], enb[pb:pb + 64, :],
               wait=[(sENB, p.tot[sENB]), (sKDT, p.tot[sKDT])], inc=sKD)
        nkd = p.tot[sKD]
        for kc in range(NKC):
            w = []
            if kc == 0:
                w = [(sKD, nkd), (sAT, p.tot[sAT]), (sKDT, p.tot[sKDT])]
            p.emit("tensor", "transpose",
                   ps_tb[:, kc * 64:(kc + 1) * 64],
                   kd_sb[pb:pb + 64, kc * 128:(kc + 1) * 128],
                   id64b[pb:pb + 64, :],
                   wait=w, inc=(sBT if kc == NKC - 1 else None))
        p.emit("vector", "tensor_copy", kdT[:], ps_tb[:],
               wait=[(sBT, p.tot[sBT])], inc=sKDT)
        nkdt = p.tot[sKDT]
        p.emit("scalar", "activation", tS[:], bT_sb[:], AF.Exp,
               scale=1.0 / G_NORM,
               wait=[(sBTe, nbte), (sQE, p.tot[sQE])], inc=sENB)
        qv = qT.rearrange("p (kc tt) -> p kc tt", kc=NKC)[
            :, :, i * 64:(i + 1) * 64]
        p.emit("vector", "tensor_mul",
               qeT.rearrange("p (kc c) -> p kc c", kc=NKC),
               qv, tS.rearrange("p (kc c) -> p kc c", kc=NKC),
               wait=[(sENB, p.tot[sENB]), (sOM, p.tot[sOM])], inc=sQE)
        nqe = p.tot[sQE]
        for kc in range(NKC):
            p.emit("scalar", "activation", ebT[:, kc:kc + 1],
                   bT_sb[:, kc * 64 + 63:kc * 64 + 64], AF.Exp,
                   scale=1.0 / G_NORM,
                   wait=[(sBTe, nbte), (sSS, p.tot[sSS])] if kc == 0 else (),
                   inc=(sEBT if kc == NKC - 1 else None))
        for kc in range(NKC):
            w = []
            if kc == 0:
                w = [(sBE, nbe)]
                if i == 0:
                    w.append((t("sl2"), nSL2))
            p.emit("vector", "tensor_add",
                   Gcum[:, kc * (G_NC + 1) + i + 1:kc * (G_NC + 1) + i + 2],
                   Gcum[:, kc * (G_NC + 1) + i:kc * (G_NC + 1) + i + 1],
                   b_sb[pb + 63:pb + 64, kc * 128:kc * 128 + 1]
                   if False else
                   bT_sb[:, kc * 64 + 63:kc * 64 + 64],
                   wait=w + ([(sGC2, p.tot[sGC2])] if i > 0 else []),
                   inc=sGC2)
        for h in range(G_H):
            for dc in range(4):
                kc = h * 4 + dc
                w = []
                if h == 0 and dc == 0:
                    w = [(sKDT, nkdt), (sQE, nqe), (sAM, p.tot[sAM])]
                p.emit("tensor", "matmul",
                       ps_a[pb:pb + 64, h * 64:(h + 1) * 64],
                       kdT[:, kc * 64:(kc + 1) * 64],
                       qeT[:, kc * 64:(kc + 1) * 64],
                       start=(dc == 0), stop=(dc == 3), wait=w,
                       inc=(sAT if (h == 3 and dc == 3) else None))
        p.emit("vector", "tensor_mul", amt[pb:pb + 64, :],
               ps_a[pb:pb + 64, :], maskT[pb:pb + 64, :],
               wait=[(sAT, p.tot[sAT]), (sOM, p.tot[sOM])], inc=sAM)
        nam = p.tot[sAM]
        for h in range(G_H):
            w = []
            if h == 0:
                w = [(sAM, nam), (sOE, p.tot[sOE]), (sSB2, p.tot[sSB2])]
            p.emit("tensor", "matmul",
                   ps_o[pb:pb + 64, h * 256:(h + 1) * 256],
                   amt[pb:pb + 64, h * 64:(h + 1) * 64],
                   v_sb[pb:pb + 64, tb * 1024 + h * 256:
                        tb * 1024 + (h + 1) * 256],
                   start=True, stop=False, wait=w)
            for dc in range(4):
                kc = h * 4 + dc
                p.emit("tensor", "matmul",
                       ps_o[pb:pb + 64, h * 256:(h + 1) * 256],
                       qeT[:, kc * 64:(kc + 1) * 64],
                       S_bf[:, kc * 256:(kc + 1) * 256],
                       start=False, stop=(dc == 3),
                       inc=(sOM if (h == 3 and dc == 3) else None))
        p.emit("vector", "tensor_copy",
               o_sb[pb:pb + 64, tb * 1024:(tb + 1) * 1024],
               ps_o[pb:pb + 64, :], wait=[(sOM, p.tot[sOM])], inc=sOE)
        for h in range(G_H):
            for dcp in range(2):
                for dci in range(2):
                    dc = dcp * 2 + dci
                    kc = h * 4 + dc
                    w = []
                    if dci == 0:
                        w = [(sTS, p.tot[sTS])]
                        if h == 0 and dcp == 0:
                            w.append((sKD, nkd))
                    p.emit("tensor", "matmul",
                           ps_c[:, dci * 256:(dci + 1) * 256],
                           kd_sb[pb:pb + 64, kc * 128:(kc + 1) * 128],
                           v_sb[pb:pb + 64, tb * 1024 + h * 256:
                                tb * 1024 + (h + 1) * 256],
                           start=True, stop=True, wait=w,
                           inc=(sCM if dci == 1 else None))
                ncm = p.tot[sCM]
                p.emit("vector", "tensor_add",
                       tS[:, dcp * 512:(dcp + 1) * 512],
                       S_all[:, h * 1024 + dcp * 512:
                             h * 1024 + (dcp + 1) * 512],
                       ps_c[:, 0:512],
                       wait=[(sCM, ncm), (sSS, p.tot[sSS])] +
                            ([(sQE, nqe)] if (h == 0 and dcp == 0) else []),
                       inc=sTS)
            nts = p.tot[sTS]
            for dc in range(4):
                kc = h * 4 + dc
                p.emit("scalar", "activation",
                       S_all[:, kc * 256:(kc + 1) * 256],
                       tS[:, dc * 256:(dc + 1) * 256], AF.Copy,
                       scale=ebT[:, kc:kc + 1],
                       wait=([(sTS, nts), (sEBT, p.tot[sEBT]),
                              (sOM, p.tot[sOM])] if dc == 0 else []) +
                            ([(t("sl2"), nSL2)] if (i == 0 and h == 0
                              and dc == 0) else []),
                       inc=(sSS if dc == 3 else None))
            p.emit("scalar", "activation", S_bf[:, h * 1024:(h + 1) * 1024],
                   S_all[:, h * 1024:(h + 1) * 1024], AF.Copy,
                   wait=[(sSS, p.tot[sSS])], inc=sSB2)
    nOE = p.tot[sOE]
    nSS_f = p.tot[sSS]
    nGC2_f = p.tot[sGC2]

    sGX, sCC1, sCC2, sEC, sSIN, sSINB, sBM2, sBE2, sBT2, sBTe2, sEBG, sQEC, \
        sCRM, sCRA = (t(x) for x in
        ("gx2", "cc1", "cc2", "ecs", "sin", "sinb", "bm2", "be2", "bt2",
         "bte2", "ebgs", "qec2", "crm", "cra"))
    for s in (sGX, sCC1, sCC2, sEC, sSIN, sSINB, sBM2, sBE2, sBT2, sBTe2,
              sEBG, sQEC, sCRM, sCRA):
        p.sem(s)
    Gx = reg_q2.enter_context(nc.sbuf_tensor(t("gx"), [128, 16], F32,
                                              side="right"))
    p.emit("vector", "tensor_copy", Gx[:],
           Gcum.rearrange("p (kc j) -> p kc j", j=G_NC + 1)[:, 0:16, G_NC],
           wait=[(sGC2, nGC2_f)], inc=sGX)
    p.emit("vector", "tensor_scalar_mul", Gcum[:, 272:544],
           Gcum[:, 0:272], 1.0 / G_NORM, inc=sGX)
    p.dma("sync", io["cc1_in"].rearrange("(kc p) v -> p kc v", p=128),
          S_all.rearrange("p (kc v) -> p kc v", kc=16), inc=t("cx1"),
          wait=[(sSS, nSS_f)])
    p.dma("sync", io["cc2_in"], Gx[:], inc=t("cx2"),
          wait=[(sGX, p.tot[sGX])])
    p.emit("gpsimd", "collective_compute", "AllGather", ALU.bypass,
           replica_groups=[list(range(8))],
           ins=[io["cc1_in"]], outs=[io["cc1_out"]],
           wait=[(t("cx1"), p.tot[t("cx1")])], inc=sCC1)
    p.emit("gpsimd", "collective_compute", "AllGather", ALU.bypass,
           replica_groups=[list(range(8))],
           ins=[io["cc2_in"]], outs=[io["cc2_out"]],
           wait=[(t("cx2"), p.tot[t("cx2")])], inc=sCC2)
    regS.close()
    reg_kv2.close()
    regC = ExitStack()
    m_sb = regC.enter_context(nc.sbuf_tensor(t("m"), [128, 64], F32))
    pen_sb = regC.enter_context(nc.sbuf_tensor(t("pn"), [128, 8], F32))
    Gg = regC.enter_context(nc.sbuf_tensor(t("gg"), [128, 8 * 16], F32))
    Ec = regC.enter_context(nc.sbuf_tensor(t("ec"), [128, 32], F32))
    S_in = regC.enter_context(nc.sbuf_tensor(t("si"), [128, 16 * 256], F32))
    S_inb = regC.enter_context(nc.sbuf_tensor(t("sib"), [128, 16 * 256], BF16))
    Cg = regC.enter_context(nc.sbuf_tensor(t("cg"), [128, 2 * 2048], F32))
    ebg = regC.enter_context(nc.sbuf_tensor(t("ebg"), [128, 1024], F32))
    qec = regC.enter_context(nc.sbuf_tensor(t("qec"), [128, 1024], BF16))
    guard = [(t("cx1"), p.tot[t("cx1")]), (t("cx2"), p.tot[t("cx2")]),
             (sOE, nOE), (sSB2, p.tot[sSB2])]
    p.dma("sync", m_sb[:], io["m_tab"], inc=t("ctl"), wait=guard)
    p.dma("sync", pen_sb[:], io["pen"], inc=t("ctl"))
    p.dma("sync", Gg.rearrange("p (s k) -> p s k", s=8),
          io["cc2_out"].rearrange("(s p) k -> p s k", p=128), inc=t("cgl"),
          wait=[(sCC2, p.tot[sCC2])])
    ncgl = p.tot[t("cgl")]
    p.emit("gpsimd", "memset", S_in[:], 0.0, inc=t("cms"),
           wait=[(t("cx1"), p.tot[t("cx1")])])
    for cp in range(8):
        for half in range(2):
            buf = (2 * cp + half) % 2
            war = [(sSIN, 16 * cp + 8 * half - 8)] if (2 * cp + half) >= 2                 else []
            p.dma("sync",
                  Cg.rearrange("p (b kc v) -> p b kc v", b=2, kc=8)[:, buf],
                  io["cc1_out"][cp * 2048 + half * 1024:
                                cp * 2048 + (half + 1) * 1024, :].rearrange(
                      "(kc p) v -> p kc v", p=128),
                  inc=t("cgb") + str(buf),
                  wait=[(sCC1, p.tot[sCC1])] + war)
        p.emit("vector", "tensor_single_scalar", Ec[:, 0:16],
               Gg[:, 0:16], m_sb[:, cp * 8:cp * 8 + 1], ALU.mult,
               wait=[(t("cgl"), ncgl), (t("ctl"), p.tot[t("ctl")]),
                     (sEC, p.tot[sEC])], inc=sEC)
        for j in range(1, 8):
            p.emit("vector", "scalar_tensor_tensor", Ec[:, 0:16],
                   Gg[:, j * 16:(j + 1) * 16],
                   m_sb[:, cp * 8 + j:cp * 8 + j + 1],
                   Ec[:, 0:16], ALU.mult, ALU.add,
                   wait=[(sEC, p.tot[sEC])], inc=sEC)
        p.emit("vector", "tensor_single_scalar", Ec[:, 16:32], Ec[:, 0:16],
               pen_sb[:, cp:cp + 1], ALU.add,
               wait=[(sEC, p.tot[sEC])], inc=sEC)
        p.emit("scalar", "activation", Ec[:, 16:32], Ec[:, 16:32], AF.Exp,
               scale=1.0 / G_NORM,
               wait=[(sEC, p.tot[sEC]), (sSIN, p.tot[sSIN])], inc=sEC)
        for kc in range(16):
            half = kc // 8
            buf = (2 * cp + half) % 2
            kcl = kc % 8
            w = []
            if cp == 0 and kc == 0:
                w.append((t("cms"), p.tot[t("cms")]))
            if cp > 0:
                w.append((sSIN, (cp - 1) * 16 + kc + 1))
            if kcl == 0:
                w.append((sEC, p.tot[sEC]))
                w.append((t("cgb") + str(buf), p.tot[t("cgb") + str(buf)]))
            p.emit("vector", "scalar_tensor_tensor",
                   S_in[:, kc * 256:(kc + 1) * 256],
                   Cg[:, buf * 2048 + kcl * 256:buf * 2048 + (kcl + 1) * 256],
                   Ec[:, 16 + kc:17 + kc], S_in[:, kc * 256:(kc + 1) * 256],
                   ALU.mult, ALU.add, wait=w, inc=sSIN)
    nSIN = p.tot[sSIN]
    for nn in range(2):
        p.emit("scalar", "activation", S_inb[:, nn * 2048:(nn + 1) * 2048],
               S_in[:, nn * 2048:(nn + 1) * 2048], AF.Copy,
               wait=[(sSIN, nSIN)] if nn == 0 else (), inc=sSINB)
    for i in range(G_NC):
        tb = i // 2
        pb = 64 * (i % 2)
        tok0 = tb * D
        for nn in range(4):
            w = [(sBE2, p.tot[sBE2])]
            if i == 0 and nn == 0:
                w += [(sSINB, p.tot[sSINB]), (sOE, nOE)]
            p.emit("tensor", "matmul", ps_b[pb:pb + 64, :],
                   LT_sb[pb:pb + 64, :],
                   gk[pb:pb + 64, tok0 + nn * 512:tok0 + (nn + 1) * 512],
                   start=True, stop=True, wait=w, inc=sBM2)
            p.emit("vector", "tensor_copy",
                   b_sb[pb:pb + 64, nn * 512:(nn + 1) * 512],
                   ps_b[pb:pb + 64, :], wait=[(sBM2, p.tot[sBM2])], inc=sBE2)
        nbe = p.tot[sBE2]
        for kc in range(NKC):
            w = []
            if kc == 0:
                w = [(sBE2, nbe), (sBTe2, p.tot[sBTe2])]
            p.emit("tensor", "transpose",
                   ps_t2[:, kc * 64:(kc + 1) * 64],
                   b_sb[pb:pb + 64, kc * 128:(kc + 1) * 128],
                   id64f[pb:pb + 64, :],
                   wait=w, inc=(sBT2 if kc == NKC - 1 else None))
        p.emit("vector", "tensor_copy", bT_sb[:], ps_t2[:],
               wait=[(sBT2, p.tot[sBT2])], inc=sBTe2)
        nbte = p.tot[sBTe2]
        for kc in range(NKC):
            w = [(sBTe2, nbte), (sQEC, p.tot[sQEC])] if kc == 0 else ()
            p.emit("scalar", "activation", ebg[:, kc * 64:(kc + 1) * 64],
                   bT_sb[:, kc * 64:(kc + 1) * 64], AF.Exp,
                   scale=1.0 / G_NORM,
                   bias=Gcum[:, 272 + kc * (G_NC + 1) + i:
                             273 + kc * (G_NC + 1) + i],
                   wait=w, inc=(sEBG if kc == NKC - 1 else None))
        qv = qT.rearrange("p (kc tt) -> p kc tt", kc=NKC)[
            :, :, i * 64:(i + 1) * 64]
        p.emit("vector", "tensor_mul",
               qec.rearrange("p (kc c) -> p kc c", kc=NKC),
               qv, ebg.rearrange("p (kc c) -> p kc c", kc=NKC),
               wait=[(sEBG, p.tot[sEBG]), (sCRM, p.tot[sCRM])], inc=sQEC)
        nqec = p.tot[sQEC]
        for h in range(G_H):
            for dc in range(4):
                kc = h * 4 + dc
                w = []
                if h == 0 and dc == 0:
                    w = [(sQEC, nqec), (sCRA, p.tot[sCRA])]
                p.emit("tensor", "matmul",
                       ps_o[pb:pb + 64, h * 256:(h + 1) * 256],
                       qec[:, kc * 64:(kc + 1) * 64],
                       S_inb[:, kc * 256:(kc + 1) * 256],
                       start=(dc == 0), stop=(dc == 3), wait=w,
                       inc=(sCRM if (h == 3 and dc == 3) else None))
        p.emit("vector", "tensor_add",
               o_sb[pb:pb + 64, tb * 1024:(tb + 1) * 1024],
               o_sb[pb:pb + 64, tb * 1024:(tb + 1) * 1024],
               ps_o[pb:pb + 64, :],
               wait=[(sCRM, p.tot[sCRM])], inc=sCRA)
    nCRA = p.tot[sCRA]
    if "dbg_gk" in io:
        p.dma("gpsimd", io["dbg_gk"].rearrange("(tb p) d -> p tb d", p=128),
              gk.rearrange("p (tb d) -> p tb d", tb=NTB), inc=t("dbg"),
              wait=[(sCRA, nCRA)])
        p.dma("gpsimd", io["dbg_o"].rearrange("(tb p) d -> p tb d", p=128),
              o_sb.rearrange("p (tb d) -> p tb d", tb=NTB), inc=t("dbg"))
        p.dma("gpsimd", io["dbg_q"].rearrange("(kc p) t -> p kc t", p=128),
              qT.rearrange("p (kc t) -> p kc t", kc=NKC), inc=t("dbg"))
        p.dma("gpsimd", io["dbg_g"].rearrange("(tb p) d -> p tb d", p=128),
              g_sb.rearrange("p (tb d) -> p tb d", tb=NTB), inc=t("dbg"))
        p.wait("gpsimd", t("dbg"), p.tot[t("dbg")])
    regC.close()
    reg_ps.close()
    reg_q2.close()

    regW = ExitStack()
    Wo_sb = regW.enter_context(nc.sbuf_tensor(t("wo"), [128, 8 * 2048], BF16))
    wnorm = regW.enter_context(nc.sbuf_tensor(t("wn"), [128, 2], F32))
    sg_sb = regW.enter_context(nc.sbuf_tensor(t("sg"), [128, 1024], BF16))
    z_sb = regW.enter_context(nc.sbuf_tensor(t("z"), [128, 1024], BF16))
    zT_sb = regW.enter_context(nc.sbuf_tensor(t("zT2"), [128, 1024], BF16))
    osq = regW.enter_context(nc.sbuf_tensor(t("oq"), [128, 1024], F32))
    ist = regW.enter_context(nc.sbuf_tensor(t("ist"), [128, 16], F32))
    xin = regW.enter_context(nc.sbuf_tensor(t("xi"), [128, 2 * D], F32))
    xout = regW.enter_context(nc.sbuf_tensor(t("xo"), [128, 2 * D], F32))
    psW = [regW.enter_context(nc.psum_tensor(t(f"pw{i}"), [128, 512], F32))
           for i in range(4)]
    psT3 = regW.enter_context(nc.psum_tensor(t("pt3"), [128, 1024], BF16))
    (sWOL, sWOG, sFLD, sOSQ, sRS, sSGs, sZs, sZM, sTP3, sTE3, sMM, sEVW,
     sXA) = (t(x) for x in ("wol", "wog", "fld", "osq", "rs", "sgs", "zs",
                            "zm", "tp3", "te3", "mm", "evw", "xa"))
    for s in (sWOL, sWOG, sFLD, sOSQ, sRS, sSGs, sZs, sZM, sTP3, sTE3, sMM,
              sEVW, sXA):
        p.sem(s)
    dbgw = ([(t("dbg"), p.tot[t("dbg")])] if "dbg_gk" in io else [])
    p.dma("sync", wnorm[:], io["wnorm_t"], inc=sWOL,
          wait=[(sCRA, nCRA)] + dbgw)
    p.dma("gpsimd", Wo_sb.rearrange("p (c n) -> p c n", c=8),
          io["Wo"].rearrange("(c p) n -> p c n", p=128), inc=sWOG,
          wait=[(sCRA, nCRA)] + dbgw)
    for c in range(8):
        p.emit("scalar", "activation", Wo_sb[:, c * 2048:(c + 1) * 2048],
               Wo_sb[:, c * 2048:(c + 1) * 2048], AF.Copy,
               scale=wnorm[:, c % 2:c % 2 + 1],
               wait=([(sWOL, p.tot[sWOL]), (sWOG, p.tot[sWOG])]
                     if c == 0 else ()), inc=sFLD)
    for tb in range(NTB):
        bi = tb % 2
        osl = o_sb[:, tb * 1024:(tb + 1) * 1024]
        p.emit("scalar", "activation", osq[:], osl, AF.Square,
               wait=([(sCRA, nCRA)] if tb == 0 else [(sRS, p.tot[sRS])]),
               inc=sOSQ)
        for h in range(G_H):
            p.emit("vector", "reduce_sum", ist[:, h:h + 1],
                   osq[:, h * 256:(h + 1) * 256], axis=AX.X,
                   wait=([(sOSQ, p.tot[sOSQ]), (sZs, p.tot[sZs])]
                         if h == 0 else [(sRS, p.tot[sRS])]), inc=sRS)
            p.emit("vector", "tensor_scalar_mul", ist[:, 4 + h:5 + h],
                   ist[:, h:h + 1], 1.0 / G_DV,
                   wait=[(sRS, p.tot[sRS])], inc=sRS)
            p.emit("scalar", "activation", ist[:, 8 + h:9 + h],
                   ist[:, 4 + h:5 + h], AF.Sqrt, bias=RMS_EPS,
                   wait=[(sRS, p.tot[sRS])], inc=sRS)
            p.emit("vector", "reciprocal", ist[:, 12 + h:13 + h],
                   ist[:, 8 + h:9 + h], wait=[(sRS, p.tot[sRS])], inc=sRS)
        nrs = p.tot[sRS]
        p.emit("scalar", "activation", sg_sb[:],
               g_sb[:, tb * 1024:(tb + 1) * 1024], AF.Silu,
               wait=[(sZM, p.tot[sZM])], inc=sSGs)
        for h in range(G_H):
            p.emit("scalar", "activation", z_sb[:, h * 256:(h + 1) * 256],
                   osl[:, h * 256:(h + 1) * 256], AF.Identity,
                   scale=ist[:, 12 + h:13 + h],
                   wait=([(sRS, nrs), (sZM, p.tot[sZM]),
                          (sTP3, p.tot[sTP3])] if h == 0 else ()),
                   inc=(sZs if h == G_H - 1 else None))
        p.emit("vector", "tensor_mul", z_sb[:], z_sb[:], sg_sb[:],
               wait=[(sZs, p.tot[sZs]), (sSGs, p.tot[sSGs])], inc=sZM)
        nzm = p.tot[sZM]
        for c in range(8):
            p.emit("tensor", "transpose", psT3[:, c * 128:(c + 1) * 128],
                   z_sb[:, c * 128:(c + 1) * 128], ident_bf[:],
                   wait=([(sZM, nzm), (sTE3, p.tot[sTE3])] if c == 0 else ()),
                   inc=(sTP3 if c == 7 else None))
        p.emit("vector", "tensor_copy", zT_sb[:], psT3[:],
               wait=[(sTP3, p.tot[sTP3]), (sMM, p.tot[sMM])], inc=sTE3)
        nte3 = p.tot[sTE3]
        p.dma("sync", xin[:, bi * D:(bi + 1) * D],
              io["x_src"][tb * 128:(tb + 1) * 128, :], inc=t("xl") + str(bi),
              wait=[(sXA, max(0, p.tot[sXA] - 1))] if tb >= 2 else ())
        xlv = p.tot[t("xl") + str(bi)]
        for c in range(8):
            for n in range(4):
                w = []
                if c == 0 and n == 0:
                    w = [(sTE3, nte3), (sEVW, p.tot[sEVW])]
                    if tb == 0:
                        w.append((sFLD, p.tot[sFLD]))
                p.emit("tensor", "matmul", psW[n][:],
                       zT_sb[:, c * 128:(c + 1) * 128],
                       Wo_sb[:, c * 2048 + n * 512:c * 2048 + (n + 1) * 512],
                       start=(c == 0), stop=(c == 7), wait=w,
                       inc=(sMM if (c == 7 and n == 3) else None))
        nmm = p.tot[sMM]
        for n in range(4):
            w = [(sMM, nmm)] if n == 0 else []
            if tb >= 2 and n == 0:
                w.append((t("xo2") + str(bi), p.tot[t("xo2") + str(bi)]))
            p.emit("scalar", "activation",
                   xout[:, bi * D + n * 512:bi * D + (n + 1) * 512],
                   psW[n][:], AF.Copy, wait=w, inc=sEVW)
        p.emit("vector", "tensor_add", xout[:, bi * D:(bi + 1) * D],
               xout[:, bi * D:(bi + 1) * D], xin[:, bi * D:(bi + 1) * D],
               wait=[(sEVW, p.tot[sEVW]), (t("xl") + str(bi), xlv)], inc=sXA)
        p.dma("sync", io["x_dst"][tb * 128:(tb + 1) * 128, :],
              xout[:, bi * D:(bi + 1) * D], inc=t("xo2") + str(bi),
              wait=[(sXA, p.tot[sXA])])
    p.wait("sync", t("xo2") + "0", p.tot[t("xo2") + "0"])
    p.wait("sync", t("xo2") + "1", p.tot[t("xo2") + "1"])
    done = [(t("xo2") + "0", p.tot[t("xo2") + "0"]),
            (t("xo2") + "1", p.tot[t("xo2") + "1"])]
    regW.close()
    reg.close()
    return done


FW_NRM = 1.0 / math.sqrt(8192.0)


def emit_l1_ffn(p, io):
    nc = p.nc
    u = p.uid()
    t = lambda n: f"f{u}{n}"
    p.const(FW_NRM)

    reg = ExitStack()
    ident_sb = reg.enter_context(nc.sbuf_tensor(t("id"), [128, 128], F32))
    ident_bf = reg.enter_context(nc.sbuf_tensor(t("ib"), [128, 128], BF16))
    H128 = reg.enter_context(nc.sbuf_tensor(t("h1"), [128, 128], BF16))
    H64b = reg.enter_context(nc.sbuf_tensor(t("h6"), [128, 128], BF16))
    p.dma("sync", ident_sb[:], io["ident"], inc=t("tl"))
    p.dma("gpsimd", ident_bf[:], io["ident"], inc=t("tlg"))
    p.dma("gpsimd", H128[:], io["H128"], inc=t("tlg"))
    p.dma("gpsimd", H64b[:], io["H64blk"], inc=t("tlg"))

    regK = ExitStack()
    w2p = [regK.enter_context(nc.sbuf_tensor(t(f"wp{i}"), [128, 2048], BF16))
           for i in range(2)]
    w2e = [regK.enter_context(nc.sbuf_tensor(t(f"we{i}"), [128, 2048], BF16))
           for i in range(2)]
    psK = [regK.enter_context(nc.psum_tensor(t(f"pk{i}"), [128, 512], F32))
           for i in range(2)]
    sKL0, sKL1, sKM, sKE, sKX0, sKX1 = (t(x) for x in
        ("kl0", "kl1", "km", "ke", "kx0", "kx1"))
    for s in (sKL0, sKL1, sKM, sKE, sKX0, sKX1):
        p.sem(s)
    w2v = io["W2T"].rearrange("(mh ml) n -> ml mh n", ml=128)
    for jp in range(64):
        buf = jp % 2
        sL = (sKL0, sKL1)[buf]
        sX = (sKX0, sKX1)[buf]
        p.dma("gpsimd", w2p[buf][:], w2v[2 * jp:2 * jp + 2],
              inc=sL, wait=[(sKM, max(0, (jp - 1) * 4))] if jp >= 2 else [])
        wlv = p.tot[sL]
        for nn in range(4):
            w = [(sKE, max(0, p.tot[sKE] - 1))]
            if nn == 0:
                w.append((sL, wlv))
                if jp == 0:
                    w.append((t("tlg"), p.tot[t("tlg")]))
            p.emit("tensor", "matmul", psK[nn % 2][:],
                   H64b[:],
                   w2p[buf][:, nn * 512:(nn + 1) * 512],
                   start=True, stop=True, wait=w, inc=sKM)
            p.emit("scalar", "activation",
                   w2e[buf][:, nn * 512:(nn + 1) * 512], psK[nn % 2][:],
                   AF.Copy,
                   wait=[(sKM, p.tot[sKM])] +
                        ([(sX, p.tot[sX])] if nn == 0 else []),
                   inc=sKE)
        p.dma("sync", io["w2s"][:, 2 * jp:2 * jp + 2, :].rearrange(
                  "c g n -> g c n"),
              w2e[buf][:],
              inc=sX, wait=[(sKE, p.tot[sKE])])
    nKX0, nKX1 = p.tot[sKX0], p.tot[sKX1]
    nKE = p.tot[sKE]

    regY = ExitStack()
    yT = regY.enter_context(nc.sbuf_tensor(t("yT"), [128, NKC * T], BF16,
                                           side="right"))
    regA = ExitStack()
    x_sb = regA.enter_context(nc.sbuf_tensor(t("x"), [128, NTB * D], F32,
                                             side="right"))
    p.dma("sync", x_sb.rearrange("p (tb d) -> p tb d", tb=NTB),
          io["x_src"].rearrange("(tb p) d -> p tb d", p=128), inc=t("xld"))
    xw = ((t("xld"), p.tot[t("xld")]), (t("tl"), p.tot[t("tl")]))
    sLN, nLN = emit_ln_yT(p, x_sb[:], yT[:], ident_sb, regA, t("ln"),
                          x_ready_v=xw, x_ready_s=xw, side="right")
    regA.close()

    regZ = ExitStack()
    w1b = [regZ.enter_context(nc.sbuf_tensor(t(f"w1{i}"), [128, NKC * 256],
                                             BF16, side="right"))
           for i in range(2)]
    w3b = [regZ.enter_context(nc.sbuf_tensor(t(f"w3{i}"), [128, NKC * 256],
                                             BF16, side="right"))
           for i in range(2)]
    htok2 = regZ.enter_context(nc.sbuf_tensor(t("ht"), [128, 512], BF16,
                                              side="right"))
    t32 = regZ.enter_context(nc.sbuf_tensor(t("t3"), [128, 512], BF16,
                                            side="right"))
    hTs = regZ.enter_context(nc.sbuf_tensor(t("hTs"), [128, 2 * 1024], BF16,
                                            side="right"))
    aev = regZ.enter_context(nc.sbuf_tensor(t("ae"), [128, 2 * 1024], BF16,
                                            side="right"))
    ps1t = regZ.enter_context(nc.psum_tensor(t("p1"), [128, 1024], F32))
    ps3t = regZ.enter_context(nc.psum_tensor(t("p3"), [128, 1024], F32))
    psTt = regZ.enter_context(nc.psum_tensor(t("pt"), [128, 512], BF16))
    psA = regZ.enter_context(nc.psum_tensor(t("pa"), [128, 512], F32))
    ps1 = [ps1t[:, 0:256], ps1t[:, 512:768]]
    ps3 = [ps3t[:, 0:256], ps3t[:, 512:768]]
    psT = [psTt[:, 0:256], psTt[:, 256:512]]
    htok = [htok2[:, 0:256], htok2[:, 256:512]]
    t3 = [t32[:, 0:256], t32[:, 256:512]]
    (sW1, sW3, sZ1, sZ3, sH1, sH3, sHM, sTP, sTE, sAM, sAE, sAX) = (
        t(x) for x in ("w1l", "w3l", "z1", "z3", "h1", "h3", "hm", "tp",
                       "te", "am", "ae2", "ax"))
    for s in (sW1, sW3, sZ1, sZ3, sH1, sH3, sHM, sTP, sTE, sAM, sAE, sAX):
        p.sem(s)
    z1_done = {}
    z3_done = {}
    pend = None

    def flush_pend():
        nonlocal pend
        if pend is None:
            return
        m, par, tb_, amw = pend
        pend = None
        for jj in range(2):
            p.emit("tensor", "transpose",
                   psT[par][:, jj * 128:(jj + 1) * 128],
                   htok[par][:, jj * 128:(jj + 1) * 128], ident_bf[:],
                   wait=([(sHM, m + 1), (sTE, max(0, m - 1))] +
                         ([(t("tlg"), p.tot[t("tlg")])] if m == 0 else [])
                         if jj == 0 else []),
                   inc=(sTP if jj == 1 else None))
        outv = hTs.rearrange("p (jj tt) -> p jj tt", jj=2)[
            :, :, tb_ * 128:(tb_ + 1) * 128]
        p.emit("vector", "tensor_copy", outv,
               psT[par].rearrange("p (jj tt) -> p jj tt", jj=2),
               wait=[(sTP, m + 1)] + ([(sAM, amw)] if tb_ == 0 else []),
               inc=sTE)

    for sl in range(32):
        buf = sl % 2
        sW1b = sW1 + str(buf)
        sW3b = sW3 + str(buf)
        war = [(sZ1, z1_done[sl - 2])] if sl >= 2 else [(sLN, nLN)]
        p.dma("gpsimd", w1b[buf].rearrange("p (kc n) -> p kc n", kc=NKC),
              io["W1T"].rearrange("(kc p) n -> p kc n", p=128)[
                  :, :, sl * 256:(sl + 1) * 256], inc=sW1b, wait=war)
        w1lv = p.tot[sW1b]
        war3 = [(sZ3, z3_done[sl - 2])] if sl >= 2 else [(sLN, nLN)]
        p.dma("gpsimd", w3b[buf].rearrange("p (kc n) -> p kc n", kc=NKC),
              io["W3T"].rearrange("(kc p) n -> p kc n", p=128)[
                  :, :, sl * 256:(sl + 1) * 256], inc=sW3b, wait=war3)
        w3lv = p.tot[sW3b]
        for tb in range(NTB):
            n = sl * NTB + tb
            par = n % 2
            for kc in range(NKC):
                w = []
                if kc == 0:
                    w = [(sW1b, w1lv), (sH1, max(0, n - 1))]
                    if sl == 0 and tb == 0:
                        w.append((sLN, nLN))
                p.emit("tensor", "matmul", ps1[par][:],
                       yT[:, kc * T + tb * 128:kc * T + (tb + 1) * 128],
                       w1b[buf][:, kc * 256:(kc + 1) * 256],
                       start=(kc == 0), stop=(kc == NKC - 1), wait=w,
                       inc=(sZ1 if kc == NKC - 1 else None))
            for kc in range(NKC):
                w = []
                if kc == 0:
                    w = [(sW3b, w3lv), (sH3, max(0, n - 1))]
                p.emit("tensor", "matmul", ps3[par][:],
                       yT[:, kc * T + tb * 128:kc * T + (tb + 1) * 128],
                       w3b[buf][:, kc * 256:(kc + 1) * 256],
                       start=(kc == 0), stop=(kc == NKC - 1), wait=w,
                       inc=(sZ3 if kc == NKC - 1 else None))
            p.emit("scalar", "activation", htok[par][:], ps1[par][:], AF.Silu,
                   wait=[(sZ1, n + 1), (sHM, max(0, n - 1)),
                         (sTP, max(0, n - 1))],
                   inc=sH1)
            p.emit("scalar", "activation", t3[par][:], ps3[par][:],
                   AF.Identity, wait=[(sZ3, n + 1)], inc=sH3)
            p.emit("vector", "tensor_mul", htok[par][:], htok[par][:],
                   t3[par][:],
                   wait=[(sH1, n + 1), (sH3, n + 1)], inc=sHM)
            flush_pend()
            pend = (n, par, tb, p.tot[sAM])
        z1_done[sl] = p.tot[sZ1]
        z3_done[sl] = p.tot[sZ3]
        flush_pend()
        nTE = p.tot[sTE]
        for jj in range(2):
            for nn in range(2):
                w = [(sAE, p.tot[sAE])]
                if nn == 0 and jj == 0:
                    w.append((sTE, nTE))
                p.emit("tensor", "matmul", psA[:], H128[:],
                       hTs[:, jj * 1024 + nn * 512:jj * 1024 + (nn + 1) * 512],
                       start=True, stop=True, wait=w, inc=sAM)
                p.emit("scalar", "activation",
                       aev[:, jj * 1024 + nn * 512:jj * 1024 + (nn + 1) * 512],
                       psA[:], AF.Copy, scale=FW_NRM,
                       wait=[(sAM, p.tot[sAM])] +
                            ([(sAX, p.tot[sAX])] if (jj == 0 and nn == 0)
                             else []),
                       inc=sAE)
        p.dma("sync",
              io["a_scr"][2 * sl:2 * sl + 2].rearrange("jj p tt -> p jj tt"),
              aev.rearrange("p (jj tt) -> p jj tt", jj=2),
              inc=sAX, wait=[(sAE, p.tot[sAE])])
    nAX = p.tot[sAX]
    regZ.close()
    regY.close()
    regK.close()

    regW = ExitStack()
    A_sb = regW.enter_context(nc.sbuf_tensor(t("A"), [128, 64 * 1024], BF16))
    w2c = [regW.enter_context(nc.sbuf_tensor(t(f"wc{i}"), [128, 8 * 512],
                                             BF16)) for i in range(2)]
    xin = regW.enter_context(nc.sbuf_tensor(t("xi"), [128, 8 * 512], F32))
    xout = regW.enter_context(nc.sbuf_tensor(t("xo"), [128, 8 * 512], F32))
    psW = [regW.enter_context(nc.psum_tensor(t(f"pw{i}"), [128, 512], F32))
           for i in range(8)]
    (sAL, sWC0, sWC1, sMM, sEV, sXL, sXA, sXO) = (t(x) for x in
        ("al", "wc0", "wc1", "mm", "ev", "xl", "xa", "xo"))
    for s in (sAL, sWC0, sWC1, sMM, sEV, sXL, sXA, sXO):
        p.sem(s)
    for q in range(8):
        p.dma("sync",
              A_sb.rearrange("p (c tt) -> p c tt", c=64)[:, q * 8:(q + 1) * 8],
              io["a_scr"][q * 8:(q + 1) * 8].rearrange("c p tt -> p c tt"),
              inc=sAL,
              wait=[(sAX, nAX), (sKX0, nKX0), (sKX1, nKX1)] if q == 0 else [])
    nAL = p.tot[sAL]
    for ds in range(4):
        for cg in range(8):
            buf = cg % 2
            sW = (sWC0, sWC1)[buf]
            war = []
            gi = ds * 8 + cg
            if gi == 0:
                war = [(sKX0, nKX0), (sKX1, nKX1), (sAX, nAX)]
            if gi >= 2:
                war = [(sMM, gi - 1)]
            p.dma("gpsimd",
                  w2c[buf].rearrange("p (c n) -> p c n", c=8),
                  io["w2s"].rearrange("(cg c) ml n -> cg ml c n", c=8)[
                      cg, :, :, ds * 512:(ds + 1) * 512],
                  inc=sW, wait=war)
            wv = p.tot[sW]
            for ci in range(8):
                c = cg * 8 + ci
                for tb in range(NTB):
                    w = []
                    if ci == 0 and tb == 0:
                        w = [(sW, wv)]
                        if cg == 0:
                            w.append((sAL, nAL))
                            w.append((sXA, p.tot[sXA]))
                    p.emit("tensor", "matmul", psW[tb][:],
                           A_sb[:, c * 1024 + tb * 128:c * 1024 + (tb + 1) * 128],
                           w2c[buf][:, ci * 512:(ci + 1) * 512],
                           start=(c == 0), stop=(c == 63), wait=w,
                           inc=(sMM if (ci == 7 and tb == NTB - 1) else None))
        nMM = p.tot[sMM]
        for tb in range(NTB):
            p.dma("sync", xin[:, tb * 512:(tb + 1) * 512],
                  io["x_src"][tb * 128:(tb + 1) * 128,
                              ds * 512:(ds + 1) * 512], inc=sXL,
                  wait=[(sXA, p.tot[sXA])] if (ds >= 1 and tb == 0) else [])
        nXL = p.tot[sXL]
        for tb in range(NTB):
            p.emit("vector", "tensor_add", xout[:, tb * 512:(tb + 1) * 512],
                   xin[:, tb * 512:(tb + 1) * 512], psW[tb][:],
                   wait=[(sMM, nMM), (sXL, nXL)] +
                        ([(sXO, p.tot[sXO])] if (ds >= 1 and tb == 0)
                         else []),
                   inc=sXA)
            p.dma("sync", io["x_dst"][tb * 128:(tb + 1) * 128,
                                      ds * 512:(ds + 1) * 512],
                  xout[:, tb * 512:(tb + 1) * 512], inc=sXO,
                  wait=[(sXA, p.tot[sXA])])
    p.wait("sync", sXO, p.tot[sXO])
    done = [(sXO, p.tot[sXO])]
    regW.close()
    reg.close()
    return done



NCORES = 8
TPC = 1024


def l0_slopes(n=16):
    start = 2.0 ** (-2.0 ** (-(math.log2(n) - 3)))
    return np.array([start * start ** i for i in range(n)], np.float64)


def l0_tables(core):
    H, C = 16, 128
    slopes = l0_slopes(H)
    r = np.exp(-slopes)
    freq = (1.0 / 1024.0) ** np.linspace(0.0, 1.0, 32)
    tg = np.arange(core * TPC, (core + 1) * TPC, dtype=np.float64)
    theta = tg[:, None] * freq[None, :]
    rotC = np.tile(np.cos(theta), (1, 16)).astype(np.float32)
    rotS = np.tile(np.sin(theta), (1, 16)).astype(np.float32)
    i = np.arange(C, dtype=np.float64)
    dmaskT = np.zeros((H * C, C), np.float32)
    qdecT = np.zeros((H * C, C), np.float32)
    kdec = np.zeros((C, H), np.float32)
    for h in range(H):
        diff = i[None, :] - i[:, None]
        m = diff >= 0
        dmaskT[h * C:(h + 1) * C, :] = np.where(m, r[h] ** np.where(m, diff, 0), 0.0)
        qdecT[h * C:(h + 1) * C, :] = (r[h] ** (i + 1.0))[None, :]
        kdec[:, h] = r[h] ** (C - 1.0 - i)
    w = np.zeros((128, 128), np.float32)
    for cp in range(8):
        for h in range(H):
            if cp < core:
                w[:, cp * 16 + h] = r[h] ** (1024.0 * (core - 1 - cp))
    return dict(rotC=rotC, rotS=rotS, dmaskT=dmaskT, qdecT=qdecT,
                kdec=kdec, w_l0=w)


def vec_pt(v):
    v = np.asarray(v, np.float32)
    return np.ascontiguousarray(v.reshape(-1, 128).T)


IDENT = np.eye(128, dtype=np.float32)


def l1_tables(core):
    LT1 = np.triu(np.ones((64, 64), np.float32))
    LT = np.tile(LT1, (2, 1))
    maskTg = np.tile(LT1, (2, 4)).astype(np.float32)
    ident64 = np.tile(np.eye(64, dtype=np.float32), (2, 1))
    m_tab = np.zeros((128, 64), np.float32)
    pen = np.zeros((128, 8), np.float32)
    for cp in range(8):
        for j in range(8):
            if cp < j < core:
                m_tab[:, cp * 8 + j] = 1.0
        pen[:, cp] = 0.0 if cp < core else -1e9
    return dict(LT=LT, maskTg=maskTg, m_tab=m_tab, pen=pen, ident64=ident64)


from concourse.bass_utils import run_bass_kernel_spmd


def _hadamard(n):
    H = np.array([[1.0]], np.float32)
    while H.shape[0] < n:
        H = np.block([[H, H], [H, -H]])
    return H


H128M = _hadamard(128)
_H64M = _hadamard(64)
H64BLK = np.block([[_H64M, np.zeros((64, 64), np.float32)],
                   [np.zeros((64, 64), np.float32), _H64M]])


def _build_fused():
    nc = bass.Bass("TRN2", target_bir_lowering=False)

    def dt(name, sh, kind="ExternalInput", dtype=F32):
        return nc.dram_tensor(name, sh, dtype, kind=kind).ap()

    x0 = dt("x_src", [1024, 2048])
    x1 = dt("x1", [1024, 2048], "Internal")
    x2 = dt("x2", [1024, 2048], "Internal")
    x3 = dt("x3", [1024, 2048], "Internal")
    xo = dt("x_dst", [1024, 2048], "ExternalOutput")
    ident = dt("ident", [128, 128])

    io0 = dict(x_src=x0, x_dst=x1, ident=ident,
               Wq=dt("Wq0", [2048, 2048]), Wk=dt("Wk0", [2048, 2048]),
               Wv=dt("Wv0", [2048, 2048]), Wo=dt("Wo0", [2048, 2048]),
               rms_w_t=dt("rms_w_t", [128, 16]),
               rotC=dt("rotC", [1024, 512]), rotS=dt("rotS", [1024, 512]),
               dmaskT=dt("dmaskT", [2048, 128]), qdecT=dt("qdecT", [2048, 128]),
               kdec=dt("kdec", [128, 16]), w_l0=dt("w_l0", [128, 128]),
               cc_in=dt("cc_in_l0", [2048, 128], "Internal"),
               cc_out=dt("cc_out_l0", [16384, 128], "Internal"))
    io1 = dict(x_src=x1, x_dst=x2, ident=ident,
               gate_wT=dt("gate_wT", [2048, 8]), bias_t=dt("bias_t", [128, 8]),
               w1T=dt("w1T", [8, 2048, 1408]), w3T=dt("w3T", [8, 2048, 1408]),
               w2T=dt("w2T", [8, 1408, 2048]),
               sw1T=dt("sw1T", [2048, 1408]), sw3T=dt("sw3T", [2048, 1408]),
               sw2T=dt("sw2T", [1408, 2048]))
    io2 = dict(x_src=x2, x_dst=x3, ident=ident,
               Wq=dt("Wq1", [2048, 2048]), Wk=dt("Wk1", [2048, 2048]),
               Wv=dt("Wv1", [2048, 1024]), Wg=dt("Wg", [2048, 1024]),
               Wgk1=dt("Wgk1", [2048, 16]), Wgk2=dt("Wgk2", [16, 2048]),
               bgk2_r=dt("bgk2_r", [1, 2048]), wnorm_t=dt("wnorm_t", [128, 2]),
               Wo=dt("Wo1", [1024, 2048]), LT=dt("LT", [128, 64]),
               maskTg=dt("maskTg", [128, 256]), ident64=dt("ident64", [128, 64]),
               m_tab=dt("m_tab", [128, 64]), pen=dt("pen", [128, 8]),
               cc1_in=dt("cc1_in", [2048, 256], "Internal"),
               cc1_out=dt("cc1_out", [16384, 256], "Internal"),
               cc2_in=dt("cc2_in", [128, 16], "Internal"),
               cc2_out=dt("cc2_out", [1024, 16], "Internal"))
    io3 = dict(x_src=x3, x_dst=xo, ident=ident,
               W1T=dt("W1T", [2048, 8192]), W3T=dt("W3T", [2048, 8192]),
               W2T=dt("W2T", [8192, 2048]),
               H128=dt("H128", [128, 128]), H64blk=dt("H64blk", [128, 128]),
               a_scr=dt("a_scr", [64, 128, 1024], "Internal", BF16),
               w2s=dt("w2s", [64, 128, 2048], "Internal", BF16))

    with ExitStack() as es:
        p = P(nc, es)
        snap = dict(p.tot)
        emit_l0_attn(p, io0)
        snap = p.epoch_end(snap)
        emit_l0_ffn(p, io1)
        snap = p.epoch_end(snap)
        emit_l1_attn(p, io2)
        snap = p.epoch_end(snap)
        emit_l1_ffn(p, io3)
        p.build_block()
        carr = p.const_array()
    return nc, carr


def _in_maps(inputs, carr):
    f = lambda k: np.ascontiguousarray(np.asarray(inputs[k], np.float32))
    tr = lambda a: np.ascontiguousarray(np.swapaxes(np.asarray(a, np.float32),
                                                    -1, -2))
    x = f("L0_attn_x").reshape(8192, 2048)
    eb = f("L0_ffn_expert_bias")
    shared = dict(
        Wq0=f("L0_attn_Wq"), Wk0=f("L0_attn_Wk"), Wv0=f("L0_attn_Wv"),
        Wo0=f("L0_attn_Wo"), rms_w_t=vec_pt(f("L0_attn_rms_w")),
        gate_wT=tr(f("L0_ffn_gate_w")), bias_t=np.tile(eb[None, :], (128, 1)),
        w1T=tr(f("L0_ffn_w1")), w3T=tr(f("L0_ffn_w3")), w2T=tr(f("L0_ffn_w2")),
        sw1T=tr(f("L0_ffn_sw1")), sw3T=tr(f("L0_ffn_sw3")),
        sw2T=tr(f("L0_ffn_sw2")),
        Wq1=f("L1_attn_Wq"), Wk1=f("L1_attn_Wk"), Wv1=f("L1_attn_Wv"),
        Wg=f("L1_attn_Wg"), Wgk1=f("L1_attn_Wgk1"), Wgk2=f("L1_attn_Wgk2"),
        bgk2_r=f("L1_attn_bgk2")[None, :],
        wnorm_t=np.ascontiguousarray(f("L1_attn_Wnorm").reshape(2, 128).T),
        Wo1=f("L1_attn_Wo"),
        W1T=tr(f("L1_ffn_W1")), W3T=tr(f("L1_ffn_W3")), W2T=tr(f("L1_ffn_W2")),
        H128=H128M, H64blk=H64BLK, ident=IDENT, consts_=carr)
    ms = []
    for c in range(NCORES):
        m = dict(shared)
        m["x_src"] = np.ascontiguousarray(x[c * TPC:(c + 1) * TPC])
        m.update(l0_tables(c))
        m.update(l1_tables(c))
        ms.append(m)
    return ms


def kernel(**inputs):
    nc, carr = _build_fused()
    ms = _in_maps(inputs, carr)
    res = run_bass_kernel_spmd(nc, ms, core_ids=list(range(NCORES)))
    out = np.concatenate([np.asarray(res.results[c]["x_dst"])
                          for c in range(NCORES)], 0)
    return out[None].astype(np.float32)
